# version 13
# baseline (speedup 1.0000x reference)
import numpy as np
import ml_dtypes
from contextlib import ExitStack

import concourse.bass as bass
import concourse.mybir as mybir
from concourse.bass_utils import run_bass_kernel_spmd

F32 = mybir.dt.float32
BF16 = mybir.dt.bfloat16
I32 = mybir.dt.int32
AF = mybir.ActivationFunctionType
ALU = mybir.AluOpType
AX = mybir.AxisListType
BFD = ml_dtypes.bfloat16

D = 2048
HD = 128
NHL = 8
FF = 8192
GQ = 64
SM_SCALE = float(np.float32(1.0) / np.sqrt(np.float32(HD)))
LN_EPS = 1e-5
RMS_EPS = float(np.finfo(np.float32).eps)
QEPS = float(np.float32(BFD(1e-8)))
TH_C = float(np.float32(513.0 / 1024.0))

ENGS = ("tensor", "vector", "scalar", "gpsimd", "sync")


class _Ev:
    __slots__ = ("fn", "waits", "incs")

    def __init__(self, fn, waits, incs):
        self.fn, self.waits, self.incs = fn, waits, incs


class _S:
    __slots__ = ("h", "n")


class Builder:
    def __init__(self, nc, ctx, S):
        self.nc = nc
        self.ctx = ctx
        self.S = S
        self.ev = {e: [] for e in ENGS}
        self.sems = {}
        self.pe = self.sem("pe")
        self.ve = self.sem("ve")
        self.ac = self.sem("ac")
        self.po = self.sem("po")
        self.bar = self.sem("bar")
        self.cc = self.sem("cc")

    def sem(self, name):
        if name not in self.sems:
            s = _S()
            s.h = self.ctx.enter_context(self.nc.semaphore(name))
            s.n = 0
            self.sems[name] = s
        return self.sems[name]

    DRAIN_ENGS = ("vector", "scalar", "gpsimd")

    def op(self, eng, fn, waits=(), inc=None, n=1, raw_inc=False, drain=None):
        if drain is False:
            raw_inc = True
        if (inc is not None and n == 1 and not raw_inc
                and eng in self.DRAIN_ENGS):
            self.ev[eng].append(_Ev(fn, list(waits), []))
            inc.n += 1
            self.ev[eng].append(_Ev(lambda e: e.drain(), [], [(inc, 1)]))
            return inc.n
        incs = []
        if inc is not None:
            inc.n += n
            incs.append((inc, n))
        self.ev[eng].append(_Ev(fn, list(waits), incs))
        return inc.n if inc is not None else 0

    def dma(self, q, out, in_, sem, waits=(), **kw):
        self.op(q, lambda e, o=out, i=in_, kw=kw: e.dma_start(out=o, in_=i, **kw),
                waits=waits, inc=sem, n=16)
        return sem.n

    def w(self, *sems):
        return [(s, s.n) for s in sems if s.n > 0]

    def barrier(self):
        t = self.bar.n + len(ENGS)
        for e in ENGS:
            if e in self.DRAIN_ENGS:
                self.op(e, lambda eng: eng.drain(), inc=self.bar, raw_inc=True)
            else:
                self.op(e, lambda eng: eng.nop(), inc=self.bar, raw_inc=True)
        for e in ENGS:
            self.ev[e].append(_Ev(None, [(self.bar, t)], []))

    def flush(self, block):
        for ename in ENGS:
            evs = self.ev[ename]
            if not evs:
                continue
            self.ev[ename] = []

            def run(eng, evs=evs):
                for ev in evs:
                    for (s, v) in ev.waits:
                        if v > 0:
                            eng.wait_ge(s.h, v)
                    if ev.fn is not None:
                        inst = ev.fn(eng)
                        for (s, k) in ev.incs:
                            inst.then_inc(s.h, k)
            getattr(block, ename)(run)


def build_nc(S=4096, debug=False):
    nc = bass.Bass("TRN2", target_bir_lowering=False, debug=False)
    T = S // 2
    OH = T // 2
    ctx = ExitStack()

    ap = lambda t: t.ap()
    x_full = ap(nc.dram_tensor("x_full", [S, D], F32, kind="ExternalInput"))
    x_res = ap(nc.dram_tensor("x_res", [T, D], F32, kind="ExternalInput"))
    wq_in = ap(nc.dram_tensor("wq_in", [NHL * HD, D], F32, kind="ExternalInput"))
    wk_in = ap(nc.dram_tensor("wk_in", [NHL * HD, D], F32, kind="ExternalInput"))
    wv_in = ap(nc.dram_tensor("wv_in", [NHL * HD, D], F32, kind="ExternalInput"))
    bq_in = ap(nc.dram_tensor("bq_in", [1, NHL * HD], F32, kind="ExternalInput"))
    bk_in = ap(nc.dram_tensor("bk_in", [1, NHL * HD], F32, kind="ExternalInput"))
    bv_in = ap(nc.dram_tensor("bv_in", [1, NHL * HD], F32, kind="ExternalInput"))
    wo_in = ap(nc.dram_tensor("wo_in", [D, D], F32, kind="ExternalInput"))
    bo_in = ap(nc.dram_tensor("bo_in", [1, D], F32, kind="ExternalInput"))
    wgu_in = ap(nc.dram_tensor("wgu_in", [2048, 2048], F32, kind="ExternalInput"))
    wp_in = ap(nc.dram_tensor("wp_in", [2048, 1024], F32, kind="ExternalInput"))
    identb_in = ap(nc.dram_tensor("identb_in", [128, 128], BF16, kind="ExternalInput"))
    masks_in = ap(nc.dram_tensor("masks_in", [128, 4 * 512], BF16, kind="ExternalInput"))
    oidx_in = ap(nc.dram_tensor("oidx_in", [128, 32], I32, kind="ExternalInput"))

    out = ap(nc.dram_tensor("out", [T, D], F32, kind="ExternalOutput"))

    wq_bf = ap(nc.dram_tensor("wq_bf", [NHL * HD, D], BF16, kind="Internal"))
    wk_bf = ap(nc.dram_tensor("wk_bf", [NHL * HD, D], BF16, kind="Internal"))
    wv_bf = ap(nc.dram_tensor("wv_bf", [NHL * HD, D], BF16, kind="Internal"))
    wo_bf = ap(nc.dram_tensor("wo_bf", [D, D], BF16, kind="Internal"))
    qc_in = ap(nc.dram_tensor("qc_in", [3072, 2048], BF16, kind="Internal"))
    qc_out = ap(nc.dram_tensor("qc_out", [8 * 3072, 2048], BF16, kind="Internal",
                               addr_space="Shared"))
    qT_scr = ap(nc.dram_tensor("qT_scr", [NHL * HD, S], BF16, kind="Internal"))
    kT_scr = ap(nc.dram_tensor("kT_scr", [NHL * HD, S], BF16, kind="Internal"))
    v_scr = ap(nc.dram_tensor("v_scr", [S, NHL * HD], BF16, kind="Internal"))
    o_loc = ap(nc.dram_tensor("o_loc", [NHL * HD, S], BF16, kind="Internal"))
    o_all_a = ap(nc.dram_tensor("o_all_a", [8 * 512, S], BF16, kind="Internal",
                                addr_space="Shared"))
    o_all_b = ap(nc.dram_tensor("o_all_b", [8 * 512, S], BF16, kind="Internal",
                                addr_space="Shared"))
    x1_scr = ap(nc.dram_tensor("x1_scr", [T, D], F32, kind="Internal"))
    x2T_scr = ap(nc.dram_tensor("x2T_scr", [D, T], BF16, kind="Internal"))

    taps = {}
    if debug:
        taps["tap_qT"] = (ap(nc.dram_tensor("tap_qT", [NHL * HD, S], BF16, kind="ExternalOutput")), qT_scr)
        taps["tap_kT"] = (ap(nc.dram_tensor("tap_kT", [NHL * HD, S], BF16, kind="ExternalOutput")), kT_scr)
        taps["tap_v"] = (ap(nc.dram_tensor("tap_v", [S, NHL * HD], BF16, kind="ExternalOutput")), v_scr)
        taps["tap_o"] = (ap(nc.dram_tensor("tap_o", [NHL * HD, S], BF16, kind="ExternalOutput")), o_loc)
        taps["tap_x1"] = (ap(nc.dram_tensor("tap_x1", [T, D], F32, kind="ExternalOutput")), x1_scr)
        taps["tap_qc"] = (ap(nc.dram_tensor("tap_qc", [3072, 2048], BF16, kind="ExternalOutput")), qc_out[0:3072, :])
        tap_ln = ap(nc.dram_tensor("tap_ln", [128, 12], F32, kind="ExternalOutput"))
        tap_xln = ap(nc.dram_tensor("tap_xln", [128, 2048], BF16, kind="ExternalOutput"))

    with ctx:
        b = Builder(nc, ctx, S)
        sb = lambda name, shape, dt: ctx.enter_context(nc.sbuf_tensor(name, shape, dt))

        identb = sb("identb", [128, 128], BF16)
        oidx = sb("oidx", [128, 32], I32)
        ones_r = sb("ones_r", [1, 512], BF16)
        ones_c = sb("ones_c", [128, 1], BF16)
        s1v = sb("s1v", [128, 2], F32)
        sqv = sb("sqv", [128, 2], F32)
        mv = sb("mv", [128, 2], F32)
        msqv = sb("msqv", [128, 2], F32)
        varv = sb("varv", [128, 2], F32)
        rstdv = sb("rstdv", [128, 2], F32)
        cst = sb("cst", [128, 2], F32)

        s_consts = b.sem("s_consts")
        s_cast = b.sem("s_cast")
        s_xt = [b.sem(f"s_xt{i}") for i in range(2)]

        block = ctx.enter_context(nc.Block())

        b.dma("sync", identb[:], identb_in, s_consts)
        b.dma("sync", oidx[:], oidx_in, s_consts)
        b.dma("gpsimd", wq_bf, wq_in, s_cast)
        b.dma("gpsimd", wk_bf, wk_in, s_cast)
        b.dma("gpsimd", wv_bf, wv_in, s_cast)
        b.dma("gpsimd", wo_bf, wo_in, s_cast)
        b.op("vector", lambda e: e.memset(ones_r[:], 1.0), inc=b.ve)
        b.op("vector", lambda e: e.memset(cst[:, 0:1], LN_EPS), inc=b.ve)
        b.op("vector", lambda e: e.memset(cst[:, 1:2], RMS_EPS), inc=b.ve)
        b.op("vector", lambda e: e.memset(ones_c[:], 1.0), inc=b.ve)


        TC = min(S, 2048)
        NHALF = S // TC
        with ExitStack() as p1:
            psb = lambda name, shape, dt: p1.enter_context(nc.psum_tensor(name, shape, dt))
            sb1 = lambda name, shape, dt: p1.enter_context(nc.sbuf_tensor(name, shape, dt))
            xln1T = sb1("xln1T", [128, 16 * TC], BF16)
            wT = [sb1(f"wT{i}", [128, 16 * 1024], BF16) for i in range(2)]
            qstage = [sb1(f"qstage{i}", [128, 512], BF16) for i in range(4)]
            xt = [sb1(f"p1xt{i}", [128, D], F32) for i in range(2)]
            xlnt = [sb1(f"p1xlnt{i}", [128, D], BF16) for i in range(2)]
            sqscr = sb1("p1sqscr", [128, D], F32)
            bq_sb = sb1("bq_sb", [1, NHL * HD], BF16)
            bk_sb = sb1("bk_sb", [1, NHL * HD], BF16)
            bv_sb = sb1("bv_sb", [1, NHL * HD], BF16)
            b.dma("gpsimd", bq_sb[:], bq_in, s_cast)
            b.dma("gpsimd", bk_sb[:], bk_in, s_cast)
            b.dma("gpsimd", bv_sb[:], bv_in, s_cast)
            psp = [psb(f"p1_psp{i}", [128, 512], F32) for i in range(4)]
            pst1 = [psb(f"p1_pst{i}", [128, 128], BF16) for i in range(2)]
            s_wT = [b.sem(f"s_wT{i}") for i in range(2)]
            s_qst = [b.sem(f"s_qst{i}") for i in range(4)]

            pst_read = [0, 0]
            psp_read = [0, 0, 0, 0]
            sqv_read = [0, 0]
            ln_ac = [0, 0]
            xt_sq = [0, 0]
            xt_ap = [0, 0]
            xlnt_read = [0, 0]

            xln_tile_ve = [0] * 16

            def emit_ln_tile(half, i):
                par = i % 2
                c = par
                t0 = half * TC + i * 128
                wl = [(b.ve, xt_ap[par]), (b.ac, xt_sq[par])] if (half + i) else []
                b.dma("sync", xt[par][:], x_full[t0:t0 + 128, :], s_xt[par], waits=wl)
                lw = [(s_xt[par], s_xt[par].n)]
                b.op("vector", lambda e, o=s1v[:, c:c + 1], a=xt[par][:]:
                     e.tensor_reduce(o, a, AX.X, ALU.add), waits=lw, inc=b.ve)
                b.op("vector", lambda e, o=mv[:, c:c + 1], a=s1v[:, c:c + 1]:
                     e.tensor_scalar(o, a, -1.0 / D, None, ALU.mult),
                     waits=[(b.ve, b.ve.n), (b.ac, ln_ac[c])], inc=b.ve)
                mv_ve = b.ve.n
                b.op("scalar", lambda e, a=xt[par][:], m2=mv[:, c:c + 1],
                     o=sqscr[:]:
                     e.activation(o, a, AF.Square, bias=m2),
                     waits=lw + [(b.ve, mv_ve), (b.ve, max(sqv_read))],
                     inc=b.ac)
                sq_ac = b.ac.n
                xt_sq[par] = sq_ac
                b.op("vector", lambda e, o=sqv[:, c:c + 1], a=sqscr[:]:
                     e.tensor_reduce(o, a, AX.X, ALU.add),
                     waits=[(b.ac, sq_ac)], inc=b.ve)
                sqv_read[par] = b.ve.n
                b.op("scalar", lambda e, o=rstdv[:, c:c + 1], a=sqv[:, c:c + 1]:
                     e.activation(o, a, AF.Ln, scale=1.0 / D, bias=cst[:, 0:1]),
                     waits=[(b.ve, sqv_read[par])], inc=b.ac)
                b.op("scalar", lambda e, o=rstdv[:, c:c + 1], a=rstdv[:, c:c + 1]:
                     e.activation(o, a, AF.Exp, scale=-0.5),
                     waits=[(b.ac, b.ac.n)], inc=b.ac)
                ln_ac[c] = b.ac.n
                b.op("vector", lambda e, o=xlnt[par][:], a=xt[par][:],
                     m2=mv[:, c:c + 1], r2=rstdv[:, c:c + 1]:
                     e.tensor_scalar(o, a, m2, r2, ALU.add, ALU.mult),
                     waits=[(b.ac, ln_ac[c]), (b.pe, xlnt_read[par])],
                     inc=b.ve)
                apply_ve = b.ve.n
                xt_ap[par] = apply_ve
                for k in range(16):
                    tp = k % 2
                    b.op("tensor", lambda e, o=pst1[tp][:],
                         a=xlnt[par][:, 128 * k:128 * (k + 1)]:
                         e.transpose(o, a, identb[:]),
                         waits=[(b.ve, apply_ve), (b.ve, pst_read[tp])],
                         inc=b.pe)
                    b.op("vector", lambda e,
                         o=xln1T[:, TC * k + 128 * i:TC * k + 128 * (i + 1)],
                         a=pst1[tp][:]: e.tensor_copy(o, a),
                         waits=b.w(b.pe), inc=b.ve, drain=False)
                    pst_read[tp] = b.ve.n
                xln_tile_ve[i] = b.ve.n
                xlnt_read[par] = b.pe.n

            def load_w(wpar, src_bf, wl):
                base = s_wT[wpar].n
                for k in range(16):
                    b.dma("sync", wT[wpar][:, 1024 * k:1024 * (k + 1)],
                          src_bf[0:1024, 128 * k:128 * (k + 1)], s_wT[wpar],
                          waits=wl if k == 0 else (), transpose=True)
                return base

            pcnt = [0]

            def emit_qk_tg(wpar, bias_sb, dst, half, tg, base, first):
                for hd in range(8):
                    pp = pcnt[0] % 4
                    pcnt[0] += 1
                    for dc in range(16):
                        wl2 = ([(b.ve, xln_tile_ve[4 * tg + 3]),
                                (b.ac, psp_read[pp])] if dc == 0 else [])
                        if first and hd == 0:
                            wl2 = wl2 + [(s_wT[wpar], base + 16 * (dc + 1))]
                        b.op("tensor", lambda e, o=psp[pp][:],
                             l=wT[wpar][:, 1024 * dc + 128 * hd:1024 * dc + 128 * (hd + 1)],
                             r=xln1T[:, TC * dc + 512 * tg:TC * dc + 512 * (tg + 1)],
                             s=(dc == 0):
                             e.matmul(o, l, r, start=s, stop=False),
                             waits=wl2)
                    b.op("tensor", lambda e, o=psp[pp][:],
                         l=bias_sb[0:1, 128 * hd:128 * (hd + 1)]:
                         e.matmul(o, l, ones_r[0:1, :], start=False, stop=True),
                         inc=b.pe)
                    b.op("scalar", lambda e, o=qstage[pp][:], a=psp[pp][:]:
                         e.activation(o, a, AF.Copy),
                         waits=b.w(b.pe) + [(s_qst[pp], s_qst[pp].n)],
                         inc=b.ac)
                    psp_read[pp] = b.ac.n
                    b.dma("scalar", dst[128 * hd:128 * (hd + 1),
                                        half * TC + 512 * tg:half * TC + 512 * (tg + 1)],
                          qstage[pp][:], s_qst[pp],
                          waits=[(b.ac, b.ac.n)])

            for half in range(NHALF):
                base_q = load_w(0, wq_bf,
                                b.w(s_cast) + (b.w(b.pe) if half else []))
                base_k = load_w(1, wk_bf, [])
                for tg in range(TC // 512):
                    if half == 0:
                        if tg == 0:
                            for i in range(0, 4):
                                emit_ln_tile(0, i)
                        if tg + 1 < TC // 512:
                            for i in range(4 * (tg + 1), 4 * (tg + 1) + 4):
                                emit_ln_tile(0, i)
                    emit_qk_tg(0, bq_sb, qT_scr, half, tg, base_q, tg == 0)
                    emit_qk_tg(1, bk_sb, kT_scr, half, tg, base_k, tg == 0)
                base_v = load_w(0, wv_bf, b.w(b.pe))
                for tt in range(TC // 128):
                    for hg in range(2):
                        pp = pcnt[0] % 4
                        pcnt[0] += 1
                        for dc in range(16):
                            wl2 = ([(b.ve, xln_tile_ve[tt]),
                                    (b.ac, psp_read[pp])] if dc == 0 else [])
                            if tt == 0 and hg == 0:
                                wl2 = wl2 + [(s_wT[0], base_v + 16 * (dc + 1))]
                            b.op("tensor", lambda e, o=psp[pp][:],
                                 l=xln1T[:, TC * dc + 128 * tt:TC * dc + 128 * (tt + 1)],
                                 r=wT[0][:, 1024 * dc + 512 * hg:1024 * dc + 512 * (hg + 1)],
                                 s=(dc == 0):
                                 e.matmul(o, l, r, start=s, stop=False),
                                 waits=wl2)
                        b.op("tensor", lambda e, o=psp[pp][:],
                             r=bv_sb[0:1, 512 * hg:512 * (hg + 1)]:
                             e.matmul(o, ones_r[0:1, 0:128], r, start=False, stop=True),
                             inc=b.pe)
                        b.op("scalar", lambda e, o=qstage[pp][:], a=psp[pp][:]:
                             e.activation(o, a, AF.Copy),
                             waits=b.w(b.pe) + [(s_qst[pp], s_qst[pp].n)],
                             inc=b.ac)
                        psp_read[pp] = b.ac.n
                        b.dma("scalar", v_scr[half * TC + 128 * tt:half * TC + 128 * (tt + 1),
                                              512 * hg:512 * (hg + 1)],
                              qstage[pp][:], s_qst[pp],
                              waits=[(b.ac, b.ac.n)])
                    if half + 1 < NHALF:
                        emit_ln_tile(half + 1, tt)

            wb = [sb1(f"q_wb{i}", [128, 2048], BF16) for i in range(2)]
            gs = sb1("q_gs", [128, 32], F32)
            meanb = sb1("q_meanb", [128, 32], BF16)
            qs32 = sb1("q_s32", [128, 32], F32)
            qth = sb1("q_th", [128, 32], F32)
            qnth = sb1("q_nth", [128, 32], F32)
            tpos = sb1("q_tpos", [128, 2048], BF16)
            tneg = sb1("q_tneg", [128, 2048], BF16)
            wqo = sb1("q_wqo", [128, 2048], BF16)
            asm_q = sb1("q_asm", [128, 16 * 256], BF16)
            s_wb = [b.sem(f"s_wb{i}") for i in range(2)]
            s_qc = b.sem("s_qc")
            wb_read = [0, 0]
            qpst_read = [0, 0]
            qtr_pe = [0]
            lnpst_ve = b.ve.n

            def quant_tile(i, src_ap, W, asm_cols):
                par = i % 2
                b.dma("gpsimd", wb[par][:, 0:W], src_ap, s_wb[par],
                      waits=[(b.ve, wb_read[par])])
                wbt = wb[par][:, 0:W]
                wb3 = wbt.rearrange("p (g k) -> p g k", k=GQ)
                ng = W // GQ
                wl = [(s_wb[par], s_wb[par].n)]
                b.op("vector", lambda e, o=gs[:, 0:ng], i3=wb3: e.tensor_reduce(
                    o, i3, AX.X, ALU.add, apply_absolute_value=True), waits=wl,
                    inc=b.ve)
                b.op("vector", lambda e, o=meanb[:, 0:ng], a=gs[:, 0:ng]:
                     e.tensor_scalar(o, a, 1.0 / GQ, None, ALU.mult),
                     waits=[(b.ve, b.ve.n)], inc=b.ve)
                b.op("vector", lambda e, o=meanb[:, 0:ng], a=meanb[:, 0:ng]:
                     e.tensor_scalar(o, a, QEPS, None, ALU.max),
                     waits=[(b.ve, b.ve.n)], inc=b.ve)
                b.op("vector", lambda e, o=qs32[:, 0:ng], a=meanb[:, 0:ng]:
                     e.tensor_copy(o, a), waits=[(b.ve, b.ve.n)], inc=b.ve)
                b.op("vector", lambda e, o=qth[:, 0:ng], a=qs32[:, 0:ng]:
                     e.tensor_scalar(o, a, TH_C, None, ALU.mult),
                     waits=[(b.ve, b.ve.n)], inc=b.ve)
                b.op("vector", lambda e, o=qnth[:, 0:ng], a=qth[:, 0:ng]:
                     e.tensor_scalar(o, a, -1.0, None, ALU.mult),
                     waits=[(b.ve, b.ve.n)], inc=b.ve)
                thb = qth[:, 0:ng].unsqueeze(2).broadcast_to([128, ng, GQ])
                nthb = qnth[:, 0:ng].unsqueeze(2).broadcast_to([128, ng, GQ])
                s32b = qs32[:, 0:ng].unsqueeze(2).broadcast_to([128, ng, GQ])
                tp3 = tpos[:, 0:W].rearrange("p (g k) -> p g k", k=GQ)
                tn3 = tneg[:, 0:W].rearrange("p (g k) -> p g k", k=GQ)

                b.op("vector", lambda e, o=tp3, a=wb3, c=thb:
                     e.tensor_tensor(o, a, c, ALU.is_gt),
                     waits=[(b.ve, b.ve.n)])
                b.op("vector", lambda e, o=tn3, a=wb3, c=nthb:
                     e.tensor_tensor(o, a, c, ALU.is_lt))
                b.op("vector", lambda e, o=tpos[:, 0:W], a=tpos[:, 0:W], c=tneg[:, 0:W]:
                     e.tensor_tensor(o, a, c, ALU.subtract))
                b.op("vector", lambda e, o=tp3, a=tp3, c=s32b:
                     e.tensor_tensor(o, a, c, ALU.mult))
                b.op("vector", lambda e, o=tneg[:, 0:W], a=tpos[:, 0:W], c=wbt:
                     e.tensor_tensor(o, a, c, ALU.subtract))
                b.op("vector", lambda e, o=wqo[:, 0:W], a=wbt, c=tneg[:, 0:W]:
                     e.tensor_tensor(o, a, c, ALU.add),
                     waits=[(b.pe, qtr_pe[0])], inc=b.ve)
                wb_read[par] = b.ve.n
                wqo_ve = b.ve.n
                nk = W // 128
                ib = i % 2
                for k in range(nk):
                    tp = k % 2
                    b.op("tensor", lambda e, o=pst1[tp][:],
                         a=wqo[:, 128 * k:128 * (k + 1)]:
                         e.transpose(o, a, identb[:]),
                         waits=[(b.ve, wqo_ve), (b.ac, qpst_read[tp]),
                                (b.ve, lnpst_ve)],
                         inc=b.pe)
                    qtr_pe[0] = b.pe.n
                    b.op("scalar", lambda e,
                         o=asm_q[:, asm_cols * k + 128 * ib:asm_cols * k + 128 * (ib + 1)],
                         a=pst1[tp][:]: e.activation(o, a, AF.Copy),
                         waits=b.w(b.pe) + [(s_qc, s_qc.n)], inc=b.ac,
                         drain=(k == nk - 1 and ib == 1))
                    qpst_read[tp] = b.ac.n

            for bb in range(8):
                for ii in range(2):
                    i = 2 * bb + ii
                    quant_tile(i, wgu_in[128 * i:128 * (i + 1), :], 2048, 256)
                b.dma("scalar",
                      qc_in[0:2048, 256 * bb:256 * (bb + 1)].rearrange(
                          "(k p) h -> p k h", p=128),
                      asm_q[:].rearrange("p (k h) -> p k h", h=256), s_qc,
                      waits=[(b.ac, b.ac.n)])
            for bb in range(8):
                for ii in range(2):
                    i = 2 * bb + ii
                    quant_tile(i, wp_in[128 * i:128 * (i + 1), :], 1024, 256)
                b.dma("scalar",
                      qc_in[2048:3072, 256 * bb:256 * (bb + 1)].rearrange(
                          "(k p) h -> p k h", p=128),
                      asm_q[:, 0:8 * 256].rearrange("p (k h) -> p k h", h=256), s_qc,
                      waits=[(b.ac, b.ac.n)])
            b.op("gpsimd", lambda e: e.collective_compute(
                "AllGather", ALU.bypass, replica_groups=[list(range(8))],
                ins=[qc_in.opt()], outs=[qc_out.opt()]),
                waits=b.w(s_qc), inc=b.cc, raw_inc=True)
            cc_q = b.cc.n
            b.barrier()
            b.flush(block)

        with ExitStack() as p2:
            psb = lambda name, shape, dt: p2.enter_context(nc.psum_tensor(name, shape, dt))
            sb2 = lambda name, shape, dt: p2.enter_context(nc.sbuf_tensor(name, shape, dt))
            qh = sb2("qh", [128, 2 * S], BF16)
            kh = sb2("kh", [128, 2 * S], BF16)
            vh = sb2("vh", [128, 2 * S], BF16)
            pT = [sb2(f"pT{i}", [128, 1024], BF16) for i in range(3)]
            acc_d = sb2("acc_d", [128, 1024], F32)
            acc_p = sb2("acc_p", [128, 1024], F32)
            acc_f = sb2("acc_f", [128, 512], F32)
            acc_bf = sb2("acc_bf", [128, 512], BF16)
            r_full = [sb2(f"r_full{i}", [1, S], F32) for i in range(2)]
            esA = [sb2(f"esA{i}", [128, 512], F32) for i in range(2)]
            esB = [sb2(f"esB{i}", [128, 512], F32) for i in range(2)]
            esum_b = [sb2(f"esum_b{i}", [128, 1024], BF16) for i in range(2)]
            r_2 = sb2("r_2", [1, S], F32)
            r_hi = sb2("r_hi", [1, S], BF16)
            r_lo = sb2("r_lo", [1, S], BF16)
            rb_sb = [sb2(f"rb_sb{i}", [128, 512], F32) for i in range(2)]
            ostage = [sb2(f"ostage{i}", [128, S], BF16) for i in range(2)]
            masks = sb2("masks", [128, 4 * 512], BF16)
            b.dma("sync", masks[:], masks_in, s_consts)
            pss = [psb(f"pss{i}", [128, 1024], F32) for i in range(2)]
            pso = [psb(f"pso{i}", [128, 512], F32) for i in range(2)]
            psr = psb("psr", [1, 512], F32)
            psbc = psb("psbc", [128, 512], F32)
            s_head = [b.sem(f"s_head{i}") for i in range(2)]
            s_ost = b.sem("s_ost")

            NG = S // 512

            def load_head(h, slot, prefetched):
                if prefetched:
                    wl = b.w(b.pe, b.ve)
                else:
                    wl = b.w(*s_qst)
                b.dma("sync", qh[:, slot * S:(slot + 1) * S],
                      qT_scr[128 * h:128 * (h + 1), :], s_head[slot], waits=wl)
                b.dma("sync", kh[:, slot * S:(slot + 1) * S],
                      kT_scr[128 * h:128 * (h + 1), :], s_head[slot])
                b.dma("sync", vh[:, slot * S:(slot + 1) * S].rearrange("p (j h) -> p j h", h=128),
                      v_scr[0:S, 128 * h:128 * (h + 1)].rearrange("(j p) h -> p j h", p=128),
                      s_head[slot])

            horder = [4, 5, 6, 7, 0, 1, 2, 3] if NHL == 8 else list(range(NHL))
            load_head(horder[0], 0, False)
            if NHL > 1:
                load_head(horder[1], 1, False)
            lastexp = [0, 0]
            pT_free = [[], [], []]
            oevac = [0, 0]
            psr_free = 0
            last_bmm = 0
            rf_read = [0, 0]
            psbc_evac = [0]
            cc_b = 0

            def emit_tail(h, hp):
                nonlocal last_bmm, cc_b
                b.op("scalar", lambda e: e.activation(r_2[0:1, 0:S],
                                                      r_full[hp][0:1, 0:S], AF.Ln),
                     waits=b.w(b.ve), inc=b.ac)
                rf_read[hp] = b.ac.n
                b.op("scalar", lambda e: e.activation(r_2[0:1, 0:S], r_2[0:1, 0:S],
                                                      AF.Exp, scale=-1.0),
                     waits=[(b.ac, b.ac.n)], inc=b.ac)
                b.op("vector", lambda e: e.tensor_copy(r_hi[0:1, 0:S], r_2[0:1, 0:S]),
                     waits=[(b.pe, last_bmm)] + b.w(b.ac), inc=b.ve, drain=False)
                b.op("vector", lambda e: e.tensor_tensor(
                    r_lo[0:1, 0:S], r_2[0:1, 0:S], r_hi[0:1, 0:S], ALU.subtract),
                    waits=[(b.ve, b.ve.n)], inc=b.ve)
                bc_ve = b.ve.n
                for g in range(NG):
                    bp = g % 2
                    wl = [(b.ve, bc_ve if g == 0 else psbc_evac[0])]
                    b.op("tensor", lambda e, r=r_hi[0:1, 512 * g:512 * (g + 1)]:
                         e.matmul(psbc[:], ones_r[0:1, 0:128], r, start=True, stop=False),
                         waits=wl)
                    b.op("tensor", lambda e, r=r_lo[0:1, 512 * g:512 * (g + 1)]:
                         e.matmul(psbc[:], ones_r[0:1, 0:128], r, start=False, stop=True),
                         inc=b.pe)
                    last_bmm = b.pe.n
                    b.op("vector", lambda e, o=rb_sb[bp][:]:
                         e.tensor_copy(o, psbc[:]),
                         waits=[(b.pe, last_bmm)], inc=b.ve, drain=False)
                    psbc_evac[0] = b.ve.n
                    b.op("vector", lambda e, o=ostage[hp][:, 512 * g:512 * (g + 1)],
                         b2=bp: e.tensor_tensor(o, o, rb_sb[b2][:], ALU.mult),
                         waits=[(b.ve, b.ve.n)], inc=b.ve)
                b.dma("scalar", o_loc[128 * h:128 * (h + 1), :], ostage[hp][:], s_ost,
                      waits=b.w(b.ve))
                if h == (7 if NHL == 8 else 3):
                    b.op("gpsimd", lambda e: e.collective_compute(
                        "AllGather", ALU.bypass, replica_groups=[list(range(8))],
                        ins=[o_loc[512:1024, :].opt()], outs=[o_all_b.opt()]),
                        waits=b.w(s_ost), inc=b.cc, raw_inc=True)
                    cc_b = b.cc.n

            summm = [0, 0]
            esum_done = [0, 0]
            for hi in range(NHL):
                h = horder[hi]
                slot = hi % 2
                hp = hi % 2
                ost_wait = s_ost.n
                hl = [(s_head[slot], s_head[slot].n)]
                steps = [(g, jp) for g in range(NG) for jp in range(2 * g + 2)]

                def emit_av(g, jp, pb, gate):
                    nonlocal psr_free
                    gp = g % 2
                    njp = 2 * g + 2
                    for half2 in range(2):
                        j = 2 * jp + half2
                        wl = []
                        if half2 == 0:
                            wl = [(b.ac, gate)]
                            if jp == 0:
                                wl.append((b.ve, oevac[gp]))
                        b.op("tensor", lambda e, o=pso[gp][:],
                             l=vh[:, slot * S + 128 * j:slot * S + 128 * (j + 1)],
                             r=pT[pb][:, 512 * half2:512 * (half2 + 1)],
                             s=(jp == 0 and half2 == 0),
                             t=(jp == njp - 1 and half2 == 1):
                             e.matmul(o, l, r, start=s, stop=t),
                             waits=wl, inc=(b.pe if half2 == 1 else None))
                    pT_free[pb] = [(b.pe, b.pe.n)]
                    if jp == njp - 1:
                        av_pe = b.pe.n
                        b.op("tensor", lambda e, m=esum_b[gp][:, 0:512]:
                             e.matmul(psr[:], ones_c[:], m, start=True, stop=False),
                             waits=[(b.ve, esum_done[gp]), (b.ve, psr_free)])
                        b.op("tensor", lambda e, m=esum_b[gp][:, 512:1024]:
                             e.matmul(psr[:], ones_c[:], m, start=False, stop=True),
                             inc=b.pe)
                        summm[gp] = b.pe.n
                        b.op("vector", lambda e, o=ostage[hp][:, 512 * g:512 * (g + 1)],
                             g2=gp: e.tensor_copy(o, pso[g2][:]),
                             waits=[(b.pe, av_pe)] +
                                   ([(s_ost, ost_wait)] if g == 0 else []),
                             inc=b.ve, drain=False)
                        oevac[gp] = b.ve.n
                        b.op("vector", lambda e, o=r_full[hp][0:1, 512 * g:512 * (g + 1)],
                             g2=gp: e.tensor_copy(o, psr[:]),
                             waits=[(b.pe, summm[gp]), (b.ac, rf_read[hp])],
                             inc=b.ve, drain=(g == NG - 1))
                        psr_free = b.ve.n

                pend = None
                esum_read = [0, 0, 0]
                for i, (g, jp) in enumerate(steps):
                    pp = i % 2
                    pb = i % 3
                    gp = g % 2
                    wl = [(b.ac, lastexp[pp])] + (hl if i == 0 else [])
                    sc_ops = []
                    for half2 in range(2):
                        j = 2 * jp + half2
                        diag = j >= 4 * g
                        sc_ops.append((
                            lambda e, o=pss[pp][:, 512 * half2:512 * (half2 + 1)],
                            l=kh[:, slot * S + 128 * j:slot * S + 128 * (j + 1)],
                            r=qh[:, slot * S + 512 * g:slot * S + 512 * (g + 1)],
                            t=(not diag):
                            e.matmul(o, l, r, start=True, stop=t),
                            wl if half2 == 0 else []))
                        if diag:
                            mi = j - 4 * g
                            sc_ops.append((
                                lambda e, o=pss[pp][:, 512 * half2:512 * (half2 + 1)],
                                m=masks[:, 512 * mi:512 * (mi + 1)]:
                                e.matmul(o, identb[:], m, start=False, stop=True),
                                b.w(s_consts) if (h == 0 and g == 0) else []))
                    for si, (fn, swl) in enumerate(sc_ops):
                        b.op("tensor", fn, waits=swl,
                             inc=(b.pe if si == len(sc_ops) - 1 else None))
                    sc_pe = b.pe.n
                    b.op("scalar", lambda e, o=pT[pb][:], a=pss[pp][:]:
                         e.activation(o, a, AF.Exp, scale=SM_SCALE),
                         waits=[(b.pe, sc_pe), (b.ve, esum_read[pb])] +
                               pT_free[pb],
                         inc=b.ac, drain=False)
                    lastexp[pp] = b.ac.n
                    gate = b.ac.n
                    if pend is not None:
                        emit_av(*pend)
                    last = (jp == 2 * g + 1)
                    if jp == 0:
                        b.op("vector", lambda e, o=esA[gp][:], a=pT[pb][:, 0:512]:
                             e.tensor_copy(o, a),
                             waits=[(b.ac, gate), (b.pe, summm[gp])],
                             inc=b.ve, drain=False)
                        b.op("vector", lambda e, o=esB[gp][:], a=pT[pb][:, 512:1024]:
                             e.tensor_copy(o, a), inc=b.ve, drain=False)
                    else:
                        oA = esum_b[gp][:, 0:512] if last else esA[gp][:]
                        oB = esum_b[gp][:, 512:1024] if last else esB[gp][:]
                        b.op("vector", lambda e, o=oA, s1=esA[gp][:],
                             a=pT[pb][:, 0:512]:
                             e.tensor_tensor(o, s1, a, ALU.add),
                             waits=[(b.ac, gate)], inc=b.ve, drain=False)
                        b.op("vector", lambda e, o=oB, s1=esB[gp][:],
                             a=pT[pb][:, 512:1024]:
                             e.tensor_tensor(o, s1, a, ALU.add),
                             inc=b.ve, drain=last)
                        if last:
                            esum_done[gp] = b.ve.n
                    esum_read[pb] = b.ve.n
                    pend = (g, jp, pb, gate)
                    if hi > 0 and i == min(40, len(steps) - 2):
                        emit_tail(horder[hi - 1], (hi - 1) % 2)
                emit_av(*pend)
                if hi + 2 < NHL:
                    load_head(horder[hi + 2], hi % 2, True)
            emit_tail(horder[NHL - 1], (NHL - 1) % 2)
            b.op("gpsimd", lambda e: e.collective_compute(
                "AllGather", ALU.bypass, replica_groups=[list(range(8))],
                ins=[o_loc[0:512, :].opt()], outs=[o_all_a.opt()]),
                waits=b.w(s_ost), inc=b.cc, raw_inc=True)
            cc_a = b.cc.n
            b.barrier()
            b.flush(block)

        with ExitStack() as p3:
            psb = lambda name, shape, dt: p3.enter_context(nc.psum_tensor(name, shape, dt))
            sb3 = lambda name, shape, dt: p3.enter_context(nc.sbuf_tensor(name, shape, dt))
            og = sb3("og", [128, 16 * OH], BF16)
            woT = sb3("woT", [128, 16 * 2048], BF16)
            x1t = [sb3(f"x1t{i}", [128, D], F32) for i in range(2)]
            x2st = [sb3(f"x2st{i}", [128, D], BF16) for i in range(2)]
            xt = [sb3(f"p3xt{i}", [128, D], F32) for i in range(2)]
            xlnt = [sb3(f"p3xlnt{i}", [128, D], BF16) for i in range(2)]
            sqscr = sb3("p3sqscr", [128, D], F32)
            bo_sb = sb3("bo_sb", [1, D], BF16)
            b.dma("gpsimd", bo_sb[:], bo_in, s_cast)
            psop = [psb(f"psop{i}", [128, 512], F32) for i in range(2)]
            pst3 = [psb(f"p3_pst{i}", [128, 128], BF16) for i in range(2)]
            s_og = b.sem("s_og")
            s_woT = b.sem("s_woT")
            s_x1 = [b.sem(f"s_x1{i}") for i in range(2)]
            s_x2T = [b.sem(f"s_x2T{i}") for i in range(2)]

            o_view_a = o_all_a.rearrange("r (q t) -> (r q) t", t=OH)
            o_view_b = o_all_b.rearrange("r (q t) -> (r q) t", t=OH)
            forder = [4, 5, 6, 7, 12, 13, 14, 15, 0, 1, 2, 3, 8, 9, 10, 11] \
                if NHL == 8 else [0, 1, 2, 3, 8, 9, 10, 11, 4, 5, 6, 7, 12, 13, 14, 15]
            for ki, k in enumerate(forder):
                b.dma("sync", woT[:, 2048 * k:2048 * (k + 1)],
                      wo_bf[0:2048, 128 * k:128 * (k + 1)], s_woT,
                      waits=b.w(s_cast) if ki == 0 else (), transpose=True)

            psop_read = [0, 0]
            pst_read = [0, 0]
            lnsq_read = [0, 0]
            lnsq_ac = [0, 0]
            ln_ac3 = [0, 0]
            xt_ap3 = [0, 0]
            xlnt_read3 = [0, 0]
            for hf in range(2):
                gw = (b.w(b.pe) if hf == 1 else [])
                for fi, f in enumerate(forder):
                    col = f + 16 * hf
                    vw = o_view_a if (f % 8) < 4 else o_view_b
                    wl0 = []
                    if fi == 0:
                        wl0 = gw + b.w(s_consts) + [(b.cc, cc_b if NHL == 8 else cc_a)]
                    elif fi == 8:
                        wl0 = [(b.cc, cc_a if NHL == 8 else cc_b)]
                    b.op("gpsimd", lambda e, o=og[:, OH * f:OH * (f + 1)],
                         ix=oidx[:, col:col + 1], v2=vw:
                         e.indirect_dma_start(o, None, v2,
                                              bass.IndirectOffsetOnAxis(ix, 0)),
                         waits=wl0, inc=s_og, n=16)
                    if fi == 7:
                        og_a = s_og.n
                ogl = [(s_og, og_a)]
                og_b = s_og.n
                for sub in range(OH // 128):
                    par = sub % 2
                    c = par
                    r0 = hf * OH + sub * 128
                    b.dma("scalar", xt[par][:], x_res[r0:r0 + 128, :], s_xt[par],
                          waits=[(b.ve, xt_ap3[par])] if (hf + sub) else [])
                    for dg in range(4):
                        pp = dg % 2
                        for fi2 in range(16):
                            fc = forder[fi2]
                            wl = (ogl + [(s_woT, 16),
                                         (b.ve, psop_read[pp])]) if fi2 == 0 else \
                                ([(s_og, og_b)] if (fi2 == 8 and sub == 0 and dg == 0
                                                    ) else [])
                            if hf == 0 and sub == 0 and dg == 0 and fi2 > 0:
                                wl = wl + [(s_woT, 16 * (fi2 + 1))]
                            b.op("tensor", lambda e, o=psop[pp][:],
                                 l=og[:, OH * fc + 128 * sub:OH * fc + 128 * (sub + 1)],
                                 r=woT[:, 2048 * fc + 512 * dg:2048 * fc + 512 * (dg + 1)],
                                 s=(fi2 == 0):
                                 e.matmul(o, l, r, start=s, stop=False), waits=wl)
                        b.op("tensor", lambda e, o=psop[pp][:],
                             r=bo_sb[0:1, 512 * dg:512 * (dg + 1)]:
                             e.matmul(o, ones_r[0:1, 0:128], r, start=False, stop=True),
                             inc=b.pe)
                        wl = [(b.pe, b.pe.n), (s_xt[par], s_xt[par].n)]
                        if dg == 0:
                            wl += [(s_x1[par], s_x1[par].n), (b.ac, lnsq_ac[par])]
                        b.op("vector", lambda e, o=x1t[par][:, 512 * dg:512 * (dg + 1)],
                             a=psop[pp][:], x2=xt[par][:, 512 * dg:512 * (dg + 1)]:
                             e.tensor_tensor(o, a, x2, ALU.add), waits=wl, inc=b.ve)
                        psop_read[pp] = b.ve.n
                    x1_ve = b.ve.n
                    xt_ap3[par] = x1_ve
                    b.dma("scalar", x1_scr[r0:r0 + 128, :], x1t[par][:], s_x1[par],
                          waits=[(b.ve, x1_ve)])
                    b.op("vector", lambda e, o=s1v[:, c:c + 1], a=x1t[par][:]:
                         e.tensor_reduce(o, a, AX.X, ALU.add),
                         waits=[(b.ve, x1_ve)], inc=b.ve)
                    b.op("vector", lambda e, o=mv[:, c:c + 1], a=s1v[:, c:c + 1]:
                         e.tensor_scalar(o, a, -1.0 / D, None, ALU.mult),
                         waits=[(b.ve, b.ve.n), (b.ac, ln_ac3[c])], inc=b.ve)
                    mv_ve3 = b.ve.n
                    b.op("scalar", lambda e, a=x1t[par][:], m2=mv[:, c:c + 1]:
                         e.activation(sqscr[:], a, AF.Square, bias=m2),
                         waits=[(b.ve, mv_ve3), (b.ve, max(lnsq_read))], inc=b.ac)
                    lnsq_ac[par] = b.ac.n
                    b.op("vector", lambda e, o=sqv[:, c:c + 1]:
                         e.tensor_reduce(o, sqscr[:], AX.X, ALU.add),
                         waits=[(b.ac, b.ac.n)], inc=b.ve)
                    lnsq_read[0] = lnsq_read[1] = b.ve.n
                    b.op("scalar", lambda e, o=rstdv[:, c:c + 1], a=sqv[:, c:c + 1]:
                         e.activation(o, a, AF.Ln, scale=1.0 / D, bias=cst[:, 0:1]),
                         waits=[(b.ve, max(lnsq_read))], inc=b.ac)
                    b.op("scalar", lambda e, o=rstdv[:, c:c + 1], a=rstdv[:, c:c + 1]:
                         e.activation(o, a, AF.Exp, scale=-0.5),
                         waits=[(b.ac, b.ac.n)], inc=b.ac)
                    ln_ac3[c] = b.ac.n
                    b.op("vector", lambda e, o=xlnt[par][:], a=x1t[par][:],
                         m2=mv[:, c:c + 1], r2=rstdv[:, c:c + 1]:
                         e.tensor_scalar(o, a, m2, r2, ALU.add, ALU.mult),
                         waits=[(b.ac, ln_ac3[c]), (b.pe, xlnt_read3[par]),
                                (s_x2T[par], s_x2T[par].n)], inc=b.ve)
                    ap_ve = b.ve.n
                    for k in range(16):
                        tp = k % 2
                        b.op("tensor", lambda e, o=pst3[tp][:],
                             a=xlnt[par][:, 128 * k:128 * (k + 1)]:
                             e.transpose(o, a, identb[:]),
                             waits=[(b.ve, ap_ve), (b.ve, pst_read[tp])],
                             inc=b.pe)
                        b.op("vector", lambda e, o=x2st[par][:, 128 * k:128 * (k + 1)],
                             a=pst3[tp][:]: e.tensor_copy(o, a),
                             waits=b.w(b.pe), inc=b.ve, drain=(k == 15))
                        if k == 15:
                            xlnt_read3[par] = b.pe.n
                        pst_read[tp] = b.ve.n
                    b.dma("scalar",
                          x2T_scr[:, r0:r0 + 128].rearrange("(k p) q -> p k q", p=128),
                          x2st[par][:].rearrange("p (k q) -> p k q", q=128),
                          s_x2T[par], waits=b.w(b.ve))
            b.barrier()
            b.flush(block)

        with ExitStack() as p4:
            psb = lambda name, shape, dt: p4.enter_context(nc.psum_tensor(name, shape, dt))
            sb4 = lambda name, shape, dt: p4.enter_context(nc.sbuf_tensor(name, shape, dt))
            hT = sb4("hT", [128, 64 * 512], BF16)
            xl2 = sb4("xl2", [128, 16 * 512], BF16)
            wgu = [sb4(f"wgu{i}", [128, 16 * 1024], BF16) for i in range(3)]
            x1p = [sb4(f"x1p{i}", [128, 512], F32) for i in range(2)]
            sg = sb4("sg", [128, 4 * 512], BF16)
            hsq = [sb4(f"hsq{i}", [128, 512], BF16) for i in range(2)]
            invr = sb4("invr", [1, 512], F32)
            inv_hi = sb4("inv_hi", [1, 512], BF16)
            inv_lo = sb4("inv_lo", [1, 512], BF16)
            invc = sb4("invc", [128, 4], F32)
            x2p = [sb4(f"x2p{i}", [128, 512], F32) for i in range(2)]
            pw = [psb(f"pw{i}", [128, 512], F32) for i in range(8)]
            s_xl2 = b.sem("s_xl2")
            s_x1p = [b.sem(f"s_x1p{i}") for i in range(2)]
            s_wgu = [b.sem(f"s_wgu{i}") for i in range(3)]
            s_out = [b.sem(f"s_out{i}") for i in range(2)]

            NGRP = T // 512
            silu_cnt = [0, 0, 0, 0]
            bank_read = [0, 0, 0, 0]
            sg_read = [0, 0, 0, 0]
            hsq_pe = [0, 0]
            wbuf_pe = [0, 0, 0]
            evac_cnt = [0] * 8
            x1p_read = [0, 0]
            ss_edge = (None, 0)
            inv55_read = 0

            def gu_window(habs):
                rb = habs // 2048
                off = habs % 2048
                return qc_out[3072 * rb:3072 * rb + 2048, off:off + 1024].rearrange(
                    "(k p) h -> p k h", p=128)

            def load_gu(k, first=False):
                buf = (k + 2) % 3
                jH = k // 2
                habs = (0 if k % 2 == 0 else FF) + 1024 * jH
                b.dma("sync", wgu[buf][:].rearrange("p (k h) -> p k h", h=1024),
                      gu_window(habs), s_wgu[buf],
                      waits=[(b.pe, wbuf_pe[buf])] +
                            ([(b.cc, cc_q)] if first else []))

            def load_xl2(grp, wl):
                t0 = grp * 512
                b.dma("gpsimd", xl2[:].rearrange("p (k t) -> p k t", t=512),
                      x2T_scr[:, t0:t0 + 512].rearrange("(k p) t -> p k t", p=128),
                      s_xl2, waits=wl)
                xl2_cnt[0] = s_xl2.n

            xl2_cnt = [0]
            load_xl2(0, b.w(s_x2T[0], s_x2T[1]))
            for grp in range(NGRP):
                t0 = grp * 512
                xl2w = [(s_xl2, xl2_cnt[0])]
                for k in ((0, 1, 2) if grp == 0 else (1, 2)):
                    load_gu(k, first=(grp == 0 and k == 0))

                for jH in range(8):
                    bg, bu = (2 * jH + 2) % 3, (2 * jH + 3) % 3
                    gl = [(s_wgu[bg], s_wgu[bg].n)]
                    ul = [(s_wgu[bu], s_wgu[bu].n)]
                    for rnd in range(2):
                        for hx4 in range(4):
                            hx = rnd * 4 + hx4
                            for dc in range(16):
                                wl2 = (gl + [(b.ve, bank_read[hx4]),
                                             (b.ve, evac_cnt[hx4])] + xl2w) \
                                    if dc == 0 else []
                                b.op("tensor", lambda e, o=pw[hx4][:], bg2=bg,
                                     l=wgu[bg][:, 1024 * dc + 128 * hx:1024 * dc + 128 * (hx + 1)],
                                     r=xl2[:, 512 * dc:512 * (dc + 1)], s=(dc == 0),
                                     t=(dc == 15):
                                     e.matmul(o, l, r, start=s, stop=t),
                                     waits=wl2, inc=(b.pe if dc == 15 else None))
                            gate_pe = b.pe.n
                            b.op("scalar", lambda e, o=sg[:, 512 * hx4:512 * (hx4 + 1)],
                                 a=pw[hx4][:]:
                                 e.activation(o, a, AF.Silu),
                                 waits=[(b.pe, gate_pe), (b.ve, sg_read[hx4])],
                                 inc=b.ac, drain=False)
                            silu_cnt[hx4] = b.ac.n
                        if rnd == 1:
                            wbuf_pe[bg] = b.pe.n
                        for hx4 in range(4):
                            hx = rnd * 4 + hx4
                            jc = 8 * jH + hx
                            for dc in range(16):
                                wl2 = (ul + [(b.ac, silu_cnt[hx4])]) if dc == 0 else []
                                b.op("tensor", lambda e, o=pw[hx4][:],
                                     l=wgu[bu][:, 1024 * dc + 128 * hx:1024 * dc + 128 * (hx + 1)],
                                     r=xl2[:, 512 * dc:512 * (dc + 1)], s=(dc == 0),
                                     t=(dc == 15):
                                     e.matmul(o, l, r, start=s, stop=t),
                                     waits=wl2, inc=(b.pe if dc == 15 else None))
                            up_pe = b.pe.n
                            b.op("vector", lambda e, o=hT[:, 512 * jc:512 * (jc + 1)],
                                 a=sg[:, 512 * hx4:512 * (hx4 + 1)], p=pw[hx4][:]:
                                 e.tensor_tensor(o, a, p, ALU.mult),
                                 waits=[(b.pe, up_pe)], inc=b.ve, drain=False)
                            bank_read[hx4] = b.ve.n
                            sg_read[hx4] = b.ve.n
                            hp = jc % 2
                            b.op("scalar", lambda e, o=hsq[hp][:],
                                 a=hT[:, 512 * jc:512 * (jc + 1)]:
                                 e.activation(o, a, AF.Square),
                                 waits=[(b.ve, b.ve.n), (b.pe, hsq_pe[hp])],
                                 inc=b.ac, drain=False)
                            wl3 = b.w(b.ac)
                            if jc == 0:
                                wl3 = wl3 + [(b.ve, evac_cnt[4])]
                                if ss_edge[0] is not None:
                                    wl3 = wl3 + [ss_edge]
                            b.op("tensor", lambda e, o=pw[4][0:1, :], a=hsq[hp][:],
                                 s=(jc == 0), t=(jc == 63):
                                 e.matmul(o, ones_c[:], a, start=s, stop=t),
                                 waits=wl3, inc=b.pe)
                            hsq_pe[hp] = b.pe.n
                        if rnd == 1:
                            wbuf_pe[bu] = b.pe.n
                    for k in (2 * jH + 3, 2 * jH + 4):
                        if k < 16:
                            load_gu(k)
                hT_done = b.ve.n
                gu_pe_last = b.pe.n
                if grp + 1 < NGRP:
                    load_xl2(grp + 1, [(b.pe, gu_pe_last)])

                b.op("scalar", lambda e: e.activation(invr[:], pw[4][0:1, :], AF.Ln,
                                                      scale=1.0 / FF, bias=cst[0:1, 1:2]),
                     waits=[(b.pe, gu_pe_last)], inc=b.ac)
                ss_edge = (b.ac, b.ac.n)
                b.op("scalar", lambda e: e.activation(invr[:], invr[:], AF.Exp,
                                                      scale=-0.5),
                     waits=[(b.ac, b.ac.n)], inc=b.ac)
                b.op("vector", lambda e: e.tensor_copy(inv_hi[:], invr[:]),
                     waits=b.w(b.ac) + [(b.pe, inv55_read)], inc=b.ve)
                b.op("vector", lambda e: e.tensor_tensor(inv_lo[:], invr[:], inv_hi[:],
                                                         ALU.subtract),
                     waits=[(b.ve, b.ve.n)], inc=b.ve)
                for sub in range(4):
                    b.op("tensor", lambda e, o=pw[5][:, sub:sub + 1],
                         l=inv_hi[0:1, 128 * sub:128 * (sub + 1)]:
                         e.matmul(o, l, ones_r[0:1, 0:1], start=True, stop=False),
                         waits=(b.w(b.ve) + [(b.ve, evac_cnt[5])]) if sub == 0 else ())
                    b.op("tensor", lambda e, o=pw[5][:, sub:sub + 1],
                         l=inv_lo[0:1, 128 * sub:128 * (sub + 1)]:
                         e.matmul(o, l, ones_r[0:1, 0:1], start=False, stop=True),
                         inc=b.pe)
                inv55_read = b.pe.n
                b.op("vector", lambda e: e.tensor_copy(invc[:], pw[5][:, 0:4]),
                     waits=b.w(b.pe), inc=b.ve)
                inv_ve = b.ve.n
                evac_cnt[5] = max(evac_cnt[5], inv_ve)

                wqp = [wgu[0][:, 0:8 * 1024], wgu[1][:, 0:8 * 1024]]
                for wave in range(2):
                    for rb in range(8):
                        wp_par = rb % 2
                        b.dma("sync", wqp[wp_par].rearrange("p (k d) -> p k d", d=1024),
                              qc_out[3072 * rb + 2048:3072 * rb + 3072,
                                     1024 * wave:1024 * (wave + 1)].rearrange(
                                         "(k p) d -> p k d", p=128),
                              s_wgu[wp_par], waits=[(b.pe, wbuf_pe[wp_par])])
                        for pair in range(8):
                            sub, dgw = pair // 2, pair % 2
                            for hx in range(8):
                                j = 8 * rb + hx
                                wl2 = []
                                if hx == 0:
                                    wl2 = [(s_wgu[wp_par], s_wgu[wp_par].n)]
                                    if rb == 0:
                                        wl2 += [(b.ve, hT_done), (b.ve, inv_ve),
                                                (b.ve, evac_cnt[pair])]
                                b.op("tensor", lambda e, o=pw[pair][:], w2=wp_par,
                                     l=hT[:, 512 * j + 128 * sub:512 * j + 128 * (sub + 1)],
                                     r=wqp[wp_par][:, 1024 * hx + 512 * dgw:1024 * hx + 512 * (dgw + 1)],
                                     s=(rb == 0 and hx == 0), t=(rb == 7 and hx == 7):
                                     e.matmul(o, l, r, start=s, stop=t),
                                     waits=wl2,
                                     inc=(b.pe if hx == 7 else None))
                        wbuf_pe[wp_par] = b.pe.n
                    if wave == 0 and grp + 1 < NGRP:
                        load_gu(0)
                    proj_pe = b.pe.n
                    for pair in range(8):
                        sub, dgw = pair // 2, pair % 2
                        dg = wave * 2 + dgw
                        xp = pair % 2
                        b.dma("sync", x1p[xp][:],
                              x1_scr[t0 + 128 * sub:t0 + 128 * (sub + 1),
                                     512 * dg:512 * (dg + 1)], s_x1p[xp],
                              waits=[(b.ve, x1p_read[xp])] +
                              (b.w(s_x1[0], s_x1[1]) if (grp == 0 and wave == 0
                                                         and pair < 2) else []))
                        wl = [(b.pe, proj_pe), (s_out[xp], s_out[xp].n),
                              (s_x1p[xp], s_x1p[xp].n)]
                        b.op("vector", lambda e, o=x2p[xp][:], p=pw[pair][:],
                             iv=invc[:, sub:sub + 1], x1=x1p[xp][:]:
                             e.scalar_tensor_tensor(o, p, iv, x1, ALU.mult, ALU.add),
                             waits=wl, inc=b.ve)
                        evac_cnt[pair] = b.ve.n
                        x1p_read[xp] = b.ve.n
                        b.dma("scalar",
                              out[t0 + 128 * sub:t0 + 128 * (sub + 1), 512 * dg:512 * (dg + 1)],
                              x2p[xp][:], s_out[xp], waits=b.w(b.ve))

            if debug:
                s_tap = b.sem("s_tap")
                tw = b.w(s_out[0], s_out[1], s_qst[0], s_qst[1], s_ost,
                         s_x1[0], s_x1[1], s_x2T[0], s_x2T[1])
                for name, (dst, src) in taps.items():
                    b.dma("gpsimd", dst, src, s_tap, waits=tw)
                b.op("gpsimd", lambda e: e.nop(), waits=b.w(s_tap))
            b.op("gpsimd", lambda e: e.nop(), waits=b.w(s_out[0], s_out[1]))
            b.op("sync", lambda e: e.nop(), waits=b.w(s_out[0], s_out[1]))
            b.flush(block)

    return nc



def _host_consts():
    identb = np.eye(128, dtype=BFD)
    tk = np.arange(128)[:, None]
    tq = np.arange(512)[None, :]
    masks = np.concatenate(
        [np.where((tk + 128 * mi) <= tq, 0.0, -30000.0).astype(BFD)
         for mi in range(4)], axis=1)
    return identb, masks


def _oidx_for_core(c, S):
    T = S // 2
    OH = T // 2
    b_, g = c // 2, c % 2
    qpr = S // OH
    idx = np.zeros((128, 32), dtype=np.int32)
    p = np.arange(128)
    for hf in range(2):
        for f in range(16):
            rank = 2 * b_ + f // 8
            hl = f % 8
            R = rank * 512 + (hl % 4) * 128 + p
            q = (T * g + OH * hf) // OH
            idx[:, f + 16 * hf] = R * qpr + q
    return idx


def kernel(L0_attn_h, L0_attn_Wq, L0_attn_bq, L0_attn_Wk, L0_attn_bk,
           L0_attn_Wv, L0_attn_bv, L0_attn_Wo, L0_attn_bo,
           L0_ffn_w_gate_up, L0_ffn_w_proj, _debug=False):
    h = np.asarray(L0_attn_h, dtype=np.float32)
    Wq = np.asarray(L0_attn_Wq, dtype=np.float32)
    bq = np.asarray(L0_attn_bq, dtype=np.float32)
    Wk = np.asarray(L0_attn_Wk, dtype=np.float32)
    bk = np.asarray(L0_attn_bk, dtype=np.float32)
    Wv = np.asarray(L0_attn_Wv, dtype=np.float32)
    bv = np.asarray(L0_attn_bv, dtype=np.float32)
    Wo = np.asarray(L0_attn_Wo, dtype=np.float32)
    bo = np.asarray(L0_attn_bo, dtype=np.float32)
    Wgu = np.asarray(L0_ffn_w_gate_up, dtype=np.float32)
    Wp = np.asarray(L0_ffn_w_proj, dtype=np.float32)

    B, S, _ = h.shape
    T = S // 2
    nc = build_nc(S=S, debug=_debug)
    identb, masks = _host_consts()

    in_maps = []
    for c in range(8):
        b_, g = c // 2, c % 2
        in_maps.append({
            "x_full": np.ascontiguousarray(h[b_]),
            "x_res": np.ascontiguousarray(h[b_, g * T:(g + 1) * T]),
            "wq_in": np.ascontiguousarray(Wq[1024 * g:1024 * (g + 1)]),
            "wk_in": np.ascontiguousarray(Wk[1024 * g:1024 * (g + 1)]),
            "wv_in": np.ascontiguousarray(Wv[1024 * g:1024 * (g + 1)]),
            "bq_in": np.ascontiguousarray(bq[1024 * g:1024 * (g + 1)].reshape(1, -1)),
            "bk_in": np.ascontiguousarray(bk[1024 * g:1024 * (g + 1)].reshape(1, -1)),
            "bv_in": np.ascontiguousarray(bv[1024 * g:1024 * (g + 1)].reshape(1, -1)),
            "wo_in": np.ascontiguousarray(Wo),
            "bo_in": np.ascontiguousarray(bo.reshape(1, -1)),
            "wgu_in": np.ascontiguousarray(Wgu[2048 * c:2048 * (c + 1)]),
            "wp_in": np.ascontiguousarray(Wp[:, 1024 * c:1024 * (c + 1)]),
            "identb_in": identb,
            "masks_in": masks,
            "oidx_in": _oidx_for_core(c, S),
        })

    res = run_bass_kernel_spmd(nc, in_maps, core_ids=list(range(8)))
    outp = np.empty((B, S, D), dtype=np.float32)
    for c in range(8):
        b_, g = c // 2, c % 2
        outp[b_, g * T:(g + 1) * T] = res.results[c]["out"]
    if _debug:
        return outp, res
    return outp


# version 14
# speedup vs baseline: 1.0025x; 1.0025x over previous
import numpy as np
import ml_dtypes
from contextlib import ExitStack

import concourse.bass as bass
import concourse.mybir as mybir
from concourse.bass_utils import run_bass_kernel_spmd

F32 = mybir.dt.float32
BF16 = mybir.dt.bfloat16
I32 = mybir.dt.int32
AF = mybir.ActivationFunctionType
ALU = mybir.AluOpType
AX = mybir.AxisListType
BFD = ml_dtypes.bfloat16

D = 2048
HD = 128
NHL = 8
FF = 8192
GQ = 64
SM_SCALE = float(np.float32(1.0) / np.sqrt(np.float32(HD)))
LN_EPS = 1e-5
RMS_EPS = float(np.finfo(np.float32).eps)
QEPS = float(np.float32(BFD(1e-8)))
TH_C = float(np.float32(513.0 / 1024.0))

ENGS = ("tensor", "vector", "scalar", "gpsimd", "sync")


class _Ev:
    __slots__ = ("fn", "waits", "incs")

    def __init__(self, fn, waits, incs):
        self.fn, self.waits, self.incs = fn, waits, incs


class _S:
    __slots__ = ("h", "n")


class Builder:
    def __init__(self, nc, ctx, S):
        self.nc = nc
        self.ctx = ctx
        self.S = S
        self.ev = {e: [] for e in ENGS}
        self.sems = {}
        self.pe = self.sem("pe")
        self.ve = self.sem("ve")
        self.ac = self.sem("ac")
        self.po = self.sem("po")
        self.bar = self.sem("bar")
        self.cc = self.sem("cc")

    def sem(self, name):
        if name not in self.sems:
            s = _S()
            s.h = self.ctx.enter_context(self.nc.semaphore(name))
            s.n = 0
            self.sems[name] = s
        return self.sems[name]

    DRAIN_ENGS = ("vector", "scalar", "gpsimd")

    def op(self, eng, fn, waits=(), inc=None, n=1, raw_inc=False, drain=None):
        if drain is False:
            raw_inc = True
        if (inc is not None and n == 1 and not raw_inc
                and eng in self.DRAIN_ENGS):
            self.ev[eng].append(_Ev(fn, list(waits), []))
            inc.n += 1
            self.ev[eng].append(_Ev(lambda e: e.drain(), [], [(inc, 1)]))
            return inc.n
        incs = []
        if inc is not None:
            inc.n += n
            incs.append((inc, n))
        self.ev[eng].append(_Ev(fn, list(waits), incs))
        return inc.n if inc is not None else 0

    def dma(self, q, out, in_, sem, waits=(), **kw):
        self.op(q, lambda e, o=out, i=in_, kw=kw: e.dma_start(out=o, in_=i, **kw),
                waits=waits, inc=sem, n=16)
        return sem.n

    def w(self, *sems):
        return [(s, s.n) for s in sems if s.n > 0]

    def barrier(self):
        t = self.bar.n + len(ENGS)
        for e in ENGS:
            if e in self.DRAIN_ENGS:
                self.op(e, lambda eng: eng.drain(), inc=self.bar, raw_inc=True)
            else:
                self.op(e, lambda eng: eng.nop(), inc=self.bar, raw_inc=True)
        for e in ENGS:
            self.ev[e].append(_Ev(None, [(self.bar, t)], []))

    def flush(self, block):
        for ename in ENGS:
            evs = self.ev[ename]
            if not evs:
                continue
            self.ev[ename] = []

            def run(eng, evs=evs):
                for ev in evs:
                    for (s, v) in ev.waits:
                        if v > 0:
                            eng.wait_ge(s.h, v)
                    if ev.fn is not None:
                        inst = ev.fn(eng)
                        for (s, k) in ev.incs:
                            inst.then_inc(s.h, k)
            getattr(block, ename)(run)


def build_nc(S=4096, debug=False):
    nc = bass.Bass("TRN2", target_bir_lowering=False, debug=False)
    T = S // 2
    OH = T // 2
    ctx = ExitStack()

    ap = lambda t: t.ap()
    x_full = ap(nc.dram_tensor("x_full", [S, D], F32, kind="ExternalInput"))
    x_res = ap(nc.dram_tensor("x_res", [T, D], F32, kind="ExternalInput"))
    wq_in = ap(nc.dram_tensor("wq_in", [NHL * HD, D], F32, kind="ExternalInput"))
    wk_in = ap(nc.dram_tensor("wk_in", [NHL * HD, D], F32, kind="ExternalInput"))
    wv_in = ap(nc.dram_tensor("wv_in", [NHL * HD, D], F32, kind="ExternalInput"))
    bq_in = ap(nc.dram_tensor("bq_in", [1, NHL * HD], F32, kind="ExternalInput"))
    bk_in = ap(nc.dram_tensor("bk_in", [1, NHL * HD], F32, kind="ExternalInput"))
    bv_in = ap(nc.dram_tensor("bv_in", [1, NHL * HD], F32, kind="ExternalInput"))
    wo_in = ap(nc.dram_tensor("wo_in", [D, D], F32, kind="ExternalInput"))
    bo_in = ap(nc.dram_tensor("bo_in", [1, D], F32, kind="ExternalInput"))
    wgu_in = ap(nc.dram_tensor("wgu_in", [2048, 2048], F32, kind="ExternalInput"))
    wp_in = ap(nc.dram_tensor("wp_in", [2048, 1024], F32, kind="ExternalInput"))
    identb_in = ap(nc.dram_tensor("identb_in", [128, 128], BF16, kind="ExternalInput"))
    masks_in = ap(nc.dram_tensor("masks_in", [128, 4 * 512], BF16, kind="ExternalInput"))
    oidx_in = ap(nc.dram_tensor("oidx_in", [128, 32], I32, kind="ExternalInput"))

    out = ap(nc.dram_tensor("out", [T, D], F32, kind="ExternalOutput"))

    wq_bf = ap(nc.dram_tensor("wq_bf", [NHL * HD, D], BF16, kind="Internal"))
    wk_bf = ap(nc.dram_tensor("wk_bf", [NHL * HD, D], BF16, kind="Internal"))
    wv_bf = ap(nc.dram_tensor("wv_bf", [NHL * HD, D], BF16, kind="Internal"))
    wo_bf = ap(nc.dram_tensor("wo_bf", [D, D], BF16, kind="Internal"))
    qc_in = ap(nc.dram_tensor("qc_in", [3072, 2048], BF16, kind="Internal"))
    qc_out = ap(nc.dram_tensor("qc_out", [8 * 3072, 2048], BF16, kind="Internal",
                               addr_space="Shared"))
    qT_scr = ap(nc.dram_tensor("qT_scr", [NHL * HD, S], BF16, kind="Internal"))
    kT_scr = ap(nc.dram_tensor("kT_scr", [NHL * HD, S], BF16, kind="Internal"))
    v_scr = ap(nc.dram_tensor("v_scr", [S, NHL * HD], BF16, kind="Internal"))
    o_loc = ap(nc.dram_tensor("o_loc", [NHL * HD, S], BF16, kind="Internal"))
    o_all_a = ap(nc.dram_tensor("o_all_a", [8 * 512, S], BF16, kind="Internal",
                                addr_space="Shared"))
    o_all_b = ap(nc.dram_tensor("o_all_b", [8 * 512, S], BF16, kind="Internal",
                                addr_space="Shared"))
    x1_scr = ap(nc.dram_tensor("x1_scr", [T, D], F32, kind="Internal"))
    x2T_scr = ap(nc.dram_tensor("x2T_scr", [D, T], BF16, kind="Internal"))

    taps = {}
    if debug:
        taps["tap_qT"] = (ap(nc.dram_tensor("tap_qT", [NHL * HD, S], BF16, kind="ExternalOutput")), qT_scr)
        taps["tap_kT"] = (ap(nc.dram_tensor("tap_kT", [NHL * HD, S], BF16, kind="ExternalOutput")), kT_scr)
        taps["tap_v"] = (ap(nc.dram_tensor("tap_v", [S, NHL * HD], BF16, kind="ExternalOutput")), v_scr)
        taps["tap_o"] = (ap(nc.dram_tensor("tap_o", [NHL * HD, S], BF16, kind="ExternalOutput")), o_loc)
        taps["tap_x1"] = (ap(nc.dram_tensor("tap_x1", [T, D], F32, kind="ExternalOutput")), x1_scr)
        taps["tap_qc"] = (ap(nc.dram_tensor("tap_qc", [3072, 2048], BF16, kind="ExternalOutput")), qc_out[0:3072, :])
        tap_ln = ap(nc.dram_tensor("tap_ln", [128, 12], F32, kind="ExternalOutput"))
        tap_xln = ap(nc.dram_tensor("tap_xln", [128, 2048], BF16, kind="ExternalOutput"))

    with ctx:
        b = Builder(nc, ctx, S)
        sb = lambda name, shape, dt: ctx.enter_context(nc.sbuf_tensor(name, shape, dt))

        identb = sb("identb", [128, 128], BF16)
        oidx = sb("oidx", [128, 32], I32)
        ones_r = sb("ones_r", [1, 512], BF16)
        ones_c = sb("ones_c", [128, 1], BF16)
        s1v = sb("s1v", [128, 2], F32)
        sqv = sb("sqv", [128, 2], F32)
        mv = sb("mv", [128, 2], F32)
        msqv = sb("msqv", [128, 2], F32)
        varv = sb("varv", [128, 2], F32)
        rstdv = sb("rstdv", [128, 2], F32)
        cst = sb("cst", [128, 2], F32)

        s_consts = b.sem("s_consts")
        s_cast = b.sem("s_cast")
        s_xt = [b.sem(f"s_xt{i}") for i in range(2)]

        block = ctx.enter_context(nc.Block())

        b.dma("sync", identb[:], identb_in, s_consts)
        b.dma("sync", oidx[:], oidx_in, s_consts)
        b.dma("gpsimd", wq_bf, wq_in, s_cast)
        b.dma("gpsimd", wk_bf, wk_in, s_cast)
        b.dma("gpsimd", wv_bf, wv_in, s_cast)
        b.dma("gpsimd", wo_bf, wo_in, s_cast)
        b.op("vector", lambda e: e.memset(ones_r[:], 1.0), inc=b.ve)
        b.op("vector", lambda e: e.memset(cst[:, 0:1], LN_EPS), inc=b.ve)
        b.op("vector", lambda e: e.memset(cst[:, 1:2], RMS_EPS), inc=b.ve)
        b.op("vector", lambda e: e.memset(ones_c[:], 1.0), inc=b.ve)


        TC = min(S, 2048)
        NHALF = S // TC
        with ExitStack() as p1:
            psb = lambda name, shape, dt: p1.enter_context(nc.psum_tensor(name, shape, dt))
            sb1 = lambda name, shape, dt: p1.enter_context(nc.sbuf_tensor(name, shape, dt))
            xln1T = sb1("xln1T", [128, 16 * TC], BF16)
            wT = [sb1(f"wT{i}", [128, 16 * 1024], BF16) for i in range(2)]
            qstage = [sb1(f"qstage{i}", [128, 512], BF16) for i in range(4)]
            xt = [sb1(f"p1xt{i}", [128, D], F32) for i in range(2)]
            xlnt = [sb1(f"p1xlnt{i}", [128, D], BF16) for i in range(2)]
            sqscr = sb1("p1sqscr", [128, D], F32)
            bq_sb = sb1("bq_sb", [1, NHL * HD], BF16)
            bk_sb = sb1("bk_sb", [1, NHL * HD], BF16)
            bv_sb = sb1("bv_sb", [1, NHL * HD], BF16)
            b.dma("gpsimd", bq_sb[:], bq_in, s_cast)
            b.dma("gpsimd", bk_sb[:], bk_in, s_cast)
            b.dma("gpsimd", bv_sb[:], bv_in, s_cast)
            psp = [psb(f"p1_psp{i}", [128, 512], F32) for i in range(4)]
            pst1 = [psb(f"p1_pst{i}", [128, 128], BF16) for i in range(2)]
            s_wT = [b.sem(f"s_wT{i}") for i in range(2)]
            s_qst = [b.sem(f"s_qst{i}") for i in range(4)]

            pst_read = [0, 0]
            psp_read = [0, 0, 0, 0]
            sqv_read = [0, 0]
            ln_ac = [0, 0]
            xt_sq = [0, 0]
            xt_ap = [0, 0]
            xlnt_read = [0, 0]

            xln_tile_ve = [0] * 16

            def emit_ln_tile(half, i):
                par = i % 2
                c = par
                t0 = half * TC + i * 128
                wl = [(b.ve, xt_ap[par]), (b.ac, xt_sq[par])] if (half + i) else []
                b.dma("sync", xt[par][:], x_full[t0:t0 + 128, :], s_xt[par], waits=wl)
                lw = [(s_xt[par], s_xt[par].n)]
                b.op("vector", lambda e, o=s1v[:, c:c + 1], a=xt[par][:]:
                     e.tensor_reduce(o, a, AX.X, ALU.add), waits=lw, inc=b.ve)
                b.op("vector", lambda e, o=mv[:, c:c + 1], a=s1v[:, c:c + 1]:
                     e.tensor_scalar(o, a, -1.0 / D, None, ALU.mult),
                     waits=[(b.ve, b.ve.n), (b.ac, ln_ac[c])], inc=b.ve)
                mv_ve = b.ve.n
                b.op("scalar", lambda e, a=xt[par][:], m2=mv[:, c:c + 1],
                     o=sqscr[:]:
                     e.activation(o, a, AF.Square, bias=m2),
                     waits=lw + [(b.ve, mv_ve), (b.ve, max(sqv_read))],
                     inc=b.ac)
                sq_ac = b.ac.n
                xt_sq[par] = sq_ac
                b.op("vector", lambda e, o=sqv[:, c:c + 1], a=sqscr[:]:
                     e.tensor_reduce(o, a, AX.X, ALU.add),
                     waits=[(b.ac, sq_ac)], inc=b.ve)
                sqv_read[par] = b.ve.n
                b.op("scalar", lambda e, o=rstdv[:, c:c + 1], a=sqv[:, c:c + 1]:
                     e.activation(o, a, AF.Ln, scale=1.0 / D, bias=cst[:, 0:1]),
                     waits=[(b.ve, sqv_read[par])], inc=b.ac)
                b.op("scalar", lambda e, o=rstdv[:, c:c + 1], a=rstdv[:, c:c + 1]:
                     e.activation(o, a, AF.Exp, scale=-0.5),
                     waits=[(b.ac, b.ac.n)], inc=b.ac)
                ln_ac[c] = b.ac.n
                b.op("vector", lambda e, o=xlnt[par][:], a=xt[par][:],
                     m2=mv[:, c:c + 1], r2=rstdv[:, c:c + 1]:
                     e.tensor_scalar(o, a, m2, r2, ALU.add, ALU.mult),
                     waits=[(b.ac, ln_ac[c]), (b.pe, xlnt_read[par])],
                     inc=b.ve)
                apply_ve = b.ve.n
                xt_ap[par] = apply_ve
                for k in range(16):
                    tp = k % 2
                    b.op("tensor", lambda e, o=pst1[tp][:],
                         a=xlnt[par][:, 128 * k:128 * (k + 1)]:
                         e.transpose(o, a, identb[:]),
                         waits=[(b.ve, apply_ve), (b.ve, pst_read[tp])],
                         inc=b.pe)
                    b.op("vector", lambda e,
                         o=xln1T[:, TC * k + 128 * i:TC * k + 128 * (i + 1)],
                         a=pst1[tp][:]: e.tensor_copy(o, a),
                         waits=b.w(b.pe), inc=b.ve, drain=False)
                    pst_read[tp] = b.ve.n
                xln_tile_ve[i] = b.ve.n
                xlnt_read[par] = b.pe.n

            def load_w(wpar, src_bf, wl):
                base = s_wT[wpar].n
                for k in range(16):
                    b.dma("sync", wT[wpar][:, 1024 * k:1024 * (k + 1)],
                          src_bf[0:1024, 128 * k:128 * (k + 1)], s_wT[wpar],
                          waits=wl if k == 0 else (), transpose=True)
                return base

            pcnt = [0]

            def emit_qk_tg(wpar, bias_sb, dst, half, tg, base, first):
                for hd in range(8):
                    pp = pcnt[0] % 4
                    pcnt[0] += 1
                    for dc in range(16):
                        wl2 = ([(b.ve, xln_tile_ve[4 * tg + 3]),
                                (b.ac, psp_read[pp])] if dc == 0 else [])
                        if first and hd == 0:
                            wl2 = wl2 + [(s_wT[wpar], base + 16 * (dc + 1))]
                        b.op("tensor", lambda e, o=psp[pp][:],
                             l=wT[wpar][:, 1024 * dc + 128 * hd:1024 * dc + 128 * (hd + 1)],
                             r=xln1T[:, TC * dc + 512 * tg:TC * dc + 512 * (tg + 1)],
                             s=(dc == 0):
                             e.matmul(o, l, r, start=s, stop=False),
                             waits=wl2)
                    b.op("tensor", lambda e, o=psp[pp][:],
                         l=bias_sb[0:1, 128 * hd:128 * (hd + 1)]:
                         e.matmul(o, l, ones_r[0:1, :], start=False, stop=True),
                         inc=b.pe)
                    b.op("scalar", lambda e, o=qstage[pp][:], a=psp[pp][:]:
                         e.activation(o, a, AF.Copy),
                         waits=b.w(b.pe) + [(s_qst[pp], s_qst[pp].n)],
                         inc=b.ac)
                    psp_read[pp] = b.ac.n
                    b.dma("scalar", dst[128 * hd:128 * (hd + 1),
                                        half * TC + 512 * tg:half * TC + 512 * (tg + 1)],
                          qstage[pp][:], s_qst[pp],
                          waits=[(b.ac, b.ac.n)])

            for half in range(NHALF):
                base_q = load_w(0, wq_bf,
                                b.w(s_cast) + (b.w(b.pe) if half else []))
                base_k = load_w(1, wk_bf, [])
                for tg in range(TC // 512):
                    if half == 0:
                        for i in range(4 * tg, 4 * tg + 4):
                            emit_ln_tile(0, i)
                    emit_qk_tg(0, bq_sb, qT_scr, half, tg, base_q, tg == 0)
                    emit_qk_tg(1, bk_sb, kT_scr, half, tg, base_k, tg == 0)
                base_v = load_w(0, wv_bf, b.w(b.pe))
                for tt in range(TC // 128):
                    for hg in range(2):
                        pp = pcnt[0] % 4
                        pcnt[0] += 1
                        for dc in range(16):
                            wl2 = ([(b.ve, xln_tile_ve[tt]),
                                    (b.ac, psp_read[pp])] if dc == 0 else [])
                            if tt == 0 and hg == 0:
                                wl2 = wl2 + [(s_wT[0], base_v + 16 * (dc + 1))]
                            b.op("tensor", lambda e, o=psp[pp][:],
                                 l=xln1T[:, TC * dc + 128 * tt:TC * dc + 128 * (tt + 1)],
                                 r=wT[0][:, 1024 * dc + 512 * hg:1024 * dc + 512 * (hg + 1)],
                                 s=(dc == 0):
                                 e.matmul(o, l, r, start=s, stop=False),
                                 waits=wl2)
                        b.op("tensor", lambda e, o=psp[pp][:],
                             r=bv_sb[0:1, 512 * hg:512 * (hg + 1)]:
                             e.matmul(o, ones_r[0:1, 0:128], r, start=False, stop=True),
                             inc=b.pe)
                        b.op("scalar", lambda e, o=qstage[pp][:], a=psp[pp][:]:
                             e.activation(o, a, AF.Copy),
                             waits=b.w(b.pe) + [(s_qst[pp], s_qst[pp].n)],
                             inc=b.ac)
                        psp_read[pp] = b.ac.n
                        b.dma("scalar", v_scr[half * TC + 128 * tt:half * TC + 128 * (tt + 1),
                                              512 * hg:512 * (hg + 1)],
                              qstage[pp][:], s_qst[pp],
                              waits=[(b.ac, b.ac.n)])
                    if half + 1 < NHALF:
                        emit_ln_tile(half + 1, tt)

            wb = [sb1(f"q_wb{i}", [128, 2048], BF16) for i in range(2)]
            gs = sb1("q_gs", [128, 32], F32)
            meanb = sb1("q_meanb", [128, 32], BF16)
            qs32 = sb1("q_s32", [128, 32], F32)
            qth = sb1("q_th", [128, 32], F32)
            qnth = sb1("q_nth", [128, 32], F32)
            tpos = sb1("q_tpos", [128, 2048], BF16)
            tneg = sb1("q_tneg", [128, 2048], BF16)
            wqo = sb1("q_wqo", [128, 2048], BF16)
            asm_q = sb1("q_asm", [128, 16 * 256], BF16)
            s_wb = [b.sem(f"s_wb{i}") for i in range(2)]
            s_qc = b.sem("s_qc")
            wb_read = [0, 0]
            qpst_read = [0, 0]
            qtr_pe = [0]
            lnpst_ve = b.ve.n

            def quant_tile(i, src_ap, W, asm_cols):
                par = i % 2
                b.dma("gpsimd", wb[par][:, 0:W], src_ap, s_wb[par],
                      waits=[(b.ve, wb_read[par])])
                wbt = wb[par][:, 0:W]
                wb3 = wbt.rearrange("p (g k) -> p g k", k=GQ)
                ng = W // GQ
                wl = [(s_wb[par], s_wb[par].n)]
                b.op("vector", lambda e, o=gs[:, 0:ng], i3=wb3: e.tensor_reduce(
                    o, i3, AX.X, ALU.add, apply_absolute_value=True), waits=wl,
                    inc=b.ve)
                b.op("vector", lambda e, o=meanb[:, 0:ng], a=gs[:, 0:ng]:
                     e.tensor_scalar(o, a, 1.0 / GQ, None, ALU.mult),
                     waits=[(b.ve, b.ve.n)], inc=b.ve)
                b.op("vector", lambda e, o=meanb[:, 0:ng], a=meanb[:, 0:ng]:
                     e.tensor_scalar(o, a, QEPS, None, ALU.max),
                     waits=[(b.ve, b.ve.n)], inc=b.ve)
                b.op("vector", lambda e, o=qs32[:, 0:ng], a=meanb[:, 0:ng]:
                     e.tensor_copy(o, a), waits=[(b.ve, b.ve.n)], inc=b.ve)
                b.op("vector", lambda e, o=qth[:, 0:ng], a=qs32[:, 0:ng]:
                     e.tensor_scalar(o, a, TH_C, None, ALU.mult),
                     waits=[(b.ve, b.ve.n)], inc=b.ve)
                b.op("vector", lambda e, o=qnth[:, 0:ng], a=qth[:, 0:ng]:
                     e.tensor_scalar(o, a, -1.0, None, ALU.mult),
                     waits=[(b.ve, b.ve.n)], inc=b.ve)
                thb = qth[:, 0:ng].unsqueeze(2).broadcast_to([128, ng, GQ])
                nthb = qnth[:, 0:ng].unsqueeze(2).broadcast_to([128, ng, GQ])
                s32b = qs32[:, 0:ng].unsqueeze(2).broadcast_to([128, ng, GQ])
                tp3 = tpos[:, 0:W].rearrange("p (g k) -> p g k", k=GQ)
                tn3 = tneg[:, 0:W].rearrange("p (g k) -> p g k", k=GQ)

                b.op("vector", lambda e, o=tp3, a=wb3, c=thb:
                     e.tensor_tensor(o, a, c, ALU.is_gt),
                     waits=[(b.ve, b.ve.n)])
                b.op("vector", lambda e, o=tn3, a=wb3, c=nthb:
                     e.tensor_tensor(o, a, c, ALU.is_lt))
                b.op("vector", lambda e, o=tpos[:, 0:W], a=tpos[:, 0:W], c=tneg[:, 0:W]:
                     e.tensor_tensor(o, a, c, ALU.subtract))
                b.op("vector", lambda e, o=tp3, a=tp3, c=s32b:
                     e.tensor_tensor(o, a, c, ALU.mult))
                b.op("vector", lambda e, o=tneg[:, 0:W], a=tpos[:, 0:W], c=wbt:
                     e.tensor_tensor(o, a, c, ALU.subtract))
                b.op("vector", lambda e, o=wqo[:, 0:W], a=wbt, c=tneg[:, 0:W]:
                     e.tensor_tensor(o, a, c, ALU.add),
                     waits=[(b.pe, qtr_pe[0])], inc=b.ve)
                wb_read[par] = b.ve.n
                wqo_ve = b.ve.n
                nk = W // 128
                ib = i % 2
                for k in range(nk):
                    tp = k % 2
                    b.op("tensor", lambda e, o=pst1[tp][:],
                         a=wqo[:, 128 * k:128 * (k + 1)]:
                         e.transpose(o, a, identb[:]),
                         waits=[(b.ve, wqo_ve), (b.ac, qpst_read[tp]),
                                (b.ve, lnpst_ve)],
                         inc=b.pe)
                    qtr_pe[0] = b.pe.n
                    b.op("scalar", lambda e,
                         o=asm_q[:, asm_cols * k + 128 * ib:asm_cols * k + 128 * (ib + 1)],
                         a=pst1[tp][:]: e.activation(o, a, AF.Copy),
                         waits=b.w(b.pe) + [(s_qc, s_qc.n)], inc=b.ac,
                         drain=(k == nk - 1 and ib == 1))
                    qpst_read[tp] = b.ac.n

            for bb in range(8):
                for ii in range(2):
                    i = 2 * bb + ii
                    quant_tile(i, wgu_in[128 * i:128 * (i + 1), :], 2048, 256)
                b.dma("scalar",
                      qc_in[0:2048, 256 * bb:256 * (bb + 1)].rearrange(
                          "(k p) h -> p k h", p=128),
                      asm_q[:].rearrange("p (k h) -> p k h", h=256), s_qc,
                      waits=[(b.ac, b.ac.n)])
            for bb in range(8):
                for ii in range(2):
                    i = 2 * bb + ii
                    quant_tile(i, wp_in[128 * i:128 * (i + 1), :], 1024, 256)
                b.dma("scalar",
                      qc_in[2048:3072, 256 * bb:256 * (bb + 1)].rearrange(
                          "(k p) h -> p k h", p=128),
                      asm_q[:, 0:8 * 256].rearrange("p (k h) -> p k h", h=256), s_qc,
                      waits=[(b.ac, b.ac.n)])
            b.op("gpsimd", lambda e: e.collective_compute(
                "AllGather", ALU.bypass, replica_groups=[list(range(8))],
                ins=[qc_in.opt()], outs=[qc_out.opt()]),
                waits=b.w(s_qc), inc=b.cc, raw_inc=True)
            cc_q = b.cc.n
            b.barrier()
            b.flush(block)

        with ExitStack() as p2:
            psb = lambda name, shape, dt: p2.enter_context(nc.psum_tensor(name, shape, dt))
            sb2 = lambda name, shape, dt: p2.enter_context(nc.sbuf_tensor(name, shape, dt))
            qh = sb2("qh", [128, 2 * S], BF16)
            kh = sb2("kh", [128, 2 * S], BF16)
            vh = sb2("vh", [128, 2 * S], BF16)
            pT = [sb2(f"pT{i}", [128, 1024], BF16) for i in range(3)]
            acc_d = sb2("acc_d", [128, 1024], F32)
            acc_p = sb2("acc_p", [128, 1024], F32)
            acc_f = sb2("acc_f", [128, 512], F32)
            acc_bf = sb2("acc_bf", [128, 512], BF16)
            r_full = [sb2(f"r_full{i}", [1, S], F32) for i in range(2)]
            esA = [sb2(f"esA{i}", [128, 512], F32) for i in range(2)]
            esB = [sb2(f"esB{i}", [128, 512], F32) for i in range(2)]
            esum_b = [sb2(f"esum_b{i}", [128, 1024], BF16) for i in range(2)]
            r_2 = sb2("r_2", [1, S], F32)
            r_hi = sb2("r_hi", [1, S], BF16)
            r_lo = sb2("r_lo", [1, S], BF16)
            rb_sb = [sb2(f"rb_sb{i}", [128, 512], F32) for i in range(2)]
            ostage = [sb2(f"ostage{i}", [128, S], BF16) for i in range(2)]
            masks = sb2("masks", [128, 4 * 512], BF16)
            b.dma("sync", masks[:], masks_in, s_consts)
            pss = [psb(f"pss{i}", [128, 1024], F32) for i in range(2)]
            pso = [psb(f"pso{i}", [128, 512], F32) for i in range(2)]
            psr = psb("psr", [1, 512], F32)
            psbc = psb("psbc", [128, 512], F32)
            s_head = [b.sem(f"s_head{i}") for i in range(2)]
            s_ost = b.sem("s_ost")

            NG = S // 512

            def load_head(h, slot, prefetched):
                if prefetched:
                    wl = b.w(b.pe, b.ve)
                else:
                    wl = b.w(*s_qst)
                b.dma("sync", qh[:, slot * S:(slot + 1) * S],
                      qT_scr[128 * h:128 * (h + 1), :], s_head[slot], waits=wl)
                b.dma("sync", kh[:, slot * S:(slot + 1) * S],
                      kT_scr[128 * h:128 * (h + 1), :], s_head[slot])
                b.dma("sync", vh[:, slot * S:(slot + 1) * S].rearrange("p (j h) -> p j h", h=128),
                      v_scr[0:S, 128 * h:128 * (h + 1)].rearrange("(j p) h -> p j h", p=128),
                      s_head[slot])

            horder = [4, 5, 6, 7, 0, 1, 2, 3] if NHL == 8 else list(range(NHL))
            load_head(horder[0], 0, False)
            if NHL > 1:
                load_head(horder[1], 1, False)
            lastexp = [0, 0]
            pT_free = [[], [], []]
            oevac = [0, 0]
            psr_free = 0
            last_bmm = 0
            rf_read = [0, 0]
            psbc_evac = [0]
            cc_b = 0

            def emit_tail(h, hp):
                nonlocal last_bmm, cc_b
                b.op("scalar", lambda e: e.activation(r_2[0:1, 0:S],
                                                      r_full[hp][0:1, 0:S], AF.Ln),
                     waits=b.w(b.ve), inc=b.ac)
                rf_read[hp] = b.ac.n
                b.op("scalar", lambda e: e.activation(r_2[0:1, 0:S], r_2[0:1, 0:S],
                                                      AF.Exp, scale=-1.0),
                     waits=[(b.ac, b.ac.n)], inc=b.ac)
                b.op("vector", lambda e: e.tensor_copy(r_hi[0:1, 0:S], r_2[0:1, 0:S]),
                     waits=[(b.pe, last_bmm)] + b.w(b.ac), inc=b.ve, drain=False)
                b.op("vector", lambda e: e.tensor_tensor(
                    r_lo[0:1, 0:S], r_2[0:1, 0:S], r_hi[0:1, 0:S], ALU.subtract),
                    waits=[(b.ve, b.ve.n)], inc=b.ve)
                bc_ve = b.ve.n
                for g in range(NG):
                    bp = g % 2
                    wl = [(b.ve, bc_ve if g == 0 else psbc_evac[0])]
                    b.op("tensor", lambda e, r=r_hi[0:1, 512 * g:512 * (g + 1)]:
                         e.matmul(psbc[:], ones_r[0:1, 0:128], r, start=True, stop=False),
                         waits=wl)
                    b.op("tensor", lambda e, r=r_lo[0:1, 512 * g:512 * (g + 1)]:
                         e.matmul(psbc[:], ones_r[0:1, 0:128], r, start=False, stop=True),
                         inc=b.pe)
                    last_bmm = b.pe.n
                    b.op("vector", lambda e, o=rb_sb[bp][:]:
                         e.tensor_copy(o, psbc[:]),
                         waits=[(b.pe, last_bmm)], inc=b.ve, drain=False)
                    psbc_evac[0] = b.ve.n
                    b.op("vector", lambda e, o=ostage[hp][:, 512 * g:512 * (g + 1)],
                         b2=bp: e.tensor_tensor(o, o, rb_sb[b2][:], ALU.mult),
                         waits=[(b.ve, b.ve.n)], inc=b.ve)
                b.dma("scalar", o_loc[128 * h:128 * (h + 1), :], ostage[hp][:], s_ost,
                      waits=b.w(b.ve))
                if h == (7 if NHL == 8 else 3):
                    b.op("gpsimd", lambda e: e.collective_compute(
                        "AllGather", ALU.bypass, replica_groups=[list(range(8))],
                        ins=[o_loc[512:1024, :].opt()], outs=[o_all_b.opt()]),
                        waits=b.w(s_ost), inc=b.cc, raw_inc=True)
                    cc_b = b.cc.n

            summm = [0, 0]
            esum_done = [0, 0]
            for hi in range(NHL):
                h = horder[hi]
                slot = hi % 2
                hp = hi % 2
                ost_wait = s_ost.n
                hl = [(s_head[slot], s_head[slot].n)]
                steps = [(g, jp) for g in range(NG) for jp in range(2 * g + 2)]

                def emit_av(g, jp, pb, gate):
                    nonlocal psr_free
                    gp = g % 2
                    njp = 2 * g + 2
                    for half2 in range(2):
                        j = 2 * jp + half2
                        wl = []
                        if half2 == 0:
                            wl = [(b.ac, gate)]
                            if jp == 0:
                                wl.append((b.ve, oevac[gp]))
                        b.op("tensor", lambda e, o=pso[gp][:],
                             l=vh[:, slot * S + 128 * j:slot * S + 128 * (j + 1)],
                             r=pT[pb][:, 512 * half2:512 * (half2 + 1)],
                             s=(jp == 0 and half2 == 0),
                             t=(jp == njp - 1 and half2 == 1):
                             e.matmul(o, l, r, start=s, stop=t),
                             waits=wl, inc=(b.pe if half2 == 1 else None))
                    pT_free[pb] = [(b.pe, b.pe.n)]
                    if jp == njp - 1:
                        av_pe = b.pe.n
                        b.op("tensor", lambda e, m=esum_b[gp][:, 0:512]:
                             e.matmul(psr[:], ones_c[:], m, start=True, stop=False),
                             waits=[(b.ve, esum_done[gp]), (b.ve, psr_free)])
                        b.op("tensor", lambda e, m=esum_b[gp][:, 512:1024]:
                             e.matmul(psr[:], ones_c[:], m, start=False, stop=True),
                             inc=b.pe)
                        summm[gp] = b.pe.n
                        b.op("vector", lambda e, o=ostage[hp][:, 512 * g:512 * (g + 1)],
                             g2=gp: e.tensor_copy(o, pso[g2][:]),
                             waits=[(b.pe, av_pe)] +
                                   ([(s_ost, ost_wait)] if g == 0 else []),
                             inc=b.ve, drain=False)
                        oevac[gp] = b.ve.n
                        b.op("vector", lambda e, o=r_full[hp][0:1, 512 * g:512 * (g + 1)],
                             g2=gp: e.tensor_copy(o, psr[:]),
                             waits=[(b.pe, summm[gp]), (b.ac, rf_read[hp])],
                             inc=b.ve, drain=(g == NG - 1))
                        psr_free = b.ve.n

                pend = None
                esum_read = [0, 0, 0]
                for i, (g, jp) in enumerate(steps):
                    pp = i % 2
                    pb = i % 3
                    gp = g % 2
                    wl = [(b.ac, lastexp[pp])] + (hl if i == 0 else [])
                    sc_ops = []
                    for half2 in range(2):
                        j = 2 * jp + half2
                        diag = j >= 4 * g
                        sc_ops.append((
                            lambda e, o=pss[pp][:, 512 * half2:512 * (half2 + 1)],
                            l=kh[:, slot * S + 128 * j:slot * S + 128 * (j + 1)],
                            r=qh[:, slot * S + 512 * g:slot * S + 512 * (g + 1)],
                            t=(not diag):
                            e.matmul(o, l, r, start=True, stop=t),
                            wl if half2 == 0 else []))
                        if diag:
                            mi = j - 4 * g
                            sc_ops.append((
                                lambda e, o=pss[pp][:, 512 * half2:512 * (half2 + 1)],
                                m=masks[:, 512 * mi:512 * (mi + 1)]:
                                e.matmul(o, identb[:], m, start=False, stop=True),
                                b.w(s_consts) if (h == 0 and g == 0) else []))
                    for si, (fn, swl) in enumerate(sc_ops):
                        b.op("tensor", fn, waits=swl,
                             inc=(b.pe if si == len(sc_ops) - 1 else None))
                    sc_pe = b.pe.n
                    b.op("scalar", lambda e, o=pT[pb][:], a=pss[pp][:]:
                         e.activation(o, a, AF.Exp, scale=SM_SCALE),
                         waits=[(b.pe, sc_pe), (b.ve, esum_read[pb])] +
                               pT_free[pb],
                         inc=b.ac, drain=False)
                    lastexp[pp] = b.ac.n
                    gate = b.ac.n
                    if pend is not None:
                        emit_av(*pend)
                    last = (jp == 2 * g + 1)
                    if jp == 0:
                        b.op("vector", lambda e, o=esA[gp][:], a=pT[pb][:, 0:512]:
                             e.tensor_copy(o, a),
                             waits=[(b.ac, gate), (b.pe, summm[gp])],
                             inc=b.ve, drain=False)
                        b.op("vector", lambda e, o=esB[gp][:], a=pT[pb][:, 512:1024]:
                             e.tensor_copy(o, a), inc=b.ve, drain=False)
                    else:
                        oA = esum_b[gp][:, 0:512] if last else esA[gp][:]
                        oB = esum_b[gp][:, 512:1024] if last else esB[gp][:]
                        b.op("vector", lambda e, o=oA, s1=esA[gp][:],
                             a=pT[pb][:, 0:512]:
                             e.tensor_tensor(o, s1, a, ALU.add),
                             waits=[(b.ac, gate)], inc=b.ve, drain=False)
                        b.op("vector", lambda e, o=oB, s1=esB[gp][:],
                             a=pT[pb][:, 512:1024]:
                             e.tensor_tensor(o, s1, a, ALU.add),
                             inc=b.ve, drain=last)
                        if last:
                            esum_done[gp] = b.ve.n
                    esum_read[pb] = b.ve.n
                    pend = (g, jp, pb, gate)
                    if hi > 0 and i == min(40, len(steps) - 2):
                        emit_tail(horder[hi - 1], (hi - 1) % 2)
                emit_av(*pend)
                if hi + 2 < NHL:
                    load_head(horder[hi + 2], hi % 2, True)
            emit_tail(horder[NHL - 1], (NHL - 1) % 2)
            b.op("gpsimd", lambda e: e.collective_compute(
                "AllGather", ALU.bypass, replica_groups=[list(range(8))],
                ins=[o_loc[0:512, :].opt()], outs=[o_all_a.opt()]),
                waits=b.w(s_ost), inc=b.cc, raw_inc=True)
            cc_a = b.cc.n
            b.barrier()
            b.flush(block)

        with ExitStack() as p3:
            psb = lambda name, shape, dt: p3.enter_context(nc.psum_tensor(name, shape, dt))
            sb3 = lambda name, shape, dt: p3.enter_context(nc.sbuf_tensor(name, shape, dt))
            og = sb3("og", [128, 16 * OH], BF16)
            woT = sb3("woT", [128, 16 * 2048], BF16)
            x1t = [sb3(f"x1t{i}", [128, D], F32) for i in range(2)]
            x2st = [sb3(f"x2st{i}", [128, D], BF16) for i in range(2)]
            xt = [sb3(f"p3xt{i}", [128, D], F32) for i in range(2)]
            xlnt = [sb3(f"p3xlnt{i}", [128, D], BF16) for i in range(2)]
            sqscr = sb3("p3sqscr", [128, D], F32)
            bo_sb = sb3("bo_sb", [1, D], BF16)
            b.dma("gpsimd", bo_sb[:], bo_in, s_cast)
            psop = [psb(f"psop{i}", [128, 512], F32) for i in range(2)]
            pst3 = [psb(f"p3_pst{i}", [128, 128], BF16) for i in range(2)]
            s_og = b.sem("s_og")
            s_woT = b.sem("s_woT")
            s_x1 = [b.sem(f"s_x1{i}") for i in range(2)]
            s_x2T = [b.sem(f"s_x2T{i}") for i in range(2)]

            o_view_a = o_all_a.rearrange("r (q t) -> (r q) t", t=OH)
            o_view_b = o_all_b.rearrange("r (q t) -> (r q) t", t=OH)
            forder = [4, 5, 6, 7, 12, 13, 14, 15, 0, 1, 2, 3, 8, 9, 10, 11] \
                if NHL == 8 else [0, 1, 2, 3, 8, 9, 10, 11, 4, 5, 6, 7, 12, 13, 14, 15]
            for ki, k in enumerate(forder):
                b.dma("sync", woT[:, 2048 * k:2048 * (k + 1)],
                      wo_bf[0:2048, 128 * k:128 * (k + 1)], s_woT,
                      waits=b.w(s_cast) if ki == 0 else (), transpose=True)

            psop_read = [0, 0]
            pst_read = [0, 0]
            lnsq_read = [0, 0]
            lnsq_ac = [0, 0]
            ln_ac3 = [0, 0]
            xt_ap3 = [0, 0]
            xlnt_read3 = [0, 0]
            for hf in range(2):
                gw = (b.w(b.pe) if hf == 1 else [])
                for fi, f in enumerate(forder):
                    col = f + 16 * hf
                    vw = o_view_a if (f % 8) < 4 else o_view_b
                    wl0 = []
                    if fi == 0:
                        wl0 = gw + b.w(s_consts) + [(b.cc, cc_b if NHL == 8 else cc_a)]
                    elif fi == 8:
                        wl0 = [(b.cc, cc_a if NHL == 8 else cc_b)]
                    b.op("gpsimd", lambda e, o=og[:, OH * f:OH * (f + 1)],
                         ix=oidx[:, col:col + 1], v2=vw:
                         e.indirect_dma_start(o, None, v2,
                                              bass.IndirectOffsetOnAxis(ix, 0)),
                         waits=wl0, inc=s_og, n=16)
                    if fi == 7:
                        og_a = s_og.n
                ogl = [(s_og, og_a)]
                og_b = s_og.n
                for sub in range(OH // 128):
                    par = sub % 2
                    c = par
                    r0 = hf * OH + sub * 128
                    b.dma("scalar", xt[par][:], x_res[r0:r0 + 128, :], s_xt[par],
                          waits=[(b.ve, xt_ap3[par])] if (hf + sub) else [])
                    for dg in range(4):
                        pp = dg % 2
                        for fi2 in range(16):
                            fc = forder[fi2]
                            wl = (ogl + [(s_woT, 16),
                                         (b.ve, psop_read[pp])]) if fi2 == 0 else \
                                ([(s_og, og_b)] if (fi2 == 8 and sub == 0 and dg == 0
                                                    ) else [])
                            if hf == 0 and sub == 0 and dg == 0 and fi2 > 0:
                                wl = wl + [(s_woT, 16 * (fi2 + 1))]
                            b.op("tensor", lambda e, o=psop[pp][:],
                                 l=og[:, OH * fc + 128 * sub:OH * fc + 128 * (sub + 1)],
                                 r=woT[:, 2048 * fc + 512 * dg:2048 * fc + 512 * (dg + 1)],
                                 s=(fi2 == 0):
                                 e.matmul(o, l, r, start=s, stop=False), waits=wl)
                        b.op("tensor", lambda e, o=psop[pp][:],
                             r=bo_sb[0:1, 512 * dg:512 * (dg + 1)]:
                             e.matmul(o, ones_r[0:1, 0:128], r, start=False, stop=True),
                             inc=b.pe)
                        wl = [(b.pe, b.pe.n), (s_xt[par], s_xt[par].n)]
                        if dg == 0:
                            wl += [(s_x1[par], s_x1[par].n), (b.ac, lnsq_ac[par])]
                        b.op("vector", lambda e, o=x1t[par][:, 512 * dg:512 * (dg + 1)],
                             a=psop[pp][:], x2=xt[par][:, 512 * dg:512 * (dg + 1)]:
                             e.tensor_tensor(o, a, x2, ALU.add), waits=wl, inc=b.ve)
                        psop_read[pp] = b.ve.n
                    x1_ve = b.ve.n
                    xt_ap3[par] = x1_ve
                    b.dma("scalar", x1_scr[r0:r0 + 128, :], x1t[par][:], s_x1[par],
                          waits=[(b.ve, x1_ve)])
                    b.op("vector", lambda e, o=s1v[:, c:c + 1], a=x1t[par][:]:
                         e.tensor_reduce(o, a, AX.X, ALU.add),
                         waits=[(b.ve, x1_ve)], inc=b.ve)
                    b.op("vector", lambda e, o=mv[:, c:c + 1], a=s1v[:, c:c + 1]:
                         e.tensor_scalar(o, a, -1.0 / D, None, ALU.mult),
                         waits=[(b.ve, b.ve.n), (b.ac, ln_ac3[c])], inc=b.ve)
                    mv_ve3 = b.ve.n
                    b.op("scalar", lambda e, a=x1t[par][:], m2=mv[:, c:c + 1]:
                         e.activation(sqscr[:], a, AF.Square, bias=m2),
                         waits=[(b.ve, mv_ve3), (b.ve, max(lnsq_read))], inc=b.ac)
                    lnsq_ac[par] = b.ac.n
                    b.op("vector", lambda e, o=sqv[:, c:c + 1]:
                         e.tensor_reduce(o, sqscr[:], AX.X, ALU.add),
                         waits=[(b.ac, b.ac.n)], inc=b.ve)
                    lnsq_read[0] = lnsq_read[1] = b.ve.n
                    b.op("scalar", lambda e, o=rstdv[:, c:c + 1], a=sqv[:, c:c + 1]:
                         e.activation(o, a, AF.Ln, scale=1.0 / D, bias=cst[:, 0:1]),
                         waits=[(b.ve, max(lnsq_read))], inc=b.ac)
                    b.op("scalar", lambda e, o=rstdv[:, c:c + 1], a=rstdv[:, c:c + 1]:
                         e.activation(o, a, AF.Exp, scale=-0.5),
                         waits=[(b.ac, b.ac.n)], inc=b.ac)
                    ln_ac3[c] = b.ac.n
                    b.op("vector", lambda e, o=xlnt[par][:], a=x1t[par][:],
                         m2=mv[:, c:c + 1], r2=rstdv[:, c:c + 1]:
                         e.tensor_scalar(o, a, m2, r2, ALU.add, ALU.mult),
                         waits=[(b.ac, ln_ac3[c]), (b.pe, xlnt_read3[par]),
                                (s_x2T[par], s_x2T[par].n)], inc=b.ve)
                    ap_ve = b.ve.n
                    for k in range(16):
                        tp = k % 2
                        b.op("tensor", lambda e, o=pst3[tp][:],
                             a=xlnt[par][:, 128 * k:128 * (k + 1)]:
                             e.transpose(o, a, identb[:]),
                             waits=[(b.ve, ap_ve), (b.ve, pst_read[tp])],
                             inc=b.pe)
                        b.op("vector", lambda e, o=x2st[par][:, 128 * k:128 * (k + 1)],
                             a=pst3[tp][:]: e.tensor_copy(o, a),
                             waits=b.w(b.pe), inc=b.ve, drain=(k == 15))
                        if k == 15:
                            xlnt_read3[par] = b.pe.n
                        pst_read[tp] = b.ve.n
                    b.dma("scalar",
                          x2T_scr[:, r0:r0 + 128].rearrange("(k p) q -> p k q", p=128),
                          x2st[par][:].rearrange("p (k q) -> p k q", q=128),
                          s_x2T[par], waits=b.w(b.ve))
            b.barrier()
            b.flush(block)

        with ExitStack() as p4:
            psb = lambda name, shape, dt: p4.enter_context(nc.psum_tensor(name, shape, dt))
            sb4 = lambda name, shape, dt: p4.enter_context(nc.sbuf_tensor(name, shape, dt))
            hT = sb4("hT", [128, 64 * 512], BF16)
            xl2 = sb4("xl2", [128, 16 * 512], BF16)
            wgu = [sb4(f"wgu{i}", [128, 16 * 1024], BF16) for i in range(3)]
            x1p = [sb4(f"x1p{i}", [128, 512], F32) for i in range(2)]
            sg = sb4("sg", [128, 4 * 512], BF16)
            hsq = [sb4(f"hsq{i}", [128, 512], BF16) for i in range(2)]
            invr = sb4("invr", [1, 512], F32)
            inv_hi = sb4("inv_hi", [1, 512], BF16)
            inv_lo = sb4("inv_lo", [1, 512], BF16)
            invc = sb4("invc", [128, 4], F32)
            x2p = [sb4(f"x2p{i}", [128, 512], F32) for i in range(2)]
            pw = [psb(f"pw{i}", [128, 512], F32) for i in range(8)]
            s_xl2 = b.sem("s_xl2")
            s_x1p = [b.sem(f"s_x1p{i}") for i in range(2)]
            s_wgu = [b.sem(f"s_wgu{i}") for i in range(3)]
            s_out = [b.sem(f"s_out{i}") for i in range(2)]

            NGRP = T // 512
            silu_cnt = [0, 0, 0, 0]
            bank_read = [0, 0, 0, 0]
            sg_read = [0, 0, 0, 0]
            hsq_pe = [0, 0]
            wbuf_pe = [0, 0, 0]
            evac_cnt = [0] * 8
            x1p_read = [0, 0]
            ss_edge = (None, 0)
            inv55_read = 0

            def gu_window(habs):
                rb = habs // 2048
                off = habs % 2048
                return qc_out[3072 * rb:3072 * rb + 2048, off:off + 1024].rearrange(
                    "(k p) h -> p k h", p=128)

            def load_gu(k, first=False):
                buf = (k + 2) % 3
                jH = k // 2
                habs = (0 if k % 2 == 0 else FF) + 1024 * jH
                b.dma("sync", wgu[buf][:].rearrange("p (k h) -> p k h", h=1024),
                      gu_window(habs), s_wgu[buf],
                      waits=[(b.pe, wbuf_pe[buf])] +
                            ([(b.cc, cc_q)] if first else []))

            def load_xl2(grp, wl):
                t0 = grp * 512
                b.dma("gpsimd", xl2[:].rearrange("p (k t) -> p k t", t=512),
                      x2T_scr[:, t0:t0 + 512].rearrange("(k p) t -> p k t", p=128),
                      s_xl2, waits=wl)
                xl2_cnt[0] = s_xl2.n

            xl2_cnt = [0]
            load_xl2(0, b.w(s_x2T[0], s_x2T[1]))
            for grp in range(NGRP):
                t0 = grp * 512
                xl2w = [(s_xl2, xl2_cnt[0])]
                for k in ((0, 1, 2) if grp == 0 else (1, 2)):
                    load_gu(k, first=(grp == 0 and k == 0))

                for jH in range(8):
                    bg, bu = (2 * jH + 2) % 3, (2 * jH + 3) % 3
                    gl = [(s_wgu[bg], s_wgu[bg].n)]
                    ul = [(s_wgu[bu], s_wgu[bu].n)]
                    for rnd in range(2):
                        for hx4 in range(4):
                            hx = rnd * 4 + hx4
                            for dc in range(16):
                                wl2 = (gl + [(b.ve, bank_read[hx4]),
                                             (b.ve, evac_cnt[hx4])] + xl2w) \
                                    if dc == 0 else []
                                b.op("tensor", lambda e, o=pw[hx4][:], bg2=bg,
                                     l=wgu[bg][:, 1024 * dc + 128 * hx:1024 * dc + 128 * (hx + 1)],
                                     r=xl2[:, 512 * dc:512 * (dc + 1)], s=(dc == 0),
                                     t=(dc == 15):
                                     e.matmul(o, l, r, start=s, stop=t),
                                     waits=wl2, inc=(b.pe if dc == 15 else None))
                            gate_pe = b.pe.n
                            b.op("scalar", lambda e, o=sg[:, 512 * hx4:512 * (hx4 + 1)],
                                 a=pw[hx4][:]:
                                 e.activation(o, a, AF.Silu),
                                 waits=[(b.pe, gate_pe), (b.ve, sg_read[hx4])],
                                 inc=b.ac, drain=False)
                            silu_cnt[hx4] = b.ac.n
                        if rnd == 1:
                            wbuf_pe[bg] = b.pe.n
                        for hx4 in range(4):
                            hx = rnd * 4 + hx4
                            jc = 8 * jH + hx
                            for dc in range(16):
                                wl2 = (ul + [(b.ac, silu_cnt[hx4])]) if dc == 0 else []
                                b.op("tensor", lambda e, o=pw[hx4][:],
                                     l=wgu[bu][:, 1024 * dc + 128 * hx:1024 * dc + 128 * (hx + 1)],
                                     r=xl2[:, 512 * dc:512 * (dc + 1)], s=(dc == 0),
                                     t=(dc == 15):
                                     e.matmul(o, l, r, start=s, stop=t),
                                     waits=wl2, inc=(b.pe if dc == 15 else None))
                            up_pe = b.pe.n
                            b.op("vector", lambda e, o=hT[:, 512 * jc:512 * (jc + 1)],
                                 a=sg[:, 512 * hx4:512 * (hx4 + 1)], p=pw[hx4][:]:
                                 e.tensor_tensor(o, a, p, ALU.mult),
                                 waits=[(b.pe, up_pe)], inc=b.ve, drain=False)
                            bank_read[hx4] = b.ve.n
                            sg_read[hx4] = b.ve.n
                            hp = jc % 2
                            b.op("scalar", lambda e, o=hsq[hp][:],
                                 a=hT[:, 512 * jc:512 * (jc + 1)]:
                                 e.activation(o, a, AF.Square),
                                 waits=[(b.ve, b.ve.n), (b.pe, hsq_pe[hp])],
                                 inc=b.ac, drain=False)
                            wl3 = b.w(b.ac)
                            if jc == 0:
                                wl3 = wl3 + [(b.ve, evac_cnt[4])]
                                if ss_edge[0] is not None:
                                    wl3 = wl3 + [ss_edge]
                            b.op("tensor", lambda e, o=pw[4][0:1, :], a=hsq[hp][:],
                                 s=(jc == 0), t=(jc == 63):
                                 e.matmul(o, ones_c[:], a, start=s, stop=t),
                                 waits=wl3, inc=b.pe)
                            hsq_pe[hp] = b.pe.n
                        if rnd == 1:
                            wbuf_pe[bu] = b.pe.n
                    for k in (2 * jH + 3, 2 * jH + 4):
                        if k < 16:
                            load_gu(k)
                hT_done = b.ve.n
                gu_pe_last = b.pe.n
                if grp + 1 < NGRP:
                    load_xl2(grp + 1, [(b.pe, gu_pe_last)])

                b.op("scalar", lambda e: e.activation(invr[:], pw[4][0:1, :], AF.Ln,
                                                      scale=1.0 / FF, bias=cst[0:1, 1:2]),
                     waits=[(b.pe, gu_pe_last)], inc=b.ac)
                ss_edge = (b.ac, b.ac.n)
                b.op("scalar", lambda e: e.activation(invr[:], invr[:], AF.Exp,
                                                      scale=-0.5),
                     waits=[(b.ac, b.ac.n)], inc=b.ac)
                b.op("vector", lambda e: e.tensor_copy(inv_hi[:], invr[:]),
                     waits=b.w(b.ac) + [(b.pe, inv55_read)], inc=b.ve)
                b.op("vector", lambda e: e.tensor_tensor(inv_lo[:], invr[:], inv_hi[:],
                                                         ALU.subtract),
                     waits=[(b.ve, b.ve.n)], inc=b.ve)
                for sub in range(4):
                    b.op("tensor", lambda e, o=pw[5][:, sub:sub + 1],
                         l=inv_hi[0:1, 128 * sub:128 * (sub + 1)]:
                         e.matmul(o, l, ones_r[0:1, 0:1], start=True, stop=False),
                         waits=(b.w(b.ve) + [(b.ve, evac_cnt[5])]) if sub == 0 else ())
                    b.op("tensor", lambda e, o=pw[5][:, sub:sub + 1],
                         l=inv_lo[0:1, 128 * sub:128 * (sub + 1)]:
                         e.matmul(o, l, ones_r[0:1, 0:1], start=False, stop=True),
                         inc=b.pe)
                inv55_read = b.pe.n
                b.op("vector", lambda e: e.tensor_copy(invc[:], pw[5][:, 0:4]),
                     waits=b.w(b.pe), inc=b.ve)
                inv_ve = b.ve.n
                evac_cnt[5] = max(evac_cnt[5], inv_ve)

                wqp = [wgu[0][:, 0:8 * 1024], wgu[1][:, 0:8 * 1024]]
                for wave in range(2):
                    for rb in range(8):
                        wp_par = rb % 2
                        b.dma("sync", wqp[wp_par].rearrange("p (k d) -> p k d", d=1024),
                              qc_out[3072 * rb + 2048:3072 * rb + 3072,
                                     1024 * wave:1024 * (wave + 1)].rearrange(
                                         "(k p) d -> p k d", p=128),
                              s_wgu[wp_par], waits=[(b.pe, wbuf_pe[wp_par])])
                        for pair in range(8):
                            sub, dgw = pair // 2, pair % 2
                            for hx in range(8):
                                j = 8 * rb + hx
                                wl2 = []
                                if hx == 0:
                                    wl2 = [(s_wgu[wp_par], s_wgu[wp_par].n)]
                                    if rb == 0:
                                        wl2 += [(b.ve, hT_done), (b.ve, inv_ve),
                                                (b.ve, evac_cnt[pair])]
                                b.op("tensor", lambda e, o=pw[pair][:], w2=wp_par,
                                     l=hT[:, 512 * j + 128 * sub:512 * j + 128 * (sub + 1)],
                                     r=wqp[wp_par][:, 1024 * hx + 512 * dgw:1024 * hx + 512 * (dgw + 1)],
                                     s=(rb == 0 and hx == 0), t=(rb == 7 and hx == 7):
                                     e.matmul(o, l, r, start=s, stop=t),
                                     waits=wl2,
                                     inc=(b.pe if hx == 7 else None))
                        wbuf_pe[wp_par] = b.pe.n
                    if wave == 0 and grp + 1 < NGRP:
                        load_gu(0)
                    proj_pe = b.pe.n
                    for pair in range(8):
                        sub, dgw = pair // 2, pair % 2
                        dg = wave * 2 + dgw
                        xp = pair % 2
                        b.dma("sync", x1p[xp][:],
                              x1_scr[t0 + 128 * sub:t0 + 128 * (sub + 1),
                                     512 * dg:512 * (dg + 1)], s_x1p[xp],
                              waits=[(b.ve, x1p_read[xp])] +
                              (b.w(s_x1[0], s_x1[1]) if (grp == 0 and wave == 0
                                                         and pair < 2) else []))
                        wl = [(b.pe, proj_pe), (s_out[xp], s_out[xp].n),
                              (s_x1p[xp], s_x1p[xp].n)]
                        b.op("vector", lambda e, o=x2p[xp][:], p=pw[pair][:],
                             iv=invc[:, sub:sub + 1], x1=x1p[xp][:]:
                             e.scalar_tensor_tensor(o, p, iv, x1, ALU.mult, ALU.add),
                             waits=wl, inc=b.ve)
                        evac_cnt[pair] = b.ve.n
                        x1p_read[xp] = b.ve.n
                        b.dma("scalar",
                              out[t0 + 128 * sub:t0 + 128 * (sub + 1), 512 * dg:512 * (dg + 1)],
                              x2p[xp][:], s_out[xp], waits=b.w(b.ve))

            if debug:
                s_tap = b.sem("s_tap")
                tw = b.w(s_out[0], s_out[1], s_qst[0], s_qst[1], s_ost,
                         s_x1[0], s_x1[1], s_x2T[0], s_x2T[1])
                for name, (dst, src) in taps.items():
                    b.dma("gpsimd", dst, src, s_tap, waits=tw)
                b.op("gpsimd", lambda e: e.nop(), waits=b.w(s_tap))
            b.op("gpsimd", lambda e: e.nop(), waits=b.w(s_out[0], s_out[1]))
            b.op("sync", lambda e: e.nop(), waits=b.w(s_out[0], s_out[1]))
            b.flush(block)

    return nc



def _host_consts():
    identb = np.eye(128, dtype=BFD)
    tk = np.arange(128)[:, None]
    tq = np.arange(512)[None, :]
    masks = np.concatenate(
        [np.where((tk + 128 * mi) <= tq, 0.0, -30000.0).astype(BFD)
         for mi in range(4)], axis=1)
    return identb, masks


def _oidx_for_core(c, S):
    T = S // 2
    OH = T // 2
    b_, g = c // 2, c % 2
    qpr = S // OH
    idx = np.zeros((128, 32), dtype=np.int32)
    p = np.arange(128)
    for hf in range(2):
        for f in range(16):
            rank = 2 * b_ + f // 8
            hl = f % 8
            R = rank * 512 + (hl % 4) * 128 + p
            q = (T * g + OH * hf) // OH
            idx[:, f + 16 * hf] = R * qpr + q
    return idx


def kernel(L0_attn_h, L0_attn_Wq, L0_attn_bq, L0_attn_Wk, L0_attn_bk,
           L0_attn_Wv, L0_attn_bv, L0_attn_Wo, L0_attn_bo,
           L0_ffn_w_gate_up, L0_ffn_w_proj, _debug=False):
    h = np.asarray(L0_attn_h, dtype=np.float32)
    Wq = np.asarray(L0_attn_Wq, dtype=np.float32)
    bq = np.asarray(L0_attn_bq, dtype=np.float32)
    Wk = np.asarray(L0_attn_Wk, dtype=np.float32)
    bk = np.asarray(L0_attn_bk, dtype=np.float32)
    Wv = np.asarray(L0_attn_Wv, dtype=np.float32)
    bv = np.asarray(L0_attn_bv, dtype=np.float32)
    Wo = np.asarray(L0_attn_Wo, dtype=np.float32)
    bo = np.asarray(L0_attn_bo, dtype=np.float32)
    Wgu = np.asarray(L0_ffn_w_gate_up, dtype=np.float32)
    Wp = np.asarray(L0_ffn_w_proj, dtype=np.float32)

    B, S, _ = h.shape
    T = S // 2
    nc = build_nc(S=S, debug=_debug)
    identb, masks = _host_consts()

    in_maps = []
    for c in range(8):
        b_, g = c // 2, c % 2
        in_maps.append({
            "x_full": np.ascontiguousarray(h[b_]),
            "x_res": np.ascontiguousarray(h[b_, g * T:(g + 1) * T]),
            "wq_in": np.ascontiguousarray(Wq[1024 * g:1024 * (g + 1)]),
            "wk_in": np.ascontiguousarray(Wk[1024 * g:1024 * (g + 1)]),
            "wv_in": np.ascontiguousarray(Wv[1024 * g:1024 * (g + 1)]),
            "bq_in": np.ascontiguousarray(bq[1024 * g:1024 * (g + 1)].reshape(1, -1)),
            "bk_in": np.ascontiguousarray(bk[1024 * g:1024 * (g + 1)].reshape(1, -1)),
            "bv_in": np.ascontiguousarray(bv[1024 * g:1024 * (g + 1)].reshape(1, -1)),
            "wo_in": np.ascontiguousarray(Wo),
            "bo_in": np.ascontiguousarray(bo.reshape(1, -1)),
            "wgu_in": np.ascontiguousarray(Wgu[2048 * c:2048 * (c + 1)]),
            "wp_in": np.ascontiguousarray(Wp[:, 1024 * c:1024 * (c + 1)]),
            "identb_in": identb,
            "masks_in": masks,
            "oidx_in": _oidx_for_core(c, S),
        })

    res = run_bass_kernel_spmd(nc, in_maps, core_ids=list(range(8)))
    outp = np.empty((B, S, D), dtype=np.float32)
    for c in range(8):
        b_, g = c // 2, c % 2
        outp[b_, g * T:(g + 1) * T] = res.results[c]["out"]
    if _debug:
        return outp, res
    return outp


# version 15
# speedup vs baseline: 1.0904x; 1.0877x over previous
import numpy as np
import ml_dtypes
from contextlib import ExitStack

import concourse.bass as bass
import concourse.mybir as mybir
from concourse.bass_utils import run_bass_kernel_spmd

F32 = mybir.dt.float32
BF16 = mybir.dt.bfloat16
I32 = mybir.dt.int32
AF = mybir.ActivationFunctionType
ALU = mybir.AluOpType
AX = mybir.AxisListType
BFD = ml_dtypes.bfloat16

D = 2048
HD = 128
NHL = 8
FF = 8192
GQ = 64
SM_SCALE = float(np.float32(1.0) / np.sqrt(np.float32(HD)))
LN_EPS = 1e-5
RMS_EPS = float(np.finfo(np.float32).eps)
QEPS = float(np.float32(BFD(1e-8)))
TH_C = float(np.float32(513.0 / 1024.0))

ENGS = ("tensor", "vector", "scalar", "gpsimd", "sync")


class _Ev:
    __slots__ = ("fn", "waits", "incs")

    def __init__(self, fn, waits, incs):
        self.fn, self.waits, self.incs = fn, waits, incs


class _S:
    __slots__ = ("h", "n")


class Builder:
    def __init__(self, nc, ctx, S):
        self.nc = nc
        self.ctx = ctx
        self.S = S
        self.ev = {e: [] for e in ENGS}
        self.sems = {}
        self.pe = self.sem("pe")
        self.ve = self.sem("ve")
        self.ac = self.sem("ac")
        self.po = self.sem("po")
        self.bar = self.sem("bar")
        self.cc = self.sem("cc")

    def sem(self, name):
        if name not in self.sems:
            s = _S()
            s.h = self.ctx.enter_context(self.nc.semaphore(name))
            s.n = 0
            self.sems[name] = s
        return self.sems[name]

    DRAIN_ENGS = ("vector", "scalar", "gpsimd")

    def op(self, eng, fn, waits=(), inc=None, n=1, raw_inc=False, drain=None):
        if drain is False:
            raw_inc = True
        if (inc is not None and n == 1 and not raw_inc
                and eng in self.DRAIN_ENGS):
            self.ev[eng].append(_Ev(fn, list(waits), []))
            inc.n += 1
            self.ev[eng].append(_Ev(lambda e: e.drain(), [], [(inc, 1)]))
            return inc.n
        incs = []
        if inc is not None:
            inc.n += n
            incs.append((inc, n))
        self.ev[eng].append(_Ev(fn, list(waits), incs))
        return inc.n if inc is not None else 0

    def dma(self, q, out, in_, sem, waits=(), **kw):
        self.op(q, lambda e, o=out, i=in_, kw=kw: e.dma_start(out=o, in_=i, **kw),
                waits=waits, inc=sem, n=16)
        return sem.n

    def w(self, *sems):
        return [(s, s.n) for s in sems if s.n > 0]

    def barrier(self):
        t = self.bar.n + len(ENGS)
        for e in ENGS:
            if e in self.DRAIN_ENGS:
                self.op(e, lambda eng: eng.drain(), inc=self.bar, raw_inc=True)
            else:
                self.op(e, lambda eng: eng.nop(), inc=self.bar, raw_inc=True)
        for e in ENGS:
            self.ev[e].append(_Ev(None, [(self.bar, t)], []))

    def flush(self, block):
        for ename in ENGS:
            evs = self.ev[ename]
            if not evs:
                continue
            self.ev[ename] = []

            def run(eng, evs=evs):
                for ev in evs:
                    for (s, v) in ev.waits:
                        if v > 0:
                            eng.wait_ge(s.h, v)
                    if ev.fn is not None:
                        inst = ev.fn(eng)
                        for (s, k) in ev.incs:
                            inst.then_inc(s.h, k)
            getattr(block, ename)(run)


def build_nc(S=4096, debug=False):
    nc = bass.Bass("TRN2", target_bir_lowering=False, debug=False)
    T = S // 2
    OH = T // 2
    ctx = ExitStack()

    ap = lambda t: t.ap()
    x_full = ap(nc.dram_tensor("x_full", [S, D], F32, kind="ExternalInput"))
    x_res = ap(nc.dram_tensor("x_res", [T, D], F32, kind="ExternalInput"))
    wq_in = ap(nc.dram_tensor("wq_in", [NHL * HD, D], F32, kind="ExternalInput"))
    wk_in = ap(nc.dram_tensor("wk_in", [NHL * HD, D], F32, kind="ExternalInput"))
    wv_in = ap(nc.dram_tensor("wv_in", [NHL * HD, D], F32, kind="ExternalInput"))
    bq_in = ap(nc.dram_tensor("bq_in", [1, NHL * HD], F32, kind="ExternalInput"))
    bk_in = ap(nc.dram_tensor("bk_in", [1, NHL * HD], F32, kind="ExternalInput"))
    bv_in = ap(nc.dram_tensor("bv_in", [1, NHL * HD], F32, kind="ExternalInput"))
    wo_in = ap(nc.dram_tensor("wo_in", [D, D], F32, kind="ExternalInput"))
    bo_in = ap(nc.dram_tensor("bo_in", [1, D], F32, kind="ExternalInput"))
    wgu_in = ap(nc.dram_tensor("wgu_in", [2048, 2048], F32, kind="ExternalInput"))
    wp_in = ap(nc.dram_tensor("wp_in", [2048, 1024], F32, kind="ExternalInput"))
    identb_in = ap(nc.dram_tensor("identb_in", [128, 128], BF16, kind="ExternalInput"))
    masks_in = ap(nc.dram_tensor("masks_in", [128, 4 * 512], BF16, kind="ExternalInput"))
    oidx_in = ap(nc.dram_tensor("oidx_in", [128, 32], I32, kind="ExternalInput"))

    out = ap(nc.dram_tensor("out", [T, D], F32, kind="ExternalOutput"))

    wq_bf = ap(nc.dram_tensor("wq_bf", [NHL * HD, D], BF16, kind="Internal"))
    wk_bf = ap(nc.dram_tensor("wk_bf", [NHL * HD, D], BF16, kind="Internal"))
    wv_bf = ap(nc.dram_tensor("wv_bf", [NHL * HD, D], BF16, kind="Internal"))
    wo_bf = ap(nc.dram_tensor("wo_bf", [D, D], BF16, kind="Internal"))
    qc_in = ap(nc.dram_tensor("qc_in", [3072, 2048], BF16, kind="Internal"))
    qc_out = ap(nc.dram_tensor("qc_out", [8 * 3072, 2048], BF16, kind="Internal",
                               addr_space="Shared"))
    qT_scr = ap(nc.dram_tensor("qT_scr", [NHL * HD, S], BF16, kind="Internal"))
    kT_scr = ap(nc.dram_tensor("kT_scr", [NHL * HD, S], BF16, kind="Internal"))
    v_scr = ap(nc.dram_tensor("v_scr", [S, NHL * HD], BF16, kind="Internal"))
    o_loc = ap(nc.dram_tensor("o_loc", [NHL * HD, S], BF16, kind="Internal"))
    o_all_a = ap(nc.dram_tensor("o_all_a", [8 * 512, S], BF16, kind="Internal",
                                addr_space="Shared"))
    o_all_b = ap(nc.dram_tensor("o_all_b", [8 * 512, S], BF16, kind="Internal",
                                addr_space="Shared"))
    x1_scr = ap(nc.dram_tensor("x1_scr", [T, D], F32, kind="Internal"))
    x2T_scr = ap(nc.dram_tensor("x2T_scr", [D, T], BF16, kind="Internal"))

    taps = {}
    if debug:
        taps["tap_qT"] = (ap(nc.dram_tensor("tap_qT", [NHL * HD, S], BF16, kind="ExternalOutput")), qT_scr)
        taps["tap_kT"] = (ap(nc.dram_tensor("tap_kT", [NHL * HD, S], BF16, kind="ExternalOutput")), kT_scr)
        taps["tap_v"] = (ap(nc.dram_tensor("tap_v", [S, NHL * HD], BF16, kind="ExternalOutput")), v_scr)
        taps["tap_o"] = (ap(nc.dram_tensor("tap_o", [NHL * HD, S], BF16, kind="ExternalOutput")), o_loc)
        taps["tap_x1"] = (ap(nc.dram_tensor("tap_x1", [T, D], F32, kind="ExternalOutput")), x1_scr)
        taps["tap_qc"] = (ap(nc.dram_tensor("tap_qc", [3072, 2048], BF16, kind="ExternalOutput")), qc_out[0:3072, :])
        tap_ln = ap(nc.dram_tensor("tap_ln", [128, 12], F32, kind="ExternalOutput"))
        tap_xln = ap(nc.dram_tensor("tap_xln", [128, 2048], BF16, kind="ExternalOutput"))

    with ctx:
        b = Builder(nc, ctx, S)
        sb = lambda name, shape, dt: ctx.enter_context(nc.sbuf_tensor(name, shape, dt))

        identb = sb("identb", [128, 128], BF16)
        oidx = sb("oidx", [128, 32], I32)
        ones_r = sb("ones_r", [1, 512], BF16)
        ones_c = sb("ones_c", [128, 1], BF16)
        s1v = sb("s1v", [128, 2], F32)
        sqv = sb("sqv", [128, 2], F32)
        mv = sb("mv", [128, 2], F32)
        msqv = sb("msqv", [128, 2], F32)
        varv = sb("varv", [128, 2], F32)
        rstdv = sb("rstdv", [128, 2], F32)
        cst = sb("cst", [128, 2], F32)

        s_consts = b.sem("s_consts")
        s_cast = b.sem("s_cast")
        s_xt = [b.sem(f"s_xt{i}") for i in range(2)]

        block = ctx.enter_context(nc.Block())

        b.dma("sync", identb[:], identb_in, s_consts)
        b.dma("sync", oidx[:], oidx_in, s_consts)
        b.dma("gpsimd", wq_bf, wq_in, s_cast)
        b.dma("gpsimd", wk_bf, wk_in, s_cast)
        b.dma("gpsimd", wv_bf, wv_in, s_cast)
        b.dma("gpsimd", wo_bf, wo_in, s_cast)
        b.op("vector", lambda e: e.memset(ones_r[:], 1.0), inc=b.ve)
        b.op("vector", lambda e: e.memset(cst[:, 0:1], LN_EPS), inc=b.ve)
        b.op("vector", lambda e: e.memset(cst[:, 1:2], RMS_EPS), inc=b.ve)
        b.op("vector", lambda e: e.memset(ones_c[:], 1.0), inc=b.ve)


        TC = min(S, 2048)
        NHALF = S // TC
        with ExitStack() as p1:
            psb = lambda name, shape, dt: p1.enter_context(nc.psum_tensor(name, shape, dt))
            sb1 = lambda name, shape, dt: p1.enter_context(nc.sbuf_tensor(name, shape, dt))
            xln1T = sb1("xln1T", [128, 16 * TC], BF16)
            wT = [sb1(f"wT{i}", [128, 16 * 1024], BF16) for i in range(2)]
            qstage = [sb1(f"qstage{i}", [128, 512], BF16) for i in range(4)]
            xt = [sb1(f"p1xt{i}", [128, D], F32) for i in range(2)]
            xlnt = [sb1(f"p1xlnt{i}", [128, D], BF16) for i in range(2)]
            sqscr = sb1("p1sqscr", [128, D], F32)
            bq_sb = sb1("bq_sb", [1, NHL * HD], BF16)
            bk_sb = sb1("bk_sb", [1, NHL * HD], BF16)
            bv_sb = sb1("bv_sb", [1, NHL * HD], BF16)
            b.dma("gpsimd", bq_sb[:], bq_in, s_cast)
            b.dma("gpsimd", bk_sb[:], bk_in, s_cast)
            b.dma("gpsimd", bv_sb[:], bv_in, s_cast)
            psp = [psb(f"p1_psp{i}", [128, 512], F32) for i in range(4)]
            pst1 = [psb(f"p1_pst{i}", [128, 128], BF16) for i in range(2)]
            s_wT = [b.sem(f"s_wT{i}") for i in range(2)]
            s_qst = [b.sem(f"s_qst{i}") for i in range(4)]

            pst_read = [0, 0]
            psp_read = [0, 0, 0, 0]
            sqv_read = [0, 0]
            ln_ac = [0, 0]
            xt_sq = [0, 0]
            xt_ap = [0, 0]
            xlnt_read = [0, 0]

            xln_tile_ve = [0] * 16

            def emit_ln_tile(half, i):
                par = i % 2
                c = par
                t0 = half * TC + i * 128
                wl = [(b.ve, xt_ap[par]), (b.ac, xt_sq[par])] if (half + i) else []
                b.dma("sync", xt[par][:], x_full[t0:t0 + 128, :], s_xt[par], waits=wl)
                lw = [(s_xt[par], s_xt[par].n)]
                b.op("vector", lambda e, o=s1v[:, c:c + 1], a=xt[par][:]:
                     e.tensor_reduce(o, a, AX.X, ALU.add), waits=lw, inc=b.ve)
                b.op("vector", lambda e, o=mv[:, c:c + 1], a=s1v[:, c:c + 1]:
                     e.tensor_scalar(o, a, -1.0 / D, None, ALU.mult),
                     waits=[(b.ve, b.ve.n), (b.ac, ln_ac[c])], inc=b.ve)
                mv_ve = b.ve.n
                b.op("scalar", lambda e, a=xt[par][:], m2=mv[:, c:c + 1],
                     o=sqscr[:]:
                     e.activation(o, a, AF.Square, bias=m2),
                     waits=lw + [(b.ve, mv_ve), (b.ve, max(sqv_read))],
                     inc=b.ac)
                sq_ac = b.ac.n
                xt_sq[par] = sq_ac
                b.op("vector", lambda e, o=sqv[:, c:c + 1], a=sqscr[:]:
                     e.tensor_reduce(o, a, AX.X, ALU.add),
                     waits=[(b.ac, sq_ac)], inc=b.ve)
                sqv_read[par] = b.ve.n
                b.op("scalar", lambda e, o=rstdv[:, c:c + 1], a=sqv[:, c:c + 1]:
                     e.activation(o, a, AF.Ln, scale=1.0 / D, bias=cst[:, 0:1]),
                     waits=[(b.ve, sqv_read[par])], inc=b.ac)
                b.op("scalar", lambda e, o=rstdv[:, c:c + 1], a=rstdv[:, c:c + 1]:
                     e.activation(o, a, AF.Exp, scale=-0.5),
                     waits=[(b.ac, b.ac.n)], inc=b.ac)
                ln_ac[c] = b.ac.n
                b.op("vector", lambda e, o=xlnt[par][:], a=xt[par][:],
                     m2=mv[:, c:c + 1], r2=rstdv[:, c:c + 1]:
                     e.tensor_scalar(o, a, m2, r2, ALU.add, ALU.mult),
                     waits=[(b.ac, ln_ac[c]), (b.pe, xlnt_read[par])],
                     inc=b.ve)
                apply_ve = b.ve.n
                xt_ap[par] = apply_ve
                for k in range(16):
                    tp = k % 2
                    b.op("tensor", lambda e, o=pst1[tp][:],
                         a=xlnt[par][:, 128 * k:128 * (k + 1)]:
                         e.transpose(o, a, identb[:]),
                         waits=[(b.ve, apply_ve), (b.ve, pst_read[tp])],
                         inc=b.pe)
                    b.op("vector", lambda e,
                         o=xln1T[:, TC * k + 128 * i:TC * k + 128 * (i + 1)],
                         a=pst1[tp][:]: e.tensor_copy(o, a),
                         waits=b.w(b.pe), inc=b.ve, drain=False)
                    pst_read[tp] = b.ve.n
                xln_tile_ve[i] = b.ve.n
                xlnt_read[par] = b.pe.n

            def load_w(wpar, src_bf, wl):
                base = s_wT[wpar].n
                for k in range(16):
                    b.dma("sync", wT[wpar][:, 1024 * k:1024 * (k + 1)],
                          src_bf[0:1024, 128 * k:128 * (k + 1)], s_wT[wpar],
                          waits=wl if k == 0 else (), transpose=True)
                return base

            pcnt = [0]

            def emit_qk_tg(wpar, bias_sb, dst, half, tg, base, first):
                for hd in range(8):
                    pp = pcnt[0] % 4
                    pcnt[0] += 1
                    for dc in range(16):
                        wl2 = ([(b.ve, xln_tile_ve[4 * tg + 3]),
                                (b.ac, psp_read[pp])] if dc == 0 else [])
                        if first and hd == 0:
                            wl2 = wl2 + [(s_wT[wpar], base + 16 * (dc + 1))]
                        b.op("tensor", lambda e, o=psp[pp][:],
                             l=wT[wpar][:, 1024 * dc + 128 * hd:1024 * dc + 128 * (hd + 1)],
                             r=xln1T[:, TC * dc + 512 * tg:TC * dc + 512 * (tg + 1)],
                             s=(dc == 0):
                             e.matmul(o, l, r, start=s, stop=False),
                             waits=wl2)
                    b.op("tensor", lambda e, o=psp[pp][:],
                         l=bias_sb[0:1, 128 * hd:128 * (hd + 1)]:
                         e.matmul(o, l, ones_r[0:1, :], start=False, stop=True),
                         inc=b.pe)
                    b.op("scalar", lambda e, o=qstage[pp][:], a=psp[pp][:]:
                         e.activation(o, a, AF.Copy),
                         waits=b.w(b.pe) + [(s_qst[pp], s_qst[pp].n)],
                         inc=b.ac)
                    psp_read[pp] = b.ac.n
                    b.dma("scalar", dst[128 * hd:128 * (hd + 1),
                                        half * TC + 512 * tg:half * TC + 512 * (tg + 1)],
                          qstage[pp][:], s_qst[pp],
                          waits=[(b.ac, b.ac.n)])

            for half in range(NHALF):
                base_q = load_w(0, wq_bf,
                                b.w(s_cast) + (b.w(b.pe) if half else []))
                base_k = load_w(1, wk_bf, [])
                for tg in range(TC // 512):
                    if half == 0:
                        for i in range(4 * tg, 4 * tg + 4):
                            emit_ln_tile(0, i)
                    emit_qk_tg(0, bq_sb, qT_scr, half, tg, base_q, tg == 0)
                    emit_qk_tg(1, bk_sb, kT_scr, half, tg, base_k, tg == 0)
                base_v = load_w(0, wv_bf, b.w(b.pe))
                for tt in range(TC // 128):
                    for hg in range(2):
                        pp = pcnt[0] % 4
                        pcnt[0] += 1
                        for dc in range(16):
                            wl2 = ([(b.ve, xln_tile_ve[tt]),
                                    (b.ac, psp_read[pp])] if dc == 0 else [])
                            if tt == 0 and hg == 0:
                                wl2 = wl2 + [(s_wT[0], base_v + 16 * (dc + 1))]
                            b.op("tensor", lambda e, o=psp[pp][:],
                                 l=xln1T[:, TC * dc + 128 * tt:TC * dc + 128 * (tt + 1)],
                                 r=wT[0][:, 1024 * dc + 512 * hg:1024 * dc + 512 * (hg + 1)],
                                 s=(dc == 0):
                                 e.matmul(o, l, r, start=s, stop=False),
                                 waits=wl2)
                        b.op("tensor", lambda e, o=psp[pp][:],
                             r=bv_sb[0:1, 512 * hg:512 * (hg + 1)]:
                             e.matmul(o, ones_r[0:1, 0:128], r, start=False, stop=True),
                             inc=b.pe)
                        b.op("scalar", lambda e, o=qstage[pp][:], a=psp[pp][:]:
                             e.activation(o, a, AF.Copy),
                             waits=b.w(b.pe) + [(s_qst[pp], s_qst[pp].n)],
                             inc=b.ac)
                        psp_read[pp] = b.ac.n
                        b.dma("scalar", v_scr[half * TC + 128 * tt:half * TC + 128 * (tt + 1),
                                              512 * hg:512 * (hg + 1)],
                              qstage[pp][:], s_qst[pp],
                              waits=[(b.ac, b.ac.n)])
                    if half + 1 < NHALF:
                        emit_ln_tile(half + 1, tt)

            wb = [sb1(f"q_wb{i}", [128, 2048], BF16) for i in range(2)]
            gs = sb1("q_gs", [128, 32], F32)
            meanb = sb1("q_meanb", [128, 32], BF16)
            qs32 = sb1("q_s32", [128, 32], F32)
            qth = sb1("q_th", [128, 32], F32)
            qnth = sb1("q_nth", [128, 32], F32)
            tpos = sb1("q_tpos", [128, 2048], BF16)
            tneg = sb1("q_tneg", [128, 2048], BF16)
            wqo = sb1("q_wqo", [128, 2048], BF16)
            asm_q = sb1("q_asm", [128, 16 * 256], BF16)
            s_wb = [b.sem(f"s_wb{i}") for i in range(2)]
            s_qc = b.sem("s_qc")
            wb_read = [0, 0]
            qpst_read = [0, 0]
            qtr_pe = [0]
            lnpst_ve = b.ve.n

            def quant_tile(i, src_ap, W, asm_cols):
                par = i % 2
                b.dma("gpsimd", wb[par][:, 0:W], src_ap, s_wb[par],
                      waits=[(b.ve, wb_read[par])])
                wbt = wb[par][:, 0:W]
                wb3 = wbt.rearrange("p (g k) -> p g k", k=GQ)
                ng = W // GQ
                wl = [(s_wb[par], s_wb[par].n)]
                b.op("vector", lambda e, o=gs[:, 0:ng], i3=wb3: e.tensor_reduce(
                    o, i3, AX.X, ALU.add, apply_absolute_value=True), waits=wl,
                    inc=b.ve)
                b.op("vector", lambda e, o=meanb[:, 0:ng], a=gs[:, 0:ng]:
                     e.tensor_scalar(o, a, 1.0 / GQ, None, ALU.mult),
                     waits=[(b.ve, b.ve.n)], inc=b.ve)
                b.op("vector", lambda e, o=meanb[:, 0:ng], a=meanb[:, 0:ng]:
                     e.tensor_scalar(o, a, QEPS, None, ALU.max),
                     waits=[(b.ve, b.ve.n)], inc=b.ve)
                b.op("vector", lambda e, o=qs32[:, 0:ng], a=meanb[:, 0:ng]:
                     e.tensor_copy(o, a), waits=[(b.ve, b.ve.n)], inc=b.ve)
                b.op("vector", lambda e, o=qth[:, 0:ng], a=qs32[:, 0:ng]:
                     e.tensor_scalar(o, a, TH_C, None, ALU.mult),
                     waits=[(b.ve, b.ve.n)], inc=b.ve)
                b.op("vector", lambda e, o=qnth[:, 0:ng], a=qth[:, 0:ng]:
                     e.tensor_scalar(o, a, -1.0, None, ALU.mult),
                     waits=[(b.ve, b.ve.n)], inc=b.ve)
                thb = qth[:, 0:ng].unsqueeze(2).broadcast_to([128, ng, GQ])
                nthb = qnth[:, 0:ng].unsqueeze(2).broadcast_to([128, ng, GQ])
                s32b = qs32[:, 0:ng].unsqueeze(2).broadcast_to([128, ng, GQ])
                tp3 = tpos[:, 0:W].rearrange("p (g k) -> p g k", k=GQ)
                tn3 = tneg[:, 0:W].rearrange("p (g k) -> p g k", k=GQ)

                b.op("vector", lambda e, o=tp3, a=wb3, c=thb:
                     e.tensor_tensor(o, a, c, ALU.is_gt),
                     waits=[(b.ve, b.ve.n)])
                b.op("vector", lambda e, o=tn3, a=wb3, c=nthb:
                     e.tensor_tensor(o, a, c, ALU.is_lt))
                b.op("vector", lambda e, o=tpos[:, 0:W], a=tpos[:, 0:W], c=tneg[:, 0:W]:
                     e.tensor_tensor(o, a, c, ALU.subtract))
                b.op("vector", lambda e, o=tp3, a=tp3, c=s32b:
                     e.tensor_tensor(o, a, c, ALU.mult))
                b.op("vector", lambda e, o=tneg[:, 0:W], a=tpos[:, 0:W], c=wbt:
                     e.tensor_tensor(o, a, c, ALU.subtract))
                b.op("vector", lambda e, o=wqo[:, 0:W], a=wbt, c=tneg[:, 0:W]:
                     e.tensor_tensor(o, a, c, ALU.add),
                     waits=[(b.pe, qtr_pe[0])], inc=b.ve)
                wb_read[par] = b.ve.n
                wqo_ve = b.ve.n
                nk = W // 128
                ib = i % 2
                for k in range(nk):
                    tp = k % 2
                    b.op("tensor", lambda e, o=pst1[tp][:],
                         a=wqo[:, 128 * k:128 * (k + 1)]:
                         e.transpose(o, a, identb[:]),
                         waits=[(b.ve, wqo_ve), (b.ac, qpst_read[tp]),
                                (b.ve, lnpst_ve)],
                         inc=b.pe)
                    qtr_pe[0] = b.pe.n
                    b.op("scalar", lambda e,
                         o=asm_q[:, asm_cols * k + 128 * ib:asm_cols * k + 128 * (ib + 1)],
                         a=pst1[tp][:]: e.activation(o, a, AF.Copy),
                         waits=b.w(b.pe) + [(s_qc, s_qc.n)], inc=b.ac,
                         drain=(k == nk - 1 and ib == 1))
                    qpst_read[tp] = b.ac.n

            for bb in range(8):
                for ii in range(2):
                    i = 2 * bb + ii
                    quant_tile(i, wgu_in[128 * i:128 * (i + 1), :], 2048, 256)
                b.dma("scalar",
                      qc_in[0:2048, 256 * bb:256 * (bb + 1)].rearrange(
                          "(k p) h -> p k h", p=128),
                      asm_q[:].rearrange("p (k h) -> p k h", h=256), s_qc,
                      waits=[(b.ac, b.ac.n)])
            for bb in range(8):
                for ii in range(2):
                    i = 2 * bb + ii
                    quant_tile(i, wp_in[128 * i:128 * (i + 1), :], 1024, 256)
                b.dma("scalar",
                      qc_in[2048:3072, 256 * bb:256 * (bb + 1)].rearrange(
                          "(k p) h -> p k h", p=128),
                      asm_q[:, 0:8 * 256].rearrange("p (k h) -> p k h", h=256), s_qc,
                      waits=[(b.ac, b.ac.n)])
            b.op("gpsimd", lambda e: e.collective_compute(
                "AllGather", ALU.bypass, replica_groups=[list(range(8))],
                ins=[qc_in.opt()], outs=[qc_out.opt()]),
                waits=b.w(s_qc), inc=b.cc, raw_inc=True)
            cc_q = b.cc.n
            b.barrier()
            b.flush(block)

        with ExitStack() as p2:
            psb = lambda name, shape, dt: p2.enter_context(nc.psum_tensor(name, shape, dt))
            sb2 = lambda name, shape, dt: p2.enter_context(nc.sbuf_tensor(name, shape, dt))
            qh = sb2("qh", [128, 2 * S], BF16)
            kh = sb2("kh", [128, 2 * S], BF16)
            vh = sb2("vh", [128, 2 * S], BF16)
            pT = [sb2(f"pT{i}", [128, 1024], BF16) for i in range(3)]
            acc_d = sb2("acc_d", [128, 1024], F32)
            acc_p = sb2("acc_p", [128, 1024], F32)
            acc_f = sb2("acc_f", [128, 512], F32)
            acc_bf = sb2("acc_bf", [128, 512], BF16)
            r_full = [sb2(f"r_full{i}", [1, S], F32) for i in range(2)]
            esA = [sb2(f"esA{i}", [128, 512], F32) for i in range(2)]
            esB = [sb2(f"esB{i}", [128, 512], F32) for i in range(2)]
            esum_b = [sb2(f"esum_b{i}", [128, 1024], BF16) for i in range(2)]
            r_2 = sb2("r_2", [1, S], F32)
            r_hi = sb2("r_hi", [1, S], BF16)
            r_lo = sb2("r_lo", [1, S], BF16)
            rb_sb = [sb2(f"rb_sb{i}", [128, 512], F32) for i in range(2)]
            ostage = [sb2(f"ostage{i}", [128, S], BF16) for i in range(2)]
            masks = sb2("masks", [128, 4 * 512], BF16)
            b.dma("sync", masks[:], masks_in, s_consts)
            pss = [psb(f"pss{i}", [128, 1024], F32) for i in range(2)]
            pso = [psb(f"pso{i}", [128, 512], F32) for i in range(2)]
            psr = psb("psr", [1, 512], F32)
            psbc = psb("psbc", [128, 512], F32)
            s_head = [b.sem(f"s_head{i}") for i in range(2)]
            s_ost = b.sem("s_ost")

            NG = S // 512

            def load_head(h, slot, prefetched):
                if prefetched:
                    wl = b.w(b.pe, b.ve)
                else:
                    wl = b.w(*s_qst)
                b.dma("sync", qh[:, slot * S:(slot + 1) * S],
                      qT_scr[128 * h:128 * (h + 1), :], s_head[slot], waits=wl)
                b.dma("sync", kh[:, slot * S:(slot + 1) * S],
                      kT_scr[128 * h:128 * (h + 1), :], s_head[slot])
                b.dma("sync", vh[:, slot * S:(slot + 1) * S].rearrange("p (j h) -> p j h", h=128),
                      v_scr[0:S, 128 * h:128 * (h + 1)].rearrange("(j p) h -> p j h", p=128),
                      s_head[slot])

            horder = [4, 5, 6, 7, 0, 1, 2, 3] if NHL == 8 else list(range(NHL))
            load_head(horder[0], 0, False)
            if NHL > 1:
                load_head(horder[1], 1, False)
            lastexp = [0, 0]
            pT_free = [[], [], []]
            oevac = [0, 0]
            psr_free = 0
            last_bmm = 0
            rf_read = [0, 0]
            rf_done = [0, 0]
            psbc_evac = [0]
            cc_b = 0

            bc_ve_s = [0]

            def emit_tail_chain(h, hp):
                nonlocal last_bmm
                b.op("scalar", lambda e: e.activation(r_2[0:1, 0:S],
                                                      r_full[hp][0:1, 0:S], AF.Ln),
                     waits=[(b.ve, rf_done[hp])], inc=b.ac)
                rf_read[hp] = b.ac.n
                b.op("scalar", lambda e: e.activation(r_2[0:1, 0:S], r_2[0:1, 0:S],
                                                      AF.Exp, scale=-1.0),
                     waits=[(b.ac, b.ac.n)], inc=b.ac)
                b.op("vector", lambda e: e.tensor_copy(r_hi[0:1, 0:S], r_2[0:1, 0:S]),
                     waits=[(b.pe, last_bmm)] + b.w(b.ac), inc=b.ve, drain=False)
                b.op("vector", lambda e: e.tensor_tensor(
                    r_lo[0:1, 0:S], r_2[0:1, 0:S], r_hi[0:1, 0:S], ALU.subtract),
                    waits=[(b.ve, b.ve.n)], inc=b.ve)
                bc_ve_s[0] = b.ve.n

            def emit_tail(h, hp):
                nonlocal last_bmm, cc_b
                bc_ve = bc_ve_s[0]
                for g in range(NG):
                    bp = g % 2
                    wl = [(b.ve, bc_ve if g == 0 else psbc_evac[0])]
                    b.op("tensor", lambda e, r=r_hi[0:1, 512 * g:512 * (g + 1)]:
                         e.matmul(psbc[:], ones_r[0:1, 0:128], r, start=True, stop=False),
                         waits=wl)
                    b.op("tensor", lambda e, r=r_lo[0:1, 512 * g:512 * (g + 1)]:
                         e.matmul(psbc[:], ones_r[0:1, 0:128], r, start=False, stop=True),
                         inc=b.pe)
                    last_bmm = b.pe.n
                    b.op("vector", lambda e, o=rb_sb[bp][:]:
                         e.tensor_copy(o, psbc[:]),
                         waits=[(b.pe, last_bmm)], inc=b.ve, drain=False)
                    psbc_evac[0] = b.ve.n
                    b.op("vector", lambda e, o=ostage[hp][:, 512 * g:512 * (g + 1)],
                         b2=bp: e.tensor_tensor(o, o, rb_sb[b2][:], ALU.mult),
                         waits=[(b.ve, b.ve.n)], inc=b.ve)
                b.dma("scalar", o_loc[128 * h:128 * (h + 1), :], ostage[hp][:], s_ost,
                      waits=b.w(b.ve))
                if h == (7 if NHL == 8 else 3):
                    b.op("gpsimd", lambda e: e.collective_compute(
                        "AllGather", ALU.bypass, replica_groups=[list(range(8))],
                        ins=[o_loc[512:1024, :].opt()], outs=[o_all_b.opt()]),
                        waits=b.w(s_ost), inc=b.cc, raw_inc=True)
                    cc_b = b.cc.n

            summm = [0, 0]
            esum_done = [0, 0]
            for hi in range(NHL):
                h = horder[hi]
                slot = hi % 2
                hp = hi % 2
                ost_wait = s_ost.n
                hl = [(s_head[slot], s_head[slot].n)]
                steps = [(g, jp) for g in range(NG) for jp in range(2 * g + 2)]

                def emit_av(g, jp, pb, gate):
                    nonlocal psr_free
                    gp = g % 2
                    njp = 2 * g + 2
                    for half2 in range(2):
                        j = 2 * jp + half2
                        wl = []
                        if half2 == 0:
                            wl = [(b.ac, gate)]
                            if jp == 0:
                                wl.append((b.ve, oevac[gp]))
                        b.op("tensor", lambda e, o=pso[gp][:],
                             l=vh[:, slot * S + 128 * j:slot * S + 128 * (j + 1)],
                             r=pT[pb][:, 512 * half2:512 * (half2 + 1)],
                             s=(jp == 0 and half2 == 0),
                             t=(jp == njp - 1 and half2 == 1):
                             e.matmul(o, l, r, start=s, stop=t),
                             waits=wl, inc=(b.pe if half2 == 1 else None))
                    pT_free[pb] = [(b.pe, b.pe.n)]
                    if jp == njp - 1:
                        av_pe = b.pe.n
                        b.op("tensor", lambda e, m=esum_b[gp][:, 0:512]:
                             e.matmul(psr[:], ones_c[:], m, start=True, stop=False),
                             waits=[(b.ve, esum_done[gp]), (b.ve, psr_free)])
                        b.op("tensor", lambda e, m=esum_b[gp][:, 512:1024]:
                             e.matmul(psr[:], ones_c[:], m, start=False, stop=True),
                             inc=b.pe)
                        summm[gp] = b.pe.n
                        b.op("vector", lambda e, o=ostage[hp][:, 512 * g:512 * (g + 1)],
                             g2=gp: e.tensor_copy(o, pso[g2][:]),
                             waits=[(b.pe, av_pe)] +
                                   ([(s_ost, ost_wait)] if g == 0 else []),
                             inc=b.ve, drain=False)
                        oevac[gp] = b.ve.n
                        b.op("vector", lambda e, o=r_full[hp][0:1, 512 * g:512 * (g + 1)],
                             g2=gp: e.tensor_copy(o, psr[:]),
                             waits=[(b.pe, summm[gp]), (b.ac, rf_read[hp])],
                             inc=b.ve, drain=(g == NG - 1))
                        psr_free = b.ve.n
                        if g == NG - 1:
                            rf_done[hp] = b.ve.n

                pend = None
                esum_read = [0, 0, 0]
                for i, (g, jp) in enumerate(steps):
                    pp = i % 2
                    pb = i % 3
                    gp = g % 2
                    wl = [(b.ac, lastexp[pp])] + (hl if i == 0 else [])
                    sc_ops = []
                    for half2 in range(2):
                        j = 2 * jp + half2
                        diag = j >= 4 * g
                        sc_ops.append((
                            lambda e, o=pss[pp][:, 512 * half2:512 * (half2 + 1)],
                            l=kh[:, slot * S + 128 * j:slot * S + 128 * (j + 1)],
                            r=qh[:, slot * S + 512 * g:slot * S + 512 * (g + 1)],
                            t=(not diag):
                            e.matmul(o, l, r, start=True, stop=t),
                            wl if half2 == 0 else []))
                        if diag:
                            mi = j - 4 * g
                            sc_ops.append((
                                lambda e, o=pss[pp][:, 512 * half2:512 * (half2 + 1)],
                                m=masks[:, 512 * mi:512 * (mi + 1)]:
                                e.matmul(o, identb[:], m, start=False, stop=True),
                                b.w(s_consts) if (h == 0 and g == 0) else []))
                    for si, (fn, swl) in enumerate(sc_ops):
                        b.op("tensor", fn, waits=swl,
                             inc=(b.pe if si == len(sc_ops) - 1 else None))
                    sc_pe = b.pe.n
                    b.op("scalar", lambda e, o=pT[pb][:], a=pss[pp][:]:
                         e.activation(o, a, AF.Exp, scale=SM_SCALE),
                         waits=[(b.pe, sc_pe), (b.ve, esum_read[pb])] +
                               pT_free[pb],
                         inc=b.ac, drain=False)
                    lastexp[pp] = b.ac.n
                    gate = b.ac.n
                    if pend is not None:
                        emit_av(*pend)
                    last = (jp == 2 * g + 1)
                    if jp == 0:
                        b.op("vector", lambda e, o=esA[gp][:], a=pT[pb][:, 0:512]:
                             e.tensor_copy(o, a),
                             waits=[(b.ac, gate), (b.pe, summm[gp])],
                             inc=b.ve, drain=False)
                        b.op("vector", lambda e, o=esB[gp][:], a=pT[pb][:, 512:1024]:
                             e.tensor_copy(o, a), inc=b.ve, drain=False)
                    else:
                        oA = esum_b[gp][:, 0:512] if last else esA[gp][:]
                        oB = esum_b[gp][:, 512:1024] if last else esB[gp][:]
                        b.op("vector", lambda e, o=oA, s1=esA[gp][:],
                             a=pT[pb][:, 0:512]:
                             e.tensor_tensor(o, s1, a, ALU.add),
                             waits=[(b.ac, gate)], inc=b.ve, drain=False)
                        b.op("vector", lambda e, o=oB, s1=esB[gp][:],
                             a=pT[pb][:, 512:1024]:
                             e.tensor_tensor(o, s1, a, ALU.add),
                             inc=b.ve, drain=last)
                        if last:
                            esum_done[gp] = b.ve.n
                    esum_read[pb] = b.ve.n
                    pend = (g, jp, pb, gate)
                    if hi > 0 and i == min(2, len(steps) - 3):
                        emit_tail_chain(horder[hi - 1], (hi - 1) % 2)
                    if hi > 0 and i == min(40, len(steps) - 2):
                        emit_tail(horder[hi - 1], (hi - 1) % 2)
                emit_av(*pend)
                if hi + 2 < NHL:
                    load_head(horder[hi + 2], hi % 2, True)
            emit_tail_chain(horder[NHL - 1], (NHL - 1) % 2)
            emit_tail(horder[NHL - 1], (NHL - 1) % 2)
            b.op("gpsimd", lambda e: e.collective_compute(
                "AllGather", ALU.bypass, replica_groups=[list(range(8))],
                ins=[o_loc[0:512, :].opt()], outs=[o_all_a.opt()]),
                waits=b.w(s_ost), inc=b.cc, raw_inc=True)
            cc_a = b.cc.n
            b.barrier()
            b.flush(block)

        with ExitStack() as p3:
            psb = lambda name, shape, dt: p3.enter_context(nc.psum_tensor(name, shape, dt))
            sb3 = lambda name, shape, dt: p3.enter_context(nc.sbuf_tensor(name, shape, dt))
            og = sb3("og", [128, 16 * OH], BF16)
            woT = sb3("woT", [128, 16 * 2048], BF16)
            x1t = [sb3(f"x1t{i}", [128, D], F32) for i in range(2)]
            x2st = [sb3(f"x2st{i}", [128, D], BF16) for i in range(2)]
            xt = [sb3(f"p3xt{i}", [128, D], F32) for i in range(2)]
            xlnt = [sb3(f"p3xlnt{i}", [128, D], BF16) for i in range(2)]
            sqscr = sb3("p3sqscr", [128, D], F32)
            bo_sb = sb3("bo_sb", [1, D], BF16)
            b.dma("gpsimd", bo_sb[:], bo_in, s_cast)
            psop = [psb(f"psop{i}", [128, 512], F32) for i in range(2)]
            pst3 = [psb(f"p3_pst{i}", [128, 128], BF16) for i in range(2)]
            s_og = b.sem("s_og")
            s_woT = b.sem("s_woT")
            s_x1 = [b.sem(f"s_x1{i}") for i in range(2)]
            s_x2T = [b.sem(f"s_x2T{i}") for i in range(2)]

            o_view_a = o_all_a.rearrange("r (q t) -> (r q) t", t=OH)
            o_view_b = o_all_b.rearrange("r (q t) -> (r q) t", t=OH)
            forder = [4, 5, 6, 7, 12, 13, 14, 15, 0, 1, 2, 3, 8, 9, 10, 11] \
                if NHL == 8 else [0, 1, 2, 3, 8, 9, 10, 11, 4, 5, 6, 7, 12, 13, 14, 15]
            for ki, k in enumerate(forder):
                b.dma("sync", woT[:, 2048 * k:2048 * (k + 1)],
                      wo_bf[0:2048, 128 * k:128 * (k + 1)], s_woT,
                      waits=b.w(s_cast) if ki == 0 else (), transpose=True)

            psop_read = [0, 0]
            pst_read = [0, 0]
            lnsq_read = [0, 0]
            lnsq_ac = [0, 0]
            ln_ac3 = [0, 0]
            xt_ap3 = [0, 0]
            xlnt_read3 = [0, 0]
            for hf in range(2):
                gw = (b.w(b.pe) if hf == 1 else [])
                for fi, f in enumerate(forder):
                    col = f + 16 * hf
                    vw = o_view_a if (f % 8) < 4 else o_view_b
                    wl0 = []
                    if fi == 0:
                        wl0 = gw + b.w(s_consts) + [(b.cc, cc_b if NHL == 8 else cc_a)]
                    elif fi == 8:
                        wl0 = [(b.cc, cc_a if NHL == 8 else cc_b)]
                    b.op("gpsimd", lambda e, o=og[:, OH * f:OH * (f + 1)],
                         ix=oidx[:, col:col + 1], v2=vw:
                         e.indirect_dma_start(o, None, v2,
                                              bass.IndirectOffsetOnAxis(ix, 0)),
                         waits=wl0, inc=s_og, n=16)
                    if fi == 7:
                        og_a = s_og.n
                ogl = [(s_og, og_a)]
                og_b = s_og.n
                for sub in range(OH // 128):
                    par = sub % 2
                    c = par
                    r0 = hf * OH + sub * 128
                    b.dma("scalar", xt[par][:], x_res[r0:r0 + 128, :], s_xt[par],
                          waits=[(b.ve, xt_ap3[par])] if (hf + sub) else [])
                    for dg in range(4):
                        pp = dg % 2
                        for fi2 in range(16):
                            fc = forder[fi2]
                            wl = (ogl + [(s_woT, 16),
                                         (b.ve, psop_read[pp])]) if fi2 == 0 else \
                                ([(s_og, og_b)] if (fi2 == 8 and sub == 0 and dg == 0
                                                    ) else [])
                            if hf == 0 and sub == 0 and dg == 0 and fi2 > 0:
                                wl = wl + [(s_woT, 16 * (fi2 + 1))]
                            b.op("tensor", lambda e, o=psop[pp][:],
                                 l=og[:, OH * fc + 128 * sub:OH * fc + 128 * (sub + 1)],
                                 r=woT[:, 2048 * fc + 512 * dg:2048 * fc + 512 * (dg + 1)],
                                 s=(fi2 == 0):
                                 e.matmul(o, l, r, start=s, stop=False), waits=wl)
                        b.op("tensor", lambda e, o=psop[pp][:],
                             r=bo_sb[0:1, 512 * dg:512 * (dg + 1)]:
                             e.matmul(o, ones_r[0:1, 0:128], r, start=False, stop=True),
                             inc=b.pe)
                        wl = [(b.pe, b.pe.n), (s_xt[par], s_xt[par].n)]
                        if dg == 0:
                            wl += [(s_x1[par], s_x1[par].n), (b.ac, lnsq_ac[par])]
                        b.op("vector", lambda e, o=x1t[par][:, 512 * dg:512 * (dg + 1)],
                             a=psop[pp][:], x2=xt[par][:, 512 * dg:512 * (dg + 1)]:
                             e.tensor_tensor(o, a, x2, ALU.add), waits=wl, inc=b.ve)
                        psop_read[pp] = b.ve.n
                    x1_ve = b.ve.n
                    xt_ap3[par] = x1_ve
                    b.dma("scalar", x1_scr[r0:r0 + 128, :], x1t[par][:], s_x1[par],
                          waits=[(b.ve, x1_ve)])
                    b.op("vector", lambda e, o=s1v[:, c:c + 1], a=x1t[par][:]:
                         e.tensor_reduce(o, a, AX.X, ALU.add),
                         waits=[(b.ve, x1_ve)], inc=b.ve)
                    b.op("vector", lambda e, o=mv[:, c:c + 1], a=s1v[:, c:c + 1]:
                         e.tensor_scalar(o, a, -1.0 / D, None, ALU.mult),
                         waits=[(b.ve, b.ve.n), (b.ac, ln_ac3[c])], inc=b.ve)
                    mv_ve3 = b.ve.n
                    b.op("scalar", lambda e, a=x1t[par][:], m2=mv[:, c:c + 1]:
                         e.activation(sqscr[:], a, AF.Square, bias=m2),
                         waits=[(b.ve, mv_ve3), (b.ve, max(lnsq_read))], inc=b.ac)
                    lnsq_ac[par] = b.ac.n
                    b.op("vector", lambda e, o=sqv[:, c:c + 1]:
                         e.tensor_reduce(o, sqscr[:], AX.X, ALU.add),
                         waits=[(b.ac, b.ac.n)], inc=b.ve)
                    lnsq_read[0] = lnsq_read[1] = b.ve.n
                    b.op("scalar", lambda e, o=rstdv[:, c:c + 1], a=sqv[:, c:c + 1]:
                         e.activation(o, a, AF.Ln, scale=1.0 / D, bias=cst[:, 0:1]),
                         waits=[(b.ve, max(lnsq_read))], inc=b.ac)
                    b.op("scalar", lambda e, o=rstdv[:, c:c + 1], a=rstdv[:, c:c + 1]:
                         e.activation(o, a, AF.Exp, scale=-0.5),
                         waits=[(b.ac, b.ac.n)], inc=b.ac)
                    ln_ac3[c] = b.ac.n
                    b.op("vector", lambda e, o=xlnt[par][:], a=x1t[par][:],
                         m2=mv[:, c:c + 1], r2=rstdv[:, c:c + 1]:
                         e.tensor_scalar(o, a, m2, r2, ALU.add, ALU.mult),
                         waits=[(b.ac, ln_ac3[c]), (b.pe, xlnt_read3[par]),
                                (s_x2T[par], s_x2T[par].n)], inc=b.ve)
                    ap_ve = b.ve.n
                    for k in range(16):
                        tp = k % 2
                        b.op("tensor", lambda e, o=pst3[tp][:],
                             a=xlnt[par][:, 128 * k:128 * (k + 1)]:
                             e.transpose(o, a, identb[:]),
                             waits=[(b.ve, ap_ve), (b.ve, pst_read[tp])],
                             inc=b.pe)
                        b.op("vector", lambda e, o=x2st[par][:, 128 * k:128 * (k + 1)],
                             a=pst3[tp][:]: e.tensor_copy(o, a),
                             waits=b.w(b.pe), inc=b.ve, drain=(k == 15))
                        if k == 15:
                            xlnt_read3[par] = b.pe.n
                        pst_read[tp] = b.ve.n
                    b.dma("scalar",
                          x2T_scr[:, r0:r0 + 128].rearrange("(k p) q -> p k q", p=128),
                          x2st[par][:].rearrange("p (k q) -> p k q", q=128),
                          s_x2T[par], waits=b.w(b.ve))
            b.barrier()
            b.flush(block)

        with ExitStack() as p4:
            psb = lambda name, shape, dt: p4.enter_context(nc.psum_tensor(name, shape, dt))
            sb4 = lambda name, shape, dt: p4.enter_context(nc.sbuf_tensor(name, shape, dt))
            hT = sb4("hT", [128, 64 * 512], BF16)
            xl2 = sb4("xl2", [128, 16 * 512], BF16)
            wgu = [sb4(f"wgu{i}", [128, 16 * 1024], BF16) for i in range(3)]
            x1p = [sb4(f"x1p{i}", [128, 512], F32) for i in range(2)]
            sg = sb4("sg", [128, 4 * 512], BF16)
            hsq = [sb4(f"hsq{i}", [128, 512], BF16) for i in range(2)]
            ssA = sb4("ssA", [128, 512], F32)
            ssB = sb4("ssB", [128, 512], F32)
            ssb16 = sb4("ssb16", [128, 512], BF16)
            invr = sb4("invr", [1, 512], F32)
            inv_hi = sb4("inv_hi", [1, 512], BF16)
            inv_lo = sb4("inv_lo", [1, 512], BF16)
            invc = sb4("invc", [128, 4], F32)
            x2p = [sb4(f"x2p{i}", [128, 512], F32) for i in range(2)]
            pw = [psb(f"pw{i}", [128, 512], F32) for i in range(8)]
            s_xl2 = b.sem("s_xl2")
            s_x1p = [b.sem(f"s_x1p{i}") for i in range(2)]
            s_wgu = [b.sem(f"s_wgu{i}") for i in range(3)]
            s_out = [b.sem(f"s_out{i}") for i in range(2)]

            NGRP = T // 512
            silu_cnt = [0, 0, 0, 0]
            bank_read = [0, 0, 0, 0]
            sg_read = [0, 0, 0, 0]
            hsq_ve = [0, 0]
            wbuf_pe = [0, 0, 0]
            evac_cnt = [0] * 8
            x1p_read = [0, 0]
            ss_edge = (None, 0)
            inv55_read = 0

            def gu_window(habs):
                rb = habs // 2048
                off = habs % 2048
                return qc_out[3072 * rb:3072 * rb + 2048, off:off + 1024].rearrange(
                    "(k p) h -> p k h", p=128)

            def load_gu(k, first=False):
                buf = (k + 2) % 3
                jH = k // 2
                habs = (0 if k % 2 == 0 else FF) + 1024 * jH
                b.dma("sync", wgu[buf][:].rearrange("p (k h) -> p k h", h=1024),
                      gu_window(habs), s_wgu[buf],
                      waits=[(b.pe, wbuf_pe[buf])] +
                            ([(b.cc, cc_q)] if first else []))

            def load_xl2(grp, wl):
                t0 = grp * 512
                b.dma("gpsimd", xl2[:].rearrange("p (k t) -> p k t", t=512),
                      x2T_scr[:, t0:t0 + 512].rearrange("(k p) t -> p k t", p=128),
                      s_xl2, waits=wl)
                xl2_cnt[0] = s_xl2.n

            xl2_cnt = [0]
            load_xl2(0, b.w(s_x2T[0], s_x2T[1]))
            for grp in range(NGRP):
                t0 = grp * 512
                xl2w = [(s_xl2, xl2_cnt[0])]
                for k in ((0, 1, 2) if grp == 0 else (1, 2)):
                    load_gu(k, first=(grp == 0 and k == 0))

                for jH in range(8):
                    bg, bu = (2 * jH + 2) % 3, (2 * jH + 3) % 3
                    gl = [(s_wgu[bg], s_wgu[bg].n)]
                    ul = [(s_wgu[bu], s_wgu[bu].n)]
                    for rnd in range(2):
                        for hx4 in range(4):
                            hx = rnd * 4 + hx4
                            for dc in range(16):
                                wl2 = (gl + [(b.ve, bank_read[hx4]),
                                             (b.ve, evac_cnt[hx4])] + xl2w) \
                                    if dc == 0 else []
                                b.op("tensor", lambda e, o=pw[hx4][:], bg2=bg,
                                     l=wgu[bg][:, 1024 * dc + 128 * hx:1024 * dc + 128 * (hx + 1)],
                                     r=xl2[:, 512 * dc:512 * (dc + 1)], s=(dc == 0),
                                     t=(dc == 15):
                                     e.matmul(o, l, r, start=s, stop=t),
                                     waits=wl2, inc=(b.pe if dc == 15 else None))
                            gate_pe = b.pe.n
                            b.op("scalar", lambda e, o=sg[:, 512 * hx4:512 * (hx4 + 1)],
                                 a=pw[hx4][:]:
                                 e.activation(o, a, AF.Silu),
                                 waits=[(b.pe, gate_pe), (b.ve, sg_read[hx4])],
                                 inc=b.ac, drain=False)
                            silu_cnt[hx4] = b.ac.n
                        if rnd == 1:
                            wbuf_pe[bg] = b.pe.n
                        for hx4 in range(4):
                            hx = rnd * 4 + hx4
                            jc = 8 * jH + hx
                            for dc in range(16):
                                wl2 = (ul + [(b.ac, silu_cnt[hx4])]) if dc == 0 else []
                                b.op("tensor", lambda e, o=pw[hx4][:],
                                     l=wgu[bu][:, 1024 * dc + 128 * hx:1024 * dc + 128 * (hx + 1)],
                                     r=xl2[:, 512 * dc:512 * (dc + 1)], s=(dc == 0),
                                     t=(dc == 15):
                                     e.matmul(o, l, r, start=s, stop=t),
                                     waits=wl2, inc=(b.pe if dc == 15 else None))
                            up_pe = b.pe.n
                            b.op("vector", lambda e, o=hT[:, 512 * jc:512 * (jc + 1)],
                                 a=sg[:, 512 * hx4:512 * (hx4 + 1)], p=pw[hx4][:]:
                                 e.tensor_tensor(o, a, p, ALU.mult),
                                 waits=[(b.pe, up_pe)], inc=b.ve, drain=False)
                            bank_read[hx4] = b.ve.n
                            sg_read[hx4] = b.ve.n
                            hp = jc % 2
                            b.op("scalar", lambda e, o=hsq[hp][:],
                                 a=hT[:, 512 * jc:512 * (jc + 1)]:
                                 e.activation(o, a, AF.Square),
                                 waits=[(b.ve, b.ve.n), (b.ve, hsq_ve[hp])],
                                 inc=b.ac, drain=False)
                            sswl = [(b.ac, b.ac.n)]
                            acc = ssA if hp == 0 else ssB
                            if jc < 2:
                                b.op("vector", lambda e, o=acc[:], a=hsq[hp][:]:
                                     e.tensor_copy(o, a),
                                     waits=sswl, inc=b.ve, drain=False)
                            else:
                                b.op("vector", lambda e, o=acc[:], s1=acc[:],
                                     a=hsq[hp][:]:
                                     e.tensor_tensor(o, s1, a, ALU.add),
                                     waits=sswl, inc=b.ve, drain=False)
                            hsq_ve[hp] = b.ve.n
                            if jc == 63:
                                b.op("vector", lambda e:
                                     e.tensor_tensor(ssb16[:], ssA[:], ssB[:],
                                                     ALU.add),
                                     waits=[(b.ve, b.ve.n)], inc=b.ve)
                                ss16_ve = b.ve.n
                                mmwl = [(b.ve, ss16_ve), (b.ve, evac_cnt[4])]
                                if ss_edge[0] is not None:
                                    mmwl = mmwl + [ss_edge]
                                b.op("tensor", lambda e:
                                     e.matmul(pw[4][0:1, :], ones_c[:], ssb16[:],
                                              start=True, stop=True),
                                     waits=mmwl, inc=b.pe)
                        if rnd == 1:
                            wbuf_pe[bu] = b.pe.n
                    for k in (2 * jH + 3, 2 * jH + 4):
                        if k < 16:
                            load_gu(k)
                hT_done = b.ve.n
                gu_pe_last = b.pe.n
                if grp + 1 < NGRP:
                    load_xl2(grp + 1, [(b.pe, gu_pe_last)])

                b.op("scalar", lambda e: e.activation(invr[:], pw[4][0:1, :], AF.Ln,
                                                      scale=1.0 / FF, bias=cst[0:1, 1:2]),
                     waits=[(b.pe, gu_pe_last)], inc=b.ac)
                ss_edge = (b.ac, b.ac.n)
                b.op("scalar", lambda e: e.activation(invr[:], invr[:], AF.Exp,
                                                      scale=-0.5),
                     waits=[(b.ac, b.ac.n)], inc=b.ac)
                b.op("vector", lambda e: e.tensor_copy(inv_hi[:], invr[:]),
                     waits=b.w(b.ac) + [(b.pe, inv55_read)], inc=b.ve)
                b.op("vector", lambda e: e.tensor_tensor(inv_lo[:], invr[:], inv_hi[:],
                                                         ALU.subtract),
                     waits=[(b.ve, b.ve.n)], inc=b.ve)
                for sub in range(4):
                    b.op("tensor", lambda e, o=pw[5][:, sub:sub + 1],
                         l=inv_hi[0:1, 128 * sub:128 * (sub + 1)]:
                         e.matmul(o, l, ones_r[0:1, 0:1], start=True, stop=False),
                         waits=(b.w(b.ve) + [(b.ve, evac_cnt[5])]) if sub == 0 else ())
                    b.op("tensor", lambda e, o=pw[5][:, sub:sub + 1],
                         l=inv_lo[0:1, 128 * sub:128 * (sub + 1)]:
                         e.matmul(o, l, ones_r[0:1, 0:1], start=False, stop=True),
                         inc=b.pe)
                inv55_read = b.pe.n
                b.op("vector", lambda e: e.tensor_copy(invc[:], pw[5][:, 0:4]),
                     waits=b.w(b.pe), inc=b.ve)
                inv_ve = b.ve.n
                evac_cnt[5] = max(evac_cnt[5], inv_ve)

                wqp = [wgu[0][:, 0:8 * 1024], wgu[1][:, 0:8 * 1024]]
                for wave in range(2):
                    for rb in range(8):
                        wp_par = rb % 2
                        b.dma("sync", wqp[wp_par].rearrange("p (k d) -> p k d", d=1024),
                              qc_out[3072 * rb + 2048:3072 * rb + 3072,
                                     1024 * wave:1024 * (wave + 1)].rearrange(
                                         "(k p) d -> p k d", p=128),
                              s_wgu[wp_par], waits=[(b.pe, wbuf_pe[wp_par])])
                        for pair in range(8):
                            sub, dgw = pair // 2, pair % 2
                            for hx in range(8):
                                j = 8 * rb + hx
                                wl2 = []
                                if hx == 0:
                                    wl2 = [(s_wgu[wp_par], s_wgu[wp_par].n)]
                                    if rb == 0:
                                        wl2 += [(b.ve, hT_done), (b.ve, inv_ve),
                                                (b.ve, evac_cnt[pair])]
                                b.op("tensor", lambda e, o=pw[pair][:], w2=wp_par,
                                     l=hT[:, 512 * j + 128 * sub:512 * j + 128 * (sub + 1)],
                                     r=wqp[wp_par][:, 1024 * hx + 512 * dgw:1024 * hx + 512 * (dgw + 1)],
                                     s=(rb == 0 and hx == 0), t=(rb == 7 and hx == 7):
                                     e.matmul(o, l, r, start=s, stop=t),
                                     waits=wl2,
                                     inc=(b.pe if hx == 7 else None))
                        wbuf_pe[wp_par] = b.pe.n
                    if wave == 0 and grp + 1 < NGRP:
                        load_gu(0)
                    proj_pe = b.pe.n
                    for pair in range(8):
                        sub, dgw = pair // 2, pair % 2
                        dg = wave * 2 + dgw
                        xp = pair % 2
                        b.dma("sync", x1p[xp][:],
                              x1_scr[t0 + 128 * sub:t0 + 128 * (sub + 1),
                                     512 * dg:512 * (dg + 1)], s_x1p[xp],
                              waits=[(b.ve, x1p_read[xp])] +
                              (b.w(s_x1[0], s_x1[1]) if (grp == 0 and wave == 0
                                                         and pair < 2) else []))
                        wl = [(b.pe, proj_pe), (s_out[xp], s_out[xp].n),
                              (s_x1p[xp], s_x1p[xp].n)]
                        b.op("vector", lambda e, o=x2p[xp][:], p=pw[pair][:],
                             iv=invc[:, sub:sub + 1], x1=x1p[xp][:]:
                             e.scalar_tensor_tensor(o, p, iv, x1, ALU.mult, ALU.add),
                             waits=wl, inc=b.ve)
                        evac_cnt[pair] = b.ve.n
                        x1p_read[xp] = b.ve.n
                        b.dma("scalar",
                              out[t0 + 128 * sub:t0 + 128 * (sub + 1), 512 * dg:512 * (dg + 1)],
                              x2p[xp][:], s_out[xp], waits=b.w(b.ve))

            if debug:
                s_tap = b.sem("s_tap")
                tw = b.w(s_out[0], s_out[1], s_qst[0], s_qst[1], s_ost,
                         s_x1[0], s_x1[1], s_x2T[0], s_x2T[1])
                for name, (dst, src) in taps.items():
                    b.dma("gpsimd", dst, src, s_tap, waits=tw)
                b.op("gpsimd", lambda e: e.nop(), waits=b.w(s_tap))
            b.op("gpsimd", lambda e: e.nop(), waits=b.w(s_out[0], s_out[1]))
            b.op("sync", lambda e: e.nop(), waits=b.w(s_out[0], s_out[1]))
            b.flush(block)

    return nc



def _host_consts():
    identb = np.eye(128, dtype=BFD)
    tk = np.arange(128)[:, None]
    tq = np.arange(512)[None, :]
    masks = np.concatenate(
        [np.where((tk + 128 * mi) <= tq, 0.0, -30000.0).astype(BFD)
         for mi in range(4)], axis=1)
    return identb, masks


def _oidx_for_core(c, S):
    T = S // 2
    OH = T // 2
    b_, g = c // 2, c % 2
    qpr = S // OH
    idx = np.zeros((128, 32), dtype=np.int32)
    p = np.arange(128)
    for hf in range(2):
        for f in range(16):
            rank = 2 * b_ + f // 8
            hl = f % 8
            R = rank * 512 + (hl % 4) * 128 + p
            q = (T * g + OH * hf) // OH
            idx[:, f + 16 * hf] = R * qpr + q
    return idx


def kernel(L0_attn_h, L0_attn_Wq, L0_attn_bq, L0_attn_Wk, L0_attn_bk,
           L0_attn_Wv, L0_attn_bv, L0_attn_Wo, L0_attn_bo,
           L0_ffn_w_gate_up, L0_ffn_w_proj, _debug=False):
    h = np.asarray(L0_attn_h, dtype=np.float32)
    Wq = np.asarray(L0_attn_Wq, dtype=np.float32)
    bq = np.asarray(L0_attn_bq, dtype=np.float32)
    Wk = np.asarray(L0_attn_Wk, dtype=np.float32)
    bk = np.asarray(L0_attn_bk, dtype=np.float32)
    Wv = np.asarray(L0_attn_Wv, dtype=np.float32)
    bv = np.asarray(L0_attn_bv, dtype=np.float32)
    Wo = np.asarray(L0_attn_Wo, dtype=np.float32)
    bo = np.asarray(L0_attn_bo, dtype=np.float32)
    Wgu = np.asarray(L0_ffn_w_gate_up, dtype=np.float32)
    Wp = np.asarray(L0_ffn_w_proj, dtype=np.float32)

    B, S, _ = h.shape
    T = S // 2
    nc = build_nc(S=S, debug=_debug)
    identb, masks = _host_consts()

    in_maps = []
    for c in range(8):
        b_, g = c // 2, c % 2
        in_maps.append({
            "x_full": np.ascontiguousarray(h[b_]),
            "x_res": np.ascontiguousarray(h[b_, g * T:(g + 1) * T]),
            "wq_in": np.ascontiguousarray(Wq[1024 * g:1024 * (g + 1)]),
            "wk_in": np.ascontiguousarray(Wk[1024 * g:1024 * (g + 1)]),
            "wv_in": np.ascontiguousarray(Wv[1024 * g:1024 * (g + 1)]),
            "bq_in": np.ascontiguousarray(bq[1024 * g:1024 * (g + 1)].reshape(1, -1)),
            "bk_in": np.ascontiguousarray(bk[1024 * g:1024 * (g + 1)].reshape(1, -1)),
            "bv_in": np.ascontiguousarray(bv[1024 * g:1024 * (g + 1)].reshape(1, -1)),
            "wo_in": np.ascontiguousarray(Wo),
            "bo_in": np.ascontiguousarray(bo.reshape(1, -1)),
            "wgu_in": np.ascontiguousarray(Wgu[2048 * c:2048 * (c + 1)]),
            "wp_in": np.ascontiguousarray(Wp[:, 1024 * c:1024 * (c + 1)]),
            "identb_in": identb,
            "masks_in": masks,
            "oidx_in": _oidx_for_core(c, S),
        })

    res = run_bass_kernel_spmd(nc, in_maps, core_ids=list(range(8)))
    outp = np.empty((B, S, D), dtype=np.float32)
    for c in range(8):
        b_, g = c // 2, c % 2
        outp[b_, g * T:(g + 1) * T] = res.results[c]["out"]
    if _debug:
        return outp, res
    return outp


# version 16
# speedup vs baseline: 1.0914x; 1.0009x over previous
import numpy as np
import ml_dtypes
from contextlib import ExitStack

import concourse.bass as bass
import concourse.mybir as mybir
from concourse.bass_utils import run_bass_kernel_spmd

F32 = mybir.dt.float32
BF16 = mybir.dt.bfloat16
I32 = mybir.dt.int32
AF = mybir.ActivationFunctionType
ALU = mybir.AluOpType
AX = mybir.AxisListType
BFD = ml_dtypes.bfloat16

D = 2048
HD = 128
NHL = 8
FF = 8192
GQ = 64
SM_SCALE = float(np.float32(1.0) / np.sqrt(np.float32(HD)))
LN_EPS = 1e-5
RMS_EPS = float(np.finfo(np.float32).eps)
QEPS = float(np.float32(BFD(1e-8)))
TH_C = float(np.float32(513.0 / 1024.0))

ENGS = ("tensor", "vector", "scalar", "gpsimd", "sync")


class _Ev:
    __slots__ = ("fn", "waits", "incs")

    def __init__(self, fn, waits, incs):
        self.fn, self.waits, self.incs = fn, waits, incs


class _S:
    __slots__ = ("h", "n")


class Builder:
    def __init__(self, nc, ctx, S):
        self.nc = nc
        self.ctx = ctx
        self.S = S
        self.ev = {e: [] for e in ENGS}
        self.sems = {}
        self.pe = self.sem("pe")
        self.ve = self.sem("ve")
        self.ac = self.sem("ac")
        self.po = self.sem("po")
        self.bar = self.sem("bar")
        self.cc = self.sem("cc")

    def sem(self, name):
        if name not in self.sems:
            s = _S()
            s.h = self.ctx.enter_context(self.nc.semaphore(name))
            s.n = 0
            self.sems[name] = s
        return self.sems[name]

    DRAIN_ENGS = ("vector", "scalar", "gpsimd")

    def op(self, eng, fn, waits=(), inc=None, n=1, raw_inc=False, drain=None):
        if drain is False:
            raw_inc = True
        if (inc is not None and n == 1 and not raw_inc
                and eng in self.DRAIN_ENGS):
            self.ev[eng].append(_Ev(fn, list(waits), []))
            inc.n += 1
            self.ev[eng].append(_Ev(lambda e: e.drain(), [], [(inc, 1)]))
            return inc.n
        incs = []
        if inc is not None:
            inc.n += n
            incs.append((inc, n))
        self.ev[eng].append(_Ev(fn, list(waits), incs))
        return inc.n if inc is not None else 0

    def dma(self, q, out, in_, sem, waits=(), **kw):
        self.op(q, lambda e, o=out, i=in_, kw=kw: e.dma_start(out=o, in_=i, **kw),
                waits=waits, inc=sem, n=16)
        return sem.n

    def w(self, *sems):
        return [(s, s.n) for s in sems if s.n > 0]

    def barrier(self):
        t = self.bar.n + len(ENGS)
        for e in ENGS:
            if e in self.DRAIN_ENGS:
                self.op(e, lambda eng: eng.drain(), inc=self.bar, raw_inc=True)
            else:
                self.op(e, lambda eng: eng.nop(), inc=self.bar, raw_inc=True)
        for e in ENGS:
            self.ev[e].append(_Ev(None, [(self.bar, t)], []))

    def flush(self, block):
        for ename in ENGS:
            evs = self.ev[ename]
            if not evs:
                continue
            self.ev[ename] = []

            def run(eng, evs=evs):
                for ev in evs:
                    for (s, v) in ev.waits:
                        if v > 0:
                            eng.wait_ge(s.h, v)
                    if ev.fn is not None:
                        inst = ev.fn(eng)
                        for (s, k) in ev.incs:
                            inst.then_inc(s.h, k)
            getattr(block, ename)(run)


def build_nc(S=4096, debug=False):
    nc = bass.Bass("TRN2", target_bir_lowering=False, debug=False)
    T = S // 2
    OH = T // 2
    ctx = ExitStack()

    ap = lambda t: t.ap()
    x_full = ap(nc.dram_tensor("x_full", [S, D], F32, kind="ExternalInput"))
    x_res = ap(nc.dram_tensor("x_res", [T, D], F32, kind="ExternalInput"))
    wq_in = ap(nc.dram_tensor("wq_in", [NHL * HD, D], F32, kind="ExternalInput"))
    wk_in = ap(nc.dram_tensor("wk_in", [NHL * HD, D], F32, kind="ExternalInput"))
    wv_in = ap(nc.dram_tensor("wv_in", [NHL * HD, D], F32, kind="ExternalInput"))
    bq_in = ap(nc.dram_tensor("bq_in", [1, NHL * HD], F32, kind="ExternalInput"))
    bk_in = ap(nc.dram_tensor("bk_in", [1, NHL * HD], F32, kind="ExternalInput"))
    bv_in = ap(nc.dram_tensor("bv_in", [1, NHL * HD], F32, kind="ExternalInput"))
    wo_in = ap(nc.dram_tensor("wo_in", [D, D], F32, kind="ExternalInput"))
    bo_in = ap(nc.dram_tensor("bo_in", [1, D], F32, kind="ExternalInput"))
    wgu_in = ap(nc.dram_tensor("wgu_in", [2048, 2048], F32, kind="ExternalInput"))
    wp_in = ap(nc.dram_tensor("wp_in", [2048, 1024], F32, kind="ExternalInput"))
    identb_in = ap(nc.dram_tensor("identb_in", [128, 128], BF16, kind="ExternalInput"))
    masks_in = ap(nc.dram_tensor("masks_in", [128, 4 * 512], BF16, kind="ExternalInput"))
    oidx_in = ap(nc.dram_tensor("oidx_in", [128, 32], I32, kind="ExternalInput"))

    out = ap(nc.dram_tensor("out", [T, D], F32, kind="ExternalOutput"))

    wq_bf = ap(nc.dram_tensor("wq_bf", [NHL * HD, D], BF16, kind="Internal"))
    wk_bf = ap(nc.dram_tensor("wk_bf", [NHL * HD, D], BF16, kind="Internal"))
    wv_bf = ap(nc.dram_tensor("wv_bf", [NHL * HD, D], BF16, kind="Internal"))
    wo_bf = ap(nc.dram_tensor("wo_bf", [D, D], BF16, kind="Internal"))
    qc_in = ap(nc.dram_tensor("qc_in", [3072, 2048], BF16, kind="Internal"))
    qc_out = ap(nc.dram_tensor("qc_out", [8 * 3072, 2048], BF16, kind="Internal",
                               addr_space="Shared"))
    qT_scr = ap(nc.dram_tensor("qT_scr", [NHL * HD, S], BF16, kind="Internal"))
    kT_scr = ap(nc.dram_tensor("kT_scr", [NHL * HD, S], BF16, kind="Internal"))
    v_scr = ap(nc.dram_tensor("v_scr", [S, NHL * HD], BF16, kind="Internal"))
    o_loc = ap(nc.dram_tensor("o_loc", [NHL * HD, S], BF16, kind="Internal"))
    o_all_a = ap(nc.dram_tensor("o_all_a", [8 * 512, S], BF16, kind="Internal",
                                addr_space="Shared"))
    o_all_b = ap(nc.dram_tensor("o_all_b", [8 * 512, S], BF16, kind="Internal",
                                addr_space="Shared"))
    x1_scr = ap(nc.dram_tensor("x1_scr", [T, D], F32, kind="Internal"))
    x2T_scr = ap(nc.dram_tensor("x2T_scr", [D, T], BF16, kind="Internal"))

    taps = {}
    if debug:
        taps["tap_qT"] = (ap(nc.dram_tensor("tap_qT", [NHL * HD, S], BF16, kind="ExternalOutput")), qT_scr)
        taps["tap_kT"] = (ap(nc.dram_tensor("tap_kT", [NHL * HD, S], BF16, kind="ExternalOutput")), kT_scr)
        taps["tap_v"] = (ap(nc.dram_tensor("tap_v", [S, NHL * HD], BF16, kind="ExternalOutput")), v_scr)
        taps["tap_o"] = (ap(nc.dram_tensor("tap_o", [NHL * HD, S], BF16, kind="ExternalOutput")), o_loc)
        taps["tap_x1"] = (ap(nc.dram_tensor("tap_x1", [T, D], F32, kind="ExternalOutput")), x1_scr)
        taps["tap_qc"] = (ap(nc.dram_tensor("tap_qc", [3072, 2048], BF16, kind="ExternalOutput")), qc_out[0:3072, :])
        tap_ln = ap(nc.dram_tensor("tap_ln", [128, 12], F32, kind="ExternalOutput"))
        tap_xln = ap(nc.dram_tensor("tap_xln", [128, 2048], BF16, kind="ExternalOutput"))

    with ctx:
        b = Builder(nc, ctx, S)
        sb = lambda name, shape, dt: ctx.enter_context(nc.sbuf_tensor(name, shape, dt))

        identb = sb("identb", [128, 128], BF16)
        oidx = sb("oidx", [128, 32], I32)
        ones_r = sb("ones_r", [1, 512], BF16)
        ones_c = sb("ones_c", [128, 1], BF16)
        s1v = sb("s1v", [128, 2], F32)
        sqv = sb("sqv", [128, 2], F32)
        mv = sb("mv", [128, 2], F32)
        msqv = sb("msqv", [128, 2], F32)
        varv = sb("varv", [128, 2], F32)
        rstdv = sb("rstdv", [128, 2], F32)
        cst = sb("cst", [128, 2], F32)

        s_consts = b.sem("s_consts")
        s_cast = b.sem("s_cast")
        s_xt = [b.sem(f"s_xt{i}") for i in range(2)]

        block = ctx.enter_context(nc.Block())

        b.dma("sync", identb[:], identb_in, s_consts)
        b.dma("sync", oidx[:], oidx_in, s_consts)
        b.dma("gpsimd", wq_bf, wq_in, s_cast)
        b.dma("gpsimd", wk_bf, wk_in, s_cast)
        b.dma("gpsimd", wv_bf, wv_in, s_cast)
        b.dma("gpsimd", wo_bf, wo_in, s_cast)
        b.op("vector", lambda e: e.memset(ones_r[:], 1.0), inc=b.ve)
        b.op("vector", lambda e: e.memset(cst[:, 0:1], LN_EPS), inc=b.ve)
        b.op("vector", lambda e: e.memset(cst[:, 1:2], RMS_EPS), inc=b.ve)
        b.op("vector", lambda e: e.memset(ones_c[:], 1.0), inc=b.ve)


        TC = min(S, 2048)
        NHALF = S // TC
        with ExitStack() as p1:
            psb = lambda name, shape, dt: p1.enter_context(nc.psum_tensor(name, shape, dt))
            sb1 = lambda name, shape, dt: p1.enter_context(nc.sbuf_tensor(name, shape, dt))
            xln1T = sb1("xln1T", [128, 16 * TC], BF16)
            wT = [sb1(f"wT{i}", [128, 16 * 1024], BF16) for i in range(2)]
            qstage = [sb1(f"qstage{i}", [128, 512], BF16) for i in range(4)]
            xt = [sb1(f"p1xt{i}", [128, D], F32) for i in range(2)]
            xlnt = [sb1(f"p1xlnt{i}", [128, D], BF16) for i in range(2)]
            sqscr = sb1("p1sqscr", [128, D], F32)
            bq_sb = sb1("bq_sb", [1, NHL * HD], BF16)
            bk_sb = sb1("bk_sb", [1, NHL * HD], BF16)
            bv_sb = sb1("bv_sb", [1, NHL * HD], BF16)
            b.dma("gpsimd", bq_sb[:], bq_in, s_cast)
            b.dma("gpsimd", bk_sb[:], bk_in, s_cast)
            b.dma("gpsimd", bv_sb[:], bv_in, s_cast)
            psp = [psb(f"p1_psp{i}", [128, 512], F32) for i in range(4)]
            pst1 = [psb(f"p1_pst{i}", [128, 128], BF16) for i in range(2)]
            s_wT = [b.sem(f"s_wT{i}") for i in range(2)]
            s_qst = [b.sem(f"s_qst{i}") for i in range(4)]

            pst_read = [0, 0]
            psp_read = [0, 0, 0, 0]
            sqv_read = [0, 0]
            ln_ac = [0, 0]
            xt_sq = [0, 0]
            xt_ap = [0, 0]
            xlnt_read = [0, 0]

            xln_tile_ve = [0] * 16

            def emit_ln_tile(half, i):
                par = i % 2
                c = par
                t0 = half * TC + i * 128
                wl = [(b.ve, xt_ap[par]), (b.ac, xt_sq[par])] if (half + i) else []
                b.dma("sync", xt[par][:], x_full[t0:t0 + 128, :], s_xt[par], waits=wl)
                lw = [(s_xt[par], s_xt[par].n)]
                b.op("vector", lambda e, o=s1v[:, c:c + 1], a=xt[par][:]:
                     e.tensor_reduce(o, a, AX.X, ALU.add), waits=lw, inc=b.ve)
                b.op("vector", lambda e, o=mv[:, c:c + 1], a=s1v[:, c:c + 1]:
                     e.tensor_scalar(o, a, -1.0 / D, None, ALU.mult),
                     waits=[(b.ve, b.ve.n), (b.ac, ln_ac[c])], inc=b.ve)
                mv_ve = b.ve.n
                b.op("scalar", lambda e, a=xt[par][:], m2=mv[:, c:c + 1],
                     o=sqscr[:]:
                     e.activation(o, a, AF.Square, bias=m2),
                     waits=lw + [(b.ve, mv_ve), (b.ve, max(sqv_read))],
                     inc=b.ac)
                sq_ac = b.ac.n
                xt_sq[par] = sq_ac
                b.op("vector", lambda e, o=sqv[:, c:c + 1], a=sqscr[:]:
                     e.tensor_reduce(o, a, AX.X, ALU.add),
                     waits=[(b.ac, sq_ac)], inc=b.ve)
                sqv_read[par] = b.ve.n
                b.op("scalar", lambda e, o=rstdv[:, c:c + 1], a=sqv[:, c:c + 1]:
                     e.activation(o, a, AF.Ln, scale=1.0 / D, bias=cst[:, 0:1]),
                     waits=[(b.ve, sqv_read[par])], inc=b.ac)
                b.op("scalar", lambda e, o=rstdv[:, c:c + 1], a=rstdv[:, c:c + 1]:
                     e.activation(o, a, AF.Exp, scale=-0.5),
                     waits=[(b.ac, b.ac.n)], inc=b.ac)
                ln_ac[c] = b.ac.n
                b.op("vector", lambda e, o=xlnt[par][:], a=xt[par][:],
                     m2=mv[:, c:c + 1], r2=rstdv[:, c:c + 1]:
                     e.tensor_scalar(o, a, m2, r2, ALU.add, ALU.mult),
                     waits=[(b.ac, ln_ac[c]), (b.pe, xlnt_read[par])],
                     inc=b.ve)
                apply_ve = b.ve.n
                xt_ap[par] = apply_ve
                for k in range(16):
                    tp = k % 2
                    b.op("tensor", lambda e, o=pst1[tp][:],
                         a=xlnt[par][:, 128 * k:128 * (k + 1)]:
                         e.transpose(o, a, identb[:]),
                         waits=[(b.ve, apply_ve), (b.ve, pst_read[tp]),
                                (b.ac, qpst_read[tp])],
                         inc=b.pe)
                    b.op("vector", lambda e,
                         o=xln1T[:, TC * k + 128 * i:TC * k + 128 * (i + 1)],
                         a=pst1[tp][:]: e.tensor_copy(o, a),
                         waits=b.w(b.pe), inc=b.ve, drain=False)
                    pst_read[tp] = b.ve.n
                xln_tile_ve[i] = b.ve.n
                xlnt_read[par] = b.pe.n

            def load_w(wpar, src_bf, wl):
                base = s_wT[wpar].n
                for k in range(16):
                    b.dma("sync", wT[wpar][:, 1024 * k:1024 * (k + 1)],
                          src_bf[0:1024, 128 * k:128 * (k + 1)], s_wT[wpar],
                          waits=wl if k == 0 else (), transpose=True)
                return base

            pcnt = [0]

            def emit_qk_tg(wpar, bias_sb, dst, half, tg, base, first):
                for hd in range(8):
                    pp = pcnt[0] % 4
                    pcnt[0] += 1
                    for dc in range(16):
                        wl2 = ([(b.ve, xln_tile_ve[4 * tg + 3]),
                                (b.ac, psp_read[pp])] if dc == 0 else [])
                        if first and hd == 0:
                            wl2 = wl2 + [(s_wT[wpar], base + 16 * (dc + 1))]
                        b.op("tensor", lambda e, o=psp[pp][:],
                             l=wT[wpar][:, 1024 * dc + 128 * hd:1024 * dc + 128 * (hd + 1)],
                             r=xln1T[:, TC * dc + 512 * tg:TC * dc + 512 * (tg + 1)],
                             s=(dc == 0):
                             e.matmul(o, l, r, start=s, stop=False),
                             waits=wl2)
                    b.op("tensor", lambda e, o=psp[pp][:],
                         l=bias_sb[0:1, 128 * hd:128 * (hd + 1)]:
                         e.matmul(o, l, ones_r[0:1, :], start=False, stop=True),
                         inc=b.pe)
                    b.op("scalar", lambda e, o=qstage[pp][:], a=psp[pp][:]:
                         e.activation(o, a, AF.Copy),
                         waits=b.w(b.pe) + [(s_qst[pp], s_qst[pp].n)],
                         inc=b.ac)
                    psp_read[pp] = b.ac.n
                    b.dma("scalar", dst[128 * hd:128 * (hd + 1),
                                        half * TC + 512 * tg:half * TC + 512 * (tg + 1)],
                          qstage[pp][:], s_qst[pp],
                          waits=[(b.ac, b.ac.n)])

            wb = [sb1(f"q_wb{i}", [128, 2048], BF16) for i in range(2)]
            gs = sb1("q_gs", [128, 32], F32)
            meanb = sb1("q_meanb", [128, 32], BF16)
            qs32 = sb1("q_s32", [128, 32], F32)
            qth = sb1("q_th", [128, 32], F32)
            qnth = sb1("q_nth", [128, 32], F32)
            tpos = sb1("q_tpos", [128, 2048], BF16)
            tneg = sb1("q_tneg", [128, 2048], BF16)
            wqo = sb1("q_wqo", [128, 2048], BF16)
            asm_q = sb1("q_asm", [128, 16 * 256], BF16)
            s_wb = [b.sem(f"s_wb{i}") for i in range(2)]
            s_qc = b.sem("s_qc")
            wb_read = [0, 0]
            qpst_read = [0, 0]
            qtr_pe = [0]

            def quant_tile(i, src_ap, W, asm_cols):
                par = i % 2
                b.dma("gpsimd", wb[par][:, 0:W], src_ap, s_wb[par],
                      waits=[(b.ve, wb_read[par])])
                wbt = wb[par][:, 0:W]
                wb3 = wbt.rearrange("p (g k) -> p g k", k=GQ)
                ng = W // GQ
                wl = [(s_wb[par], s_wb[par].n)]
                b.op("vector", lambda e, o=gs[:, 0:ng], i3=wb3: e.tensor_reduce(
                    o, i3, AX.X, ALU.add, apply_absolute_value=True), waits=wl,
                    inc=b.ve)
                b.op("vector", lambda e, o=meanb[:, 0:ng], a=gs[:, 0:ng]:
                     e.tensor_scalar(o, a, 1.0 / GQ, None, ALU.mult),
                     waits=[(b.ve, b.ve.n)], inc=b.ve)
                b.op("vector", lambda e, o=meanb[:, 0:ng], a=meanb[:, 0:ng]:
                     e.tensor_scalar(o, a, QEPS, None, ALU.max),
                     waits=[(b.ve, b.ve.n)], inc=b.ve)
                b.op("vector", lambda e, o=qs32[:, 0:ng], a=meanb[:, 0:ng]:
                     e.tensor_copy(o, a), waits=[(b.ve, b.ve.n)], inc=b.ve)
                b.op("vector", lambda e, o=qth[:, 0:ng], a=qs32[:, 0:ng]:
                     e.tensor_scalar(o, a, TH_C, None, ALU.mult),
                     waits=[(b.ve, b.ve.n)], inc=b.ve)
                b.op("vector", lambda e, o=qnth[:, 0:ng], a=qth[:, 0:ng]:
                     e.tensor_scalar(o, a, -1.0, None, ALU.mult),
                     waits=[(b.ve, b.ve.n)], inc=b.ve)
                thb = qth[:, 0:ng].unsqueeze(2).broadcast_to([128, ng, GQ])
                nthb = qnth[:, 0:ng].unsqueeze(2).broadcast_to([128, ng, GQ])
                s32b = qs32[:, 0:ng].unsqueeze(2).broadcast_to([128, ng, GQ])
                tp3 = tpos[:, 0:W].rearrange("p (g k) -> p g k", k=GQ)
                tn3 = tneg[:, 0:W].rearrange("p (g k) -> p g k", k=GQ)

                b.op("vector", lambda e, o=tp3, a=wb3, c=thb:
                     e.tensor_tensor(o, a, c, ALU.is_gt),
                     waits=[(b.ve, b.ve.n)])
                b.op("vector", lambda e, o=tn3, a=wb3, c=nthb:
                     e.tensor_tensor(o, a, c, ALU.is_lt))
                b.op("vector", lambda e, o=tpos[:, 0:W], a=tpos[:, 0:W], c=tneg[:, 0:W]:
                     e.tensor_tensor(o, a, c, ALU.subtract))
                b.op("vector", lambda e, o=tp3, a=tp3, c=s32b:
                     e.tensor_tensor(o, a, c, ALU.mult))
                b.op("vector", lambda e, o=tneg[:, 0:W], a=tpos[:, 0:W], c=wbt:
                     e.tensor_tensor(o, a, c, ALU.subtract))
                b.op("vector", lambda e, o=wqo[:, 0:W], a=wbt, c=tneg[:, 0:W]:
                     e.tensor_tensor(o, a, c, ALU.add),
                     waits=[(b.pe, qtr_pe[0])], inc=b.ve)
                wb_read[par] = b.ve.n
                wqo_ve = b.ve.n
                nk = W // 128
                ib = i % 2
                for k in range(nk):
                    tp = k % 2
                    b.op("tensor", lambda e, o=pst1[tp][:],
                         a=wqo[:, 128 * k:128 * (k + 1)]:
                         e.transpose(o, a, identb[:]),
                         waits=[(b.ve, wqo_ve), (b.ac, qpst_read[tp]),
                                (b.ve, pst_read[tp])],
                         inc=b.pe)
                    qtr_pe[0] = b.pe.n
                    b.op("scalar", lambda e,
                         o=asm_q[:, asm_cols * k + 128 * ib:asm_cols * k + 128 * (ib + 1)],
                         a=pst1[tp][:]: e.activation(o, a, AF.Copy),
                         waits=b.w(b.pe) + [(s_qc, s_qc.n)], inc=b.ac,
                         drain=(k == nk - 1 and ib == 1))
                    qpst_read[tp] = b.ac.n

            qu_ptr = [0]

            def emit_quant_unit():
                if qu_ptr[0] >= 16:
                    return
                bb = qu_ptr[0] % 8
                is_gu = qu_ptr[0] < 8
                qu_ptr[0] += 1
                if is_gu:
                    for ii in range(2):
                        i = 2 * bb + ii
                        quant_tile(i, wgu_in[128 * i:128 * (i + 1), :], 2048, 256)
                    b.dma("scalar",
                          qc_in[0:2048, 256 * bb:256 * (bb + 1)].rearrange(
                              "(k p) h -> p k h", p=128),
                          asm_q[:].rearrange("p (k h) -> p k h", h=256), s_qc,
                          waits=[(b.ac, b.ac.n)])
                else:
                    for ii in range(2):
                        i = 2 * bb + ii
                        quant_tile(i, wp_in[128 * i:128 * (i + 1), :], 1024, 256)
                    b.dma("scalar",
                          qc_in[2048:3072, 256 * bb:256 * (bb + 1)].rearrange(
                              "(k p) h -> p k h", p=128),
                          asm_q[:, 0:8 * 256].rearrange("p (k h) -> p k h", h=256),
                          s_qc, waits=[(b.ac, b.ac.n)])

            for half in range(NHALF):
                base_q = load_w(0, wq_bf,
                                b.w(s_cast) + (b.w(b.pe) if half else []))
                base_k = load_w(1, wk_bf, [])
                for tg in range(TC // 512):
                    if half == 0:
                        for i in range(4 * tg, 4 * tg + 4):
                            emit_ln_tile(0, i)
                    emit_qk_tg(0, bq_sb, qT_scr, half, tg, base_q, tg == 0)
                    emit_quant_unit()
                    emit_qk_tg(1, bk_sb, kT_scr, half, tg, base_k, tg == 0)
                    emit_quant_unit()
                base_v = load_w(0, wv_bf, b.w(b.pe))
                for tt in range(TC // 128):
                    for hg in range(2):
                        pp = pcnt[0] % 4
                        pcnt[0] += 1
                        for dc in range(16):
                            wl2 = ([(b.ve, xln_tile_ve[tt]),
                                    (b.ac, psp_read[pp])] if dc == 0 else [])
                            if tt == 0 and hg == 0:
                                wl2 = wl2 + [(s_wT[0], base_v + 16 * (dc + 1))]
                            b.op("tensor", lambda e, o=psp[pp][:],
                                 l=xln1T[:, TC * dc + 128 * tt:TC * dc + 128 * (tt + 1)],
                                 r=wT[0][:, 1024 * dc + 512 * hg:1024 * dc + 512 * (hg + 1)],
                                 s=(dc == 0):
                                 e.matmul(o, l, r, start=s, stop=False),
                                 waits=wl2)
                        b.op("tensor", lambda e, o=psp[pp][:],
                             r=bv_sb[0:1, 512 * hg:512 * (hg + 1)]:
                             e.matmul(o, ones_r[0:1, 0:128], r, start=False, stop=True),
                             inc=b.pe)
                        b.op("scalar", lambda e, o=qstage[pp][:], a=psp[pp][:]:
                             e.activation(o, a, AF.Copy),
                             waits=b.w(b.pe) + [(s_qst[pp], s_qst[pp].n)],
                             inc=b.ac)
                        psp_read[pp] = b.ac.n
                        b.dma("scalar", v_scr[half * TC + 128 * tt:half * TC + 128 * (tt + 1),
                                              512 * hg:512 * (hg + 1)],
                              qstage[pp][:], s_qst[pp],
                              waits=[(b.ac, b.ac.n)])
                    if half + 1 < NHALF:
                        emit_ln_tile(half + 1, tt)
                    elif NHALF == 1:
                        emit_quant_unit()

            while qu_ptr[0] < 16:
                emit_quant_unit()
            b.op("gpsimd", lambda e: e.collective_compute(
                "AllGather", ALU.bypass, replica_groups=[list(range(8))],
                ins=[qc_in.opt()], outs=[qc_out.opt()]),
                waits=b.w(s_qc), inc=b.cc, raw_inc=True)
            cc_q = b.cc.n
            b.barrier()
            b.flush(block)

        with ExitStack() as p2:
            psb = lambda name, shape, dt: p2.enter_context(nc.psum_tensor(name, shape, dt))
            sb2 = lambda name, shape, dt: p2.enter_context(nc.sbuf_tensor(name, shape, dt))
            qh = sb2("qh", [128, 2 * S], BF16)
            kh = sb2("kh", [128, 2 * S], BF16)
            vh = sb2("vh", [128, 2 * S], BF16)
            pT = [sb2(f"pT{i}", [128, 1024], BF16) for i in range(3)]
            r_full = [sb2(f"r_full{i}", [1, S], F32) for i in range(2)]
            esA = [sb2(f"esA{i}", [128, 512], F32) for i in range(2)]
            esB = [sb2(f"esB{i}", [128, 512], F32) for i in range(2)]
            esum_b = [sb2(f"esum_b{i}", [128, 1024], BF16) for i in range(2)]
            r_2 = sb2("r_2", [1, S], F32)
            r_hi = sb2("r_hi", [1, S], BF16)
            r_lo = sb2("r_lo", [1, S], BF16)
            rb_sb = [sb2(f"rb_sb{i}", [128, 512], F32) for i in range(2)]
            ostage = [sb2(f"ostage{i}", [128, S], BF16) for i in range(2)]
            masks = sb2("masks", [128, 4 * 512], BF16)
            b.dma("sync", masks[:], masks_in, s_consts)
            pss = [psb(f"pss{i}", [128, 1024], F32) for i in range(2)]
            pso = [psb(f"pso{i}", [128, 512], F32) for i in range(2)]
            psr = psb("psr", [1, 512], F32)
            psbc = psb("psbc", [128, 512], F32)
            s_head = [b.sem(f"s_head{i}") for i in range(2)]
            s_ost = b.sem("s_ost")

            NG = S // 512

            def load_head(h, slot, prefetched):
                if prefetched:
                    wl = b.w(b.pe, b.ve)
                else:
                    wl = b.w(*s_qst)
                b.dma("sync", qh[:, slot * S:(slot + 1) * S],
                      qT_scr[128 * h:128 * (h + 1), :], s_head[slot], waits=wl)
                b.dma("sync", kh[:, slot * S:(slot + 1) * S],
                      kT_scr[128 * h:128 * (h + 1), :], s_head[slot])
                b.dma("sync", vh[:, slot * S:(slot + 1) * S].rearrange("p (j h) -> p j h", h=128),
                      v_scr[0:S, 128 * h:128 * (h + 1)].rearrange("(j p) h -> p j h", p=128),
                      s_head[slot])

            horder = [4, 5, 6, 7, 0, 1, 2, 3] if NHL == 8 else list(range(NHL))
            load_head(horder[0], 0, False)
            if NHL > 1:
                load_head(horder[1], 1, False)
            lastexp = [0, 0]
            pT_free = [[], [], []]
            oevac = [0, 0]
            psr_free = 0
            last_bmm = 0
            rf_read = [0, 0]
            rf_done = [0, 0]
            psbc_evac = [0]
            cc_b = 0

            bc_ve_s = [0]

            def emit_tail_chain(h, hp):
                nonlocal last_bmm
                b.op("scalar", lambda e: e.activation(r_2[0:1, 0:S],
                                                      r_full[hp][0:1, 0:S], AF.Ln),
                     waits=[(b.ve, rf_done[hp])], inc=b.ac)
                rf_read[hp] = b.ac.n
                b.op("scalar", lambda e: e.activation(r_2[0:1, 0:S], r_2[0:1, 0:S],
                                                      AF.Exp, scale=-1.0),
                     waits=[(b.ac, b.ac.n)], inc=b.ac)
                b.op("vector", lambda e: e.tensor_copy(r_hi[0:1, 0:S], r_2[0:1, 0:S]),
                     waits=[(b.pe, last_bmm)] + b.w(b.ac), inc=b.ve, drain=False)
                b.op("vector", lambda e: e.tensor_tensor(
                    r_lo[0:1, 0:S], r_2[0:1, 0:S], r_hi[0:1, 0:S], ALU.subtract),
                    waits=[(b.ve, b.ve.n)], inc=b.ve)
                bc_ve_s[0] = b.ve.n

            def emit_tail(h, hp):
                nonlocal last_bmm, cc_b
                bc_ve = bc_ve_s[0]
                for g in range(NG):
                    bp = g % 2
                    wl = [(b.ve, bc_ve if g == 0 else psbc_evac[0])]
                    b.op("tensor", lambda e, r=r_hi[0:1, 512 * g:512 * (g + 1)]:
                         e.matmul(psbc[:], ones_r[0:1, 0:128], r, start=True, stop=False),
                         waits=wl)
                    b.op("tensor", lambda e, r=r_lo[0:1, 512 * g:512 * (g + 1)]:
                         e.matmul(psbc[:], ones_r[0:1, 0:128], r, start=False, stop=True),
                         inc=b.pe)
                    last_bmm = b.pe.n
                    b.op("vector", lambda e, o=rb_sb[bp][:]:
                         e.tensor_copy(o, psbc[:]),
                         waits=[(b.pe, last_bmm)], inc=b.ve, drain=False)
                    psbc_evac[0] = b.ve.n
                    b.op("vector", lambda e, o=ostage[hp][:, 512 * g:512 * (g + 1)],
                         b2=bp: e.tensor_tensor(o, o, rb_sb[b2][:], ALU.mult),
                         waits=[(b.ve, b.ve.n)], inc=b.ve)
                b.dma("scalar", o_loc[128 * h:128 * (h + 1), :], ostage[hp][:], s_ost,
                      waits=b.w(b.ve))
                if h == (7 if NHL == 8 else 3):
                    b.op("gpsimd", lambda e: e.collective_compute(
                        "AllGather", ALU.bypass, replica_groups=[list(range(8))],
                        ins=[o_loc[512:1024, :].opt()], outs=[o_all_b.opt()]),
                        waits=b.w(s_ost), inc=b.cc, raw_inc=True)
                    cc_b = b.cc.n

            summm = [0, 0]
            esum_done = [0, 0]
            for hi in range(NHL):
                h = horder[hi]
                slot = hi % 2
                hp = hi % 2
                ost_wait = s_ost.n
                hl = [(s_head[slot], s_head[slot].n)]
                steps = [(g, jp) for g in range(NG) for jp in range(2 * g + 2)]

                def emit_av(g, jp, pb, gate):
                    nonlocal psr_free
                    gp = g % 2
                    njp = 2 * g + 2
                    for half2 in range(2):
                        j = 2 * jp + half2
                        wl = []
                        if half2 == 0:
                            wl = [(b.ac, gate)]
                            if jp == 0:
                                wl.append((b.ve, oevac[gp]))
                        b.op("tensor", lambda e, o=pso[gp][:],
                             l=vh[:, slot * S + 128 * j:slot * S + 128 * (j + 1)],
                             r=pT[pb][:, 512 * half2:512 * (half2 + 1)],
                             s=(jp == 0 and half2 == 0),
                             t=(jp == njp - 1 and half2 == 1):
                             e.matmul(o, l, r, start=s, stop=t),
                             waits=wl, inc=(b.pe if half2 == 1 else None))
                    pT_free[pb] = [(b.pe, b.pe.n)]
                    if jp == njp - 1:
                        av_pe = b.pe.n
                        b.op("tensor", lambda e, m=esum_b[gp][:, 0:512]:
                             e.matmul(psr[:], ones_c[:], m, start=True, stop=False),
                             waits=[(b.ve, esum_done[gp]), (b.ve, psr_free)])
                        b.op("tensor", lambda e, m=esum_b[gp][:, 512:1024]:
                             e.matmul(psr[:], ones_c[:], m, start=False, stop=True),
                             inc=b.pe)
                        summm[gp] = b.pe.n
                        b.op("vector", lambda e, o=ostage[hp][:, 512 * g:512 * (g + 1)],
                             g2=gp: e.tensor_copy(o, pso[g2][:]),
                             waits=[(b.pe, av_pe)] +
                                   ([(s_ost, ost_wait)] if g == 0 else []),
                             inc=b.ve, drain=False)
                        oevac[gp] = b.ve.n
                        b.op("vector", lambda e, o=r_full[hp][0:1, 512 * g:512 * (g + 1)],
                             g2=gp: e.tensor_copy(o, psr[:]),
                             waits=[(b.pe, summm[gp]), (b.ac, rf_read[hp])],
                             inc=b.ve, drain=(g == NG - 1))
                        psr_free = b.ve.n
                        if g == NG - 1:
                            rf_done[hp] = b.ve.n

                pend = None
                esum_read = [0, 0, 0]
                for i, (g, jp) in enumerate(steps):
                    pp = i % 2
                    pb = i % 3
                    gp = g % 2
                    wl = [(b.ac, lastexp[pp])] + (hl if i == 0 else [])
                    sc_ops = []
                    for half2 in range(2):
                        j = 2 * jp + half2
                        diag = j >= 4 * g
                        sc_ops.append((
                            lambda e, o=pss[pp][:, 512 * half2:512 * (half2 + 1)],
                            l=kh[:, slot * S + 128 * j:slot * S + 128 * (j + 1)],
                            r=qh[:, slot * S + 512 * g:slot * S + 512 * (g + 1)],
                            t=(not diag):
                            e.matmul(o, l, r, start=True, stop=t),
                            wl if half2 == 0 else []))
                        if diag:
                            mi = j - 4 * g
                            sc_ops.append((
                                lambda e, o=pss[pp][:, 512 * half2:512 * (half2 + 1)],
                                m=masks[:, 512 * mi:512 * (mi + 1)]:
                                e.matmul(o, identb[:], m, start=False, stop=True),
                                b.w(s_consts) if (h == 0 and g == 0) else []))
                    for si, (fn, swl) in enumerate(sc_ops):
                        b.op("tensor", fn, waits=swl,
                             inc=(b.pe if si == len(sc_ops) - 1 else None))
                    sc_pe = b.pe.n
                    b.op("scalar", lambda e, o=pT[pb][:], a=pss[pp][:]:
                         e.activation(o, a, AF.Exp, scale=SM_SCALE),
                         waits=[(b.pe, sc_pe), (b.ve, esum_read[pb])] +
                               pT_free[pb],
                         inc=b.ac, drain=False)
                    lastexp[pp] = b.ac.n
                    gate = b.ac.n
                    if pend is not None:
                        emit_av(*pend)
                    last = (jp == 2 * g + 1)
                    if jp == 0:
                        b.op("vector", lambda e, o=esA[gp][:], a=pT[pb][:, 0:512]:
                             e.tensor_copy(o, a),
                             waits=[(b.ac, gate), (b.pe, summm[gp])],
                             inc=b.ve, drain=False)
                        b.op("vector", lambda e, o=esB[gp][:], a=pT[pb][:, 512:1024]:
                             e.tensor_copy(o, a), inc=b.ve, drain=False)
                    else:
                        oA = esum_b[gp][:, 0:512] if last else esA[gp][:]
                        oB = esum_b[gp][:, 512:1024] if last else esB[gp][:]
                        b.op("vector", lambda e, o=oA, s1=esA[gp][:],
                             a=pT[pb][:, 0:512]:
                             e.tensor_tensor(o, s1, a, ALU.add),
                             waits=[(b.ac, gate)], inc=b.ve, drain=False)
                        b.op("vector", lambda e, o=oB, s1=esB[gp][:],
                             a=pT[pb][:, 512:1024]:
                             e.tensor_tensor(o, s1, a, ALU.add),
                             inc=b.ve, drain=last)
                        if last:
                            esum_done[gp] = b.ve.n
                    esum_read[pb] = b.ve.n
                    pend = (g, jp, pb, gate)
                    if hi > 0 and i == min(2, len(steps) - 3):
                        emit_tail_chain(horder[hi - 1], (hi - 1) % 2)
                    if hi > 0 and i == min(40, len(steps) - 2):
                        emit_tail(horder[hi - 1], (hi - 1) % 2)
                emit_av(*pend)
                if hi + 2 < NHL:
                    load_head(horder[hi + 2], hi % 2, True)
            emit_tail_chain(horder[NHL - 1], (NHL - 1) % 2)
            emit_tail(horder[NHL - 1], (NHL - 1) % 2)
            b.op("gpsimd", lambda e: e.collective_compute(
                "AllGather", ALU.bypass, replica_groups=[list(range(8))],
                ins=[o_loc[0:512, :].opt()], outs=[o_all_a.opt()]),
                waits=b.w(s_ost), inc=b.cc, raw_inc=True)
            cc_a = b.cc.n
            b.barrier()
            b.flush(block)

        with ExitStack() as p3:
            psb = lambda name, shape, dt: p3.enter_context(nc.psum_tensor(name, shape, dt))
            sb3 = lambda name, shape, dt: p3.enter_context(nc.sbuf_tensor(name, shape, dt))
            og = sb3("og", [128, 16 * OH], BF16)
            woT = sb3("woT", [128, 16 * 2048], BF16)
            x1t = [sb3(f"x1t{i}", [128, D], F32) for i in range(2)]
            x2st = [sb3(f"x2st{i}", [128, D], BF16) for i in range(2)]
            xt = [sb3(f"p3xt{i}", [128, D], F32) for i in range(2)]
            xlnt = [sb3(f"p3xlnt{i}", [128, D], BF16) for i in range(2)]
            sqscr = sb3("p3sqscr", [128, D], F32)
            bo_sb = sb3("bo_sb", [1, D], BF16)
            b.dma("gpsimd", bo_sb[:], bo_in, s_cast)
            psop = [psb(f"psop{i}", [128, 512], F32) for i in range(2)]
            pst3 = [psb(f"p3_pst{i}", [128, 128], BF16) for i in range(2)]
            s_og = b.sem("s_og")
            s_woT = b.sem("s_woT")
            s_x1 = [b.sem(f"s_x1{i}") for i in range(2)]
            s_x2T = [b.sem(f"s_x2T{i}") for i in range(2)]

            o_view_a = o_all_a.rearrange("r (q t) -> (r q) t", t=OH)
            o_view_b = o_all_b.rearrange("r (q t) -> (r q) t", t=OH)
            forder = [4, 5, 6, 7, 12, 13, 14, 15, 0, 1, 2, 3, 8, 9, 10, 11] \
                if NHL == 8 else [0, 1, 2, 3, 8, 9, 10, 11, 4, 5, 6, 7, 12, 13, 14, 15]
            for ki, k in enumerate(forder):
                b.dma("sync", woT[:, 2048 * k:2048 * (k + 1)],
                      wo_bf[0:2048, 128 * k:128 * (k + 1)], s_woT,
                      waits=b.w(s_cast) if ki == 0 else (), transpose=True)

            psop_read = [0, 0]
            pst_read = [0, 0]
            lnsq_read = [0, 0]
            lnsq_ac = [0, 0]
            ln_ac3 = [0, 0]
            xt_ap3 = [0, 0]
            xlnt_read3 = [0, 0]
            for hf in range(2):
                gw = (b.w(b.pe) if hf == 1 else [])
                for fi, f in enumerate(forder):
                    col = f + 16 * hf
                    vw = o_view_a if (f % 8) < 4 else o_view_b
                    wl0 = []
                    if fi == 0:
                        wl0 = gw + b.w(s_consts) + [(b.cc, cc_b if NHL == 8 else cc_a)]
                    elif fi == 8:
                        wl0 = [(b.cc, cc_a if NHL == 8 else cc_b)]
                    b.op("gpsimd", lambda e, o=og[:, OH * f:OH * (f + 1)],
                         ix=oidx[:, col:col + 1], v2=vw:
                         e.indirect_dma_start(o, None, v2,
                                              bass.IndirectOffsetOnAxis(ix, 0)),
                         waits=wl0, inc=s_og, n=16)
                    if fi == 7:
                        og_a = s_og.n
                ogl = [(s_og, og_a)]
                og_b = s_og.n
                for sub in range(OH // 128):
                    par = sub % 2
                    c = par
                    r0 = hf * OH + sub * 128
                    b.dma("scalar", xt[par][:], x_res[r0:r0 + 128, :], s_xt[par],
                          waits=[(b.ve, xt_ap3[par])] if (hf + sub) else [])
                    for dg in range(4):
                        pp = dg % 2
                        for fi2 in range(16):
                            fc = forder[fi2]
                            wl = (ogl + [(s_woT, 16),
                                         (b.ve, psop_read[pp])]) if fi2 == 0 else \
                                ([(s_og, og_b)] if (fi2 == 8 and sub == 0 and dg == 0
                                                    ) else [])
                            if hf == 0 and sub == 0 and dg == 0 and fi2 > 0:
                                wl = wl + [(s_woT, 16 * (fi2 + 1))]
                            b.op("tensor", lambda e, o=psop[pp][:],
                                 l=og[:, OH * fc + 128 * sub:OH * fc + 128 * (sub + 1)],
                                 r=woT[:, 2048 * fc + 512 * dg:2048 * fc + 512 * (dg + 1)],
                                 s=(fi2 == 0):
                                 e.matmul(o, l, r, start=s, stop=False), waits=wl)
                        b.op("tensor", lambda e, o=psop[pp][:],
                             r=bo_sb[0:1, 512 * dg:512 * (dg + 1)]:
                             e.matmul(o, ones_r[0:1, 0:128], r, start=False, stop=True),
                             inc=b.pe)
                        wl = [(b.pe, b.pe.n), (s_xt[par], s_xt[par].n)]
                        if dg == 0:
                            wl += [(s_x1[par], s_x1[par].n), (b.ac, lnsq_ac[par])]
                        b.op("vector", lambda e, o=x1t[par][:, 512 * dg:512 * (dg + 1)],
                             a=psop[pp][:], x2=xt[par][:, 512 * dg:512 * (dg + 1)]:
                             e.tensor_tensor(o, a, x2, ALU.add), waits=wl, inc=b.ve)
                        psop_read[pp] = b.ve.n
                    x1_ve = b.ve.n
                    xt_ap3[par] = x1_ve
                    b.dma("scalar", x1_scr[r0:r0 + 128, :], x1t[par][:], s_x1[par],
                          waits=[(b.ve, x1_ve)])
                    b.op("vector", lambda e, o=s1v[:, c:c + 1], a=x1t[par][:]:
                         e.tensor_reduce(o, a, AX.X, ALU.add),
                         waits=[(b.ve, x1_ve)], inc=b.ve)
                    b.op("vector", lambda e, o=mv[:, c:c + 1], a=s1v[:, c:c + 1]:
                         e.tensor_scalar(o, a, -1.0 / D, None, ALU.mult),
                         waits=[(b.ve, b.ve.n), (b.ac, ln_ac3[c])], inc=b.ve)
                    mv_ve3 = b.ve.n
                    b.op("scalar", lambda e, a=x1t[par][:], m2=mv[:, c:c + 1]:
                         e.activation(sqscr[:], a, AF.Square, bias=m2),
                         waits=[(b.ve, mv_ve3), (b.ve, max(lnsq_read))], inc=b.ac)
                    lnsq_ac[par] = b.ac.n
                    b.op("vector", lambda e, o=sqv[:, c:c + 1]:
                         e.tensor_reduce(o, sqscr[:], AX.X, ALU.add),
                         waits=[(b.ac, b.ac.n)], inc=b.ve)
                    lnsq_read[0] = lnsq_read[1] = b.ve.n
                    b.op("scalar", lambda e, o=rstdv[:, c:c + 1], a=sqv[:, c:c + 1]:
                         e.activation(o, a, AF.Ln, scale=1.0 / D, bias=cst[:, 0:1]),
                         waits=[(b.ve, max(lnsq_read))], inc=b.ac)
                    b.op("scalar", lambda e, o=rstdv[:, c:c + 1], a=rstdv[:, c:c + 1]:
                         e.activation(o, a, AF.Exp, scale=-0.5),
                         waits=[(b.ac, b.ac.n)], inc=b.ac)
                    ln_ac3[c] = b.ac.n
                    b.op("vector", lambda e, o=xlnt[par][:], a=x1t[par][:],
                         m2=mv[:, c:c + 1], r2=rstdv[:, c:c + 1]:
                         e.tensor_scalar(o, a, m2, r2, ALU.add, ALU.mult),
                         waits=[(b.ac, ln_ac3[c]), (b.pe, xlnt_read3[par]),
                                (s_x2T[par], s_x2T[par].n)], inc=b.ve)
                    ap_ve = b.ve.n
                    for k in range(16):
                        tp = k % 2
                        b.op("tensor", lambda e, o=pst3[tp][:],
                             a=xlnt[par][:, 128 * k:128 * (k + 1)]:
                             e.transpose(o, a, identb[:]),
                             waits=[(b.ve, ap_ve), (b.ve, pst_read[tp])],
                             inc=b.pe)
                        b.op("vector", lambda e, o=x2st[par][:, 128 * k:128 * (k + 1)],
                             a=pst3[tp][:]: e.tensor_copy(o, a),
                             waits=b.w(b.pe), inc=b.ve, drain=(k == 15))
                        if k == 15:
                            xlnt_read3[par] = b.pe.n
                        pst_read[tp] = b.ve.n
                    b.dma("scalar",
                          x2T_scr[:, r0:r0 + 128].rearrange("(k p) q -> p k q", p=128),
                          x2st[par][:].rearrange("p (k q) -> p k q", q=128),
                          s_x2T[par], waits=b.w(b.ve))
            b.barrier()
            b.flush(block)

        with ExitStack() as p4:
            psb = lambda name, shape, dt: p4.enter_context(nc.psum_tensor(name, shape, dt))
            sb4 = lambda name, shape, dt: p4.enter_context(nc.sbuf_tensor(name, shape, dt))
            hT = sb4("hT", [128, 64 * 512], BF16)
            xl2 = sb4("xl2", [128, 16 * 512], BF16)
            wgu = [sb4(f"wgu{i}", [128, 16 * 1024], BF16) for i in range(3)]
            x1p = [sb4(f"x1p{i}", [128, 512], F32) for i in range(2)]
            sg = sb4("sg", [128, 4 * 512], BF16)
            hsq = [sb4(f"hsq{i}", [128, 512], BF16) for i in range(2)]
            ssA = sb4("ssA", [128, 512], F32)
            ssB = sb4("ssB", [128, 512], F32)
            ssb16 = sb4("ssb16", [128, 512], BF16)
            invr = sb4("invr", [1, 512], F32)
            inv_hi = sb4("inv_hi", [1, 512], BF16)
            inv_lo = sb4("inv_lo", [1, 512], BF16)
            invc = sb4("invc", [128, 4], F32)
            x2p = [sb4(f"x2p{i}", [128, 512], F32) for i in range(2)]
            pw = [psb(f"pw{i}", [128, 512], F32) for i in range(8)]
            s_xl2 = b.sem("s_xl2")
            s_x1p = [b.sem(f"s_x1p{i}") for i in range(2)]
            s_wgu = [b.sem(f"s_wgu{i}") for i in range(3)]
            s_out = [b.sem(f"s_out{i}") for i in range(2)]

            NGRP = T // 512
            silu_cnt = [0, 0, 0, 0]
            bank_read = [0, 0, 0, 0]
            sg_read = [0, 0, 0, 0]
            hsq_ve = [0, 0]
            wbuf_pe = [0, 0, 0]
            evac_cnt = [0] * 8
            x1p_read = [0, 0]
            ss_edge = (None, 0)
            inv55_read = 0

            def gu_window(habs):
                rb = habs // 2048
                off = habs % 2048
                return qc_out[3072 * rb:3072 * rb + 2048, off:off + 1024].rearrange(
                    "(k p) h -> p k h", p=128)

            def load_gu(k, first=False):
                buf = (k + 2) % 3
                jH = k // 2
                habs = (0 if k % 2 == 0 else FF) + 1024 * jH
                b.dma("sync", wgu[buf][:].rearrange("p (k h) -> p k h", h=1024),
                      gu_window(habs), s_wgu[buf],
                      waits=[(b.pe, wbuf_pe[buf])] +
                            ([(b.cc, cc_q)] if first else []))

            def load_xl2(grp, wl):
                t0 = grp * 512
                b.dma("gpsimd", xl2[:].rearrange("p (k t) -> p k t", t=512),
                      x2T_scr[:, t0:t0 + 512].rearrange("(k p) t -> p k t", p=128),
                      s_xl2, waits=wl)
                xl2_cnt[0] = s_xl2.n

            xl2_cnt = [0]
            load_xl2(0, b.w(s_x2T[0], s_x2T[1]))
            for grp in range(NGRP):
                t0 = grp * 512
                xl2w = [(s_xl2, xl2_cnt[0])]
                for k in ((0, 1, 2) if grp == 0 else (1, 2)):
                    load_gu(k, first=(grp == 0 and k == 0))

                for jH in range(8):
                    bg, bu = (2 * jH + 2) % 3, (2 * jH + 3) % 3
                    gl = [(s_wgu[bg], s_wgu[bg].n)]
                    ul = [(s_wgu[bu], s_wgu[bu].n)]
                    for rnd in range(2):
                        for hx4 in range(4):
                            hx = rnd * 4 + hx4
                            for dc in range(16):
                                wl2 = (gl + [(b.ve, bank_read[hx4]),
                                             (b.ve, evac_cnt[hx4])] + xl2w) \
                                    if dc == 0 else []
                                b.op("tensor", lambda e, o=pw[hx4][:], bg2=bg,
                                     l=wgu[bg][:, 1024 * dc + 128 * hx:1024 * dc + 128 * (hx + 1)],
                                     r=xl2[:, 512 * dc:512 * (dc + 1)], s=(dc == 0),
                                     t=(dc == 15):
                                     e.matmul(o, l, r, start=s, stop=t),
                                     waits=wl2, inc=(b.pe if dc == 15 else None))
                            gate_pe = b.pe.n
                            b.op("scalar", lambda e, o=sg[:, 512 * hx4:512 * (hx4 + 1)],
                                 a=pw[hx4][:]:
                                 e.activation(o, a, AF.Silu),
                                 waits=[(b.pe, gate_pe), (b.ve, sg_read[hx4])],
                                 inc=b.ac, drain=False)
                            silu_cnt[hx4] = b.ac.n
                        if rnd == 1:
                            wbuf_pe[bg] = b.pe.n
                        for hx4 in range(4):
                            hx = rnd * 4 + hx4
                            jc = 8 * jH + hx
                            for dc in range(16):
                                wl2 = (ul + [(b.ac, silu_cnt[hx4])]) if dc == 0 else []
                                b.op("tensor", lambda e, o=pw[hx4][:],
                                     l=wgu[bu][:, 1024 * dc + 128 * hx:1024 * dc + 128 * (hx + 1)],
                                     r=xl2[:, 512 * dc:512 * (dc + 1)], s=(dc == 0),
                                     t=(dc == 15):
                                     e.matmul(o, l, r, start=s, stop=t),
                                     waits=wl2, inc=(b.pe if dc == 15 else None))
                            up_pe = b.pe.n
                            b.op("vector", lambda e, o=hT[:, 512 * jc:512 * (jc + 1)],
                                 a=sg[:, 512 * hx4:512 * (hx4 + 1)], p=pw[hx4][:]:
                                 e.tensor_tensor(o, a, p, ALU.mult),
                                 waits=[(b.pe, up_pe)], inc=b.ve, drain=False)
                            bank_read[hx4] = b.ve.n
                            sg_read[hx4] = b.ve.n
                            hp = jc % 2
                            b.op("scalar", lambda e, o=hsq[hp][:],
                                 a=hT[:, 512 * jc:512 * (jc + 1)]:
                                 e.activation(o, a, AF.Square),
                                 waits=[(b.ve, b.ve.n), (b.ve, hsq_ve[hp])],
                                 inc=b.ac, drain=False)
                            sswl = [(b.ac, b.ac.n)]
                            acc = ssA if hp == 0 else ssB
                            if jc < 2:
                                b.op("vector", lambda e, o=acc[:], a=hsq[hp][:]:
                                     e.tensor_copy(o, a),
                                     waits=sswl, inc=b.ve, drain=False)
                            else:
                                b.op("vector", lambda e, o=acc[:], s1=acc[:],
                                     a=hsq[hp][:]:
                                     e.tensor_tensor(o, s1, a, ALU.add),
                                     waits=sswl, inc=b.ve, drain=False)
                            hsq_ve[hp] = b.ve.n
                            if jc == 63:
                                b.op("vector", lambda e:
                                     e.tensor_tensor(ssb16[:], ssA[:], ssB[:],
                                                     ALU.add),
                                     waits=[(b.ve, b.ve.n)], inc=b.ve)
                                ss16_ve = b.ve.n
                                mmwl = [(b.ve, ss16_ve), (b.ve, evac_cnt[4])]
                                if ss_edge[0] is not None:
                                    mmwl = mmwl + [ss_edge]
                                b.op("tensor", lambda e:
                                     e.matmul(pw[4][0:1, :], ones_c[:], ssb16[:],
                                              start=True, stop=True),
                                     waits=mmwl, inc=b.pe)
                        if rnd == 1:
                            wbuf_pe[bu] = b.pe.n
                    for k in (2 * jH + 3, 2 * jH + 4):
                        if k < 16:
                            load_gu(k)
                hT_done = b.ve.n
                gu_pe_last = b.pe.n
                if grp + 1 < NGRP:
                    load_xl2(grp + 1, [(b.pe, gu_pe_last)])

                b.op("scalar", lambda e: e.activation(invr[:], pw[4][0:1, :], AF.Ln,
                                                      scale=1.0 / FF, bias=cst[0:1, 1:2]),
                     waits=[(b.pe, gu_pe_last)], inc=b.ac)
                ss_edge = (b.ac, b.ac.n)
                b.op("scalar", lambda e: e.activation(invr[:], invr[:], AF.Exp,
                                                      scale=-0.5),
                     waits=[(b.ac, b.ac.n)], inc=b.ac)
                b.op("vector", lambda e: e.tensor_copy(inv_hi[:], invr[:]),
                     waits=b.w(b.ac) + [(b.pe, inv55_read)], inc=b.ve)
                b.op("vector", lambda e: e.tensor_tensor(inv_lo[:], invr[:], inv_hi[:],
                                                         ALU.subtract),
                     waits=[(b.ve, b.ve.n)], inc=b.ve)
                for sub in range(4):
                    b.op("tensor", lambda e, o=pw[5][:, sub:sub + 1],
                         l=inv_hi[0:1, 128 * sub:128 * (sub + 1)]:
                         e.matmul(o, l, ones_r[0:1, 0:1], start=True, stop=False),
                         waits=(b.w(b.ve) + [(b.ve, evac_cnt[5])]) if sub == 0 else ())
                    b.op("tensor", lambda e, o=pw[5][:, sub:sub + 1],
                         l=inv_lo[0:1, 128 * sub:128 * (sub + 1)]:
                         e.matmul(o, l, ones_r[0:1, 0:1], start=False, stop=True),
                         inc=b.pe)
                inv55_read = b.pe.n
                b.op("vector", lambda e: e.tensor_copy(invc[:], pw[5][:, 0:4]),
                     waits=b.w(b.pe), inc=b.ve)
                inv_ve = b.ve.n
                evac_cnt[5] = max(evac_cnt[5], inv_ve)

                wqp = [wgu[0][:, 0:8 * 1024], wgu[1][:, 0:8 * 1024]]
                for wave in range(2):
                    for rb in range(8):
                        wp_par = rb % 2
                        b.dma("sync", wqp[wp_par].rearrange("p (k d) -> p k d", d=1024),
                              qc_out[3072 * rb + 2048:3072 * rb + 3072,
                                     1024 * wave:1024 * (wave + 1)].rearrange(
                                         "(k p) d -> p k d", p=128),
                              s_wgu[wp_par], waits=[(b.pe, wbuf_pe[wp_par])])
                        for pair in range(8):
                            sub, dgw = pair // 2, pair % 2
                            for hx in range(8):
                                j = 8 * rb + hx
                                wl2 = []
                                if hx == 0:
                                    wl2 = [(s_wgu[wp_par], s_wgu[wp_par].n)]
                                    if rb == 0:
                                        wl2 += [(b.ve, hT_done), (b.ve, inv_ve),
                                                (b.ve, evac_cnt[pair])]
                                b.op("tensor", lambda e, o=pw[pair][:], w2=wp_par,
                                     l=hT[:, 512 * j + 128 * sub:512 * j + 128 * (sub + 1)],
                                     r=wqp[wp_par][:, 1024 * hx + 512 * dgw:1024 * hx + 512 * (dgw + 1)],
                                     s=(rb == 0 and hx == 0), t=(rb == 7 and hx == 7):
                                     e.matmul(o, l, r, start=s, stop=t),
                                     waits=wl2,
                                     inc=(b.pe if hx == 7 else None))
                        wbuf_pe[wp_par] = b.pe.n
                    if wave == 0 and grp + 1 < NGRP:
                        load_gu(0)
                    proj_pe = b.pe.n
                    for pair in range(8):
                        sub, dgw = pair // 2, pair % 2
                        dg = wave * 2 + dgw
                        xp = pair % 2
                        b.dma("sync", x1p[xp][:],
                              x1_scr[t0 + 128 * sub:t0 + 128 * (sub + 1),
                                     512 * dg:512 * (dg + 1)], s_x1p[xp],
                              waits=[(b.ve, x1p_read[xp])] +
                              (b.w(s_x1[0], s_x1[1]) if (grp == 0 and wave == 0
                                                         and pair < 2) else []))
                        wl = [(b.pe, proj_pe), (s_out[xp], s_out[xp].n),
                              (s_x1p[xp], s_x1p[xp].n)]
                        b.op("vector", lambda e, o=x2p[xp][:], p=pw[pair][:],
                             iv=invc[:, sub:sub + 1], x1=x1p[xp][:]:
                             e.scalar_tensor_tensor(o, p, iv, x1, ALU.mult, ALU.add),
                             waits=wl, inc=b.ve)
                        evac_cnt[pair] = b.ve.n
                        x1p_read[xp] = b.ve.n
                        b.dma("scalar",
                              out[t0 + 128 * sub:t0 + 128 * (sub + 1), 512 * dg:512 * (dg + 1)],
                              x2p[xp][:], s_out[xp], waits=b.w(b.ve))

            if debug:
                s_tap = b.sem("s_tap")
                tw = b.w(s_out[0], s_out[1], s_qst[0], s_qst[1], s_ost,
                         s_x1[0], s_x1[1], s_x2T[0], s_x2T[1])
                for name, (dst, src) in taps.items():
                    b.dma("gpsimd", dst, src, s_tap, waits=tw)
                b.op("gpsimd", lambda e: e.nop(), waits=b.w(s_tap))
            b.op("gpsimd", lambda e: e.nop(), waits=b.w(s_out[0], s_out[1]))
            b.op("sync", lambda e: e.nop(), waits=b.w(s_out[0], s_out[1]))
            b.flush(block)

    return nc



def _host_consts():
    identb = np.eye(128, dtype=BFD)
    tk = np.arange(128)[:, None]
    tq = np.arange(512)[None, :]
    masks = np.concatenate(
        [np.where((tk + 128 * mi) <= tq, 0.0, -30000.0).astype(BFD)
         for mi in range(4)], axis=1)
    return identb, masks


def _oidx_for_core(c, S):
    T = S // 2
    OH = T // 2
    b_, g = c // 2, c % 2
    qpr = S // OH
    idx = np.zeros((128, 32), dtype=np.int32)
    p = np.arange(128)
    for hf in range(2):
        for f in range(16):
            rank = 2 * b_ + f // 8
            hl = f % 8
            R = rank * 512 + (hl % 4) * 128 + p
            q = (T * g + OH * hf) // OH
            idx[:, f + 16 * hf] = R * qpr + q
    return idx


def kernel(L0_attn_h, L0_attn_Wq, L0_attn_bq, L0_attn_Wk, L0_attn_bk,
           L0_attn_Wv, L0_attn_bv, L0_attn_Wo, L0_attn_bo,
           L0_ffn_w_gate_up, L0_ffn_w_proj, _debug=False):
    h = np.asarray(L0_attn_h, dtype=np.float32)
    Wq = np.asarray(L0_attn_Wq, dtype=np.float32)
    bq = np.asarray(L0_attn_bq, dtype=np.float32)
    Wk = np.asarray(L0_attn_Wk, dtype=np.float32)
    bk = np.asarray(L0_attn_bk, dtype=np.float32)
    Wv = np.asarray(L0_attn_Wv, dtype=np.float32)
    bv = np.asarray(L0_attn_bv, dtype=np.float32)
    Wo = np.asarray(L0_attn_Wo, dtype=np.float32)
    bo = np.asarray(L0_attn_bo, dtype=np.float32)
    Wgu = np.asarray(L0_ffn_w_gate_up, dtype=np.float32)
    Wp = np.asarray(L0_ffn_w_proj, dtype=np.float32)

    B, S, _ = h.shape
    T = S // 2
    nc = build_nc(S=S, debug=_debug)
    identb, masks = _host_consts()

    in_maps = []
    for c in range(8):
        b_, g = c // 2, c % 2
        in_maps.append({
            "x_full": np.ascontiguousarray(h[b_]),
            "x_res": np.ascontiguousarray(h[b_, g * T:(g + 1) * T]),
            "wq_in": np.ascontiguousarray(Wq[1024 * g:1024 * (g + 1)]),
            "wk_in": np.ascontiguousarray(Wk[1024 * g:1024 * (g + 1)]),
            "wv_in": np.ascontiguousarray(Wv[1024 * g:1024 * (g + 1)]),
            "bq_in": np.ascontiguousarray(bq[1024 * g:1024 * (g + 1)].reshape(1, -1)),
            "bk_in": np.ascontiguousarray(bk[1024 * g:1024 * (g + 1)].reshape(1, -1)),
            "bv_in": np.ascontiguousarray(bv[1024 * g:1024 * (g + 1)].reshape(1, -1)),
            "wo_in": np.ascontiguousarray(Wo),
            "bo_in": np.ascontiguousarray(bo.reshape(1, -1)),
            "wgu_in": np.ascontiguousarray(Wgu[2048 * c:2048 * (c + 1)]),
            "wp_in": np.ascontiguousarray(Wp[:, 1024 * c:1024 * (c + 1)]),
            "identb_in": identb,
            "masks_in": masks,
            "oidx_in": _oidx_for_core(c, S),
        })

    res = run_bass_kernel_spmd(nc, in_maps, core_ids=list(range(8)))
    outp = np.empty((B, S, D), dtype=np.float32)
    for c in range(8):
        b_, g = c // 2, c % 2
        outp[b_, g * T:(g + 1) * T] = res.results[c]["out"]
    if _debug:
        return outp, res
    return outp


# version 17
# speedup vs baseline: 1.1056x; 1.0130x over previous
import numpy as np
import ml_dtypes
from contextlib import ExitStack

import concourse.bass as bass
import concourse.mybir as mybir
from concourse.bass_utils import run_bass_kernel_spmd

F32 = mybir.dt.float32
BF16 = mybir.dt.bfloat16
I32 = mybir.dt.int32
AF = mybir.ActivationFunctionType
ALU = mybir.AluOpType
AX = mybir.AxisListType
BFD = ml_dtypes.bfloat16

D = 2048
HD = 128
NHL = 8
FF = 8192
GQ = 64
SM_SCALE = float(np.float32(1.0) / np.sqrt(np.float32(HD)))
LN_EPS = 1e-5
RMS_EPS = float(np.finfo(np.float32).eps)
QEPS = float(np.float32(BFD(1e-8)))
TH_C = float(np.float32(513.0 / 1024.0))

ENGS = ("tensor", "vector", "scalar", "gpsimd", "sync")


class _Ev:
    __slots__ = ("fn", "waits", "incs")

    def __init__(self, fn, waits, incs):
        self.fn, self.waits, self.incs = fn, waits, incs


class _S:
    __slots__ = ("h", "n")


class Builder:
    def __init__(self, nc, ctx, S):
        self.nc = nc
        self.ctx = ctx
        self.S = S
        self.ev = {e: [] for e in ENGS}
        self.sems = {}
        self.pe = self.sem("pe")
        self.ve = self.sem("ve")
        self.ac = self.sem("ac")
        self.po = self.sem("po")
        self.bar = self.sem("bar")
        self.cc = self.sem("cc")

    def sem(self, name):
        if name not in self.sems:
            s = _S()
            s.h = self.ctx.enter_context(self.nc.semaphore(name))
            s.n = 0
            self.sems[name] = s
        return self.sems[name]

    DRAIN_ENGS = ("vector", "scalar", "gpsimd")

    def op(self, eng, fn, waits=(), inc=None, n=1, raw_inc=False, drain=None):
        if drain is False:
            raw_inc = True
        if (inc is not None and n == 1 and not raw_inc
                and eng in self.DRAIN_ENGS):
            self.ev[eng].append(_Ev(fn, list(waits), []))
            inc.n += 1
            self.ev[eng].append(_Ev(lambda e: e.drain(), [], [(inc, 1)]))
            return inc.n
        incs = []
        if inc is not None:
            inc.n += n
            incs.append((inc, n))
        self.ev[eng].append(_Ev(fn, list(waits), incs))
        return inc.n if inc is not None else 0

    def dma(self, q, out, in_, sem, waits=(), **kw):
        self.op(q, lambda e, o=out, i=in_, kw=kw: e.dma_start(out=o, in_=i, **kw),
                waits=waits, inc=sem, n=16)
        return sem.n

    def w(self, *sems):
        return [(s, s.n) for s in sems if s.n > 0]

    def barrier(self):
        t = self.bar.n + len(ENGS)
        for e in ENGS:
            if e in self.DRAIN_ENGS:
                self.op(e, lambda eng: eng.drain(), inc=self.bar, raw_inc=True)
            else:
                self.op(e, lambda eng: eng.nop(), inc=self.bar, raw_inc=True)
        for e in ENGS:
            self.ev[e].append(_Ev(None, [(self.bar, t)], []))

    def flush(self, block):
        for ename in ENGS:
            evs = self.ev[ename]
            if not evs:
                continue
            self.ev[ename] = []

            def run(eng, evs=evs):
                for ev in evs:
                    for (s, v) in ev.waits:
                        if v > 0:
                            eng.wait_ge(s.h, v)
                    if ev.fn is not None:
                        inst = ev.fn(eng)
                        for (s, k) in ev.incs:
                            inst.then_inc(s.h, k)
            getattr(block, ename)(run)


def build_nc(S=4096, debug=False):
    nc = bass.Bass("TRN2", target_bir_lowering=False, debug=False)
    T = S // 2
    OH = T // 2
    ctx = ExitStack()

    ap = lambda t: t.ap()
    x_full = ap(nc.dram_tensor("x_full", [S, D], F32, kind="ExternalInput"))
    x_res = ap(nc.dram_tensor("x_res", [T, D], F32, kind="ExternalInput"))
    wq_in = ap(nc.dram_tensor("wq_in", [NHL * HD, D], F32, kind="ExternalInput"))
    wk_in = ap(nc.dram_tensor("wk_in", [NHL * HD, D], F32, kind="ExternalInput"))
    wv_in = ap(nc.dram_tensor("wv_in", [NHL * HD, D], F32, kind="ExternalInput"))
    bq_in = ap(nc.dram_tensor("bq_in", [1, NHL * HD], F32, kind="ExternalInput"))
    bk_in = ap(nc.dram_tensor("bk_in", [1, NHL * HD], F32, kind="ExternalInput"))
    bv_in = ap(nc.dram_tensor("bv_in", [1, NHL * HD], F32, kind="ExternalInput"))
    wo_in = ap(nc.dram_tensor("wo_in", [D, D], F32, kind="ExternalInput"))
    bo_in = ap(nc.dram_tensor("bo_in", [1, D], F32, kind="ExternalInput"))
    wgu_in = ap(nc.dram_tensor("wgu_in", [2048, 2048], F32, kind="ExternalInput"))
    wp_in = ap(nc.dram_tensor("wp_in", [2048, 1024], F32, kind="ExternalInput"))
    identb_in = ap(nc.dram_tensor("identb_in", [128, 128], BF16, kind="ExternalInput"))
    masks_in = ap(nc.dram_tensor("masks_in", [128, 4 * 512], BF16, kind="ExternalInput"))
    oidx_in = ap(nc.dram_tensor("oidx_in", [128, 32], I32, kind="ExternalInput"))

    out = ap(nc.dram_tensor("out", [T, D], F32, kind="ExternalOutput"))

    wq_bf = ap(nc.dram_tensor("wq_bf", [NHL * HD, D], BF16, kind="Internal"))
    wk_bf = ap(nc.dram_tensor("wk_bf", [NHL * HD, D], BF16, kind="Internal"))
    wv_bf = ap(nc.dram_tensor("wv_bf", [NHL * HD, D], BF16, kind="Internal"))
    wo_bf = ap(nc.dram_tensor("wo_bf", [D, D], BF16, kind="Internal"))
    qc_in = ap(nc.dram_tensor("qc_in", [3072, 2048], BF16, kind="Internal"))
    qc_out = ap(nc.dram_tensor("qc_out", [8 * 3072, 2048], BF16, kind="Internal",
                               addr_space="Shared"))
    qT_scr = ap(nc.dram_tensor("qT_scr", [NHL * HD, S], BF16, kind="Internal"))
    kT_scr = ap(nc.dram_tensor("kT_scr", [NHL * HD, S], BF16, kind="Internal"))
    v_scr = ap(nc.dram_tensor("v_scr", [S, NHL * HD], BF16, kind="Internal"))
    o_loc = ap(nc.dram_tensor("o_loc", [NHL * HD, S], BF16, kind="Internal"))
    o_all_a = ap(nc.dram_tensor("o_all_a", [8 * 512, S], BF16, kind="Internal",
                                addr_space="Shared"))
    o_all_b = ap(nc.dram_tensor("o_all_b", [8 * 512, S], BF16, kind="Internal",
                                addr_space="Shared"))
    x1_scr = ap(nc.dram_tensor("x1_scr", [T, D], F32, kind="Internal"))
    x2T_scr = ap(nc.dram_tensor("x2T_scr", [D, T], BF16, kind="Internal"))

    taps = {}
    if debug:
        taps["tap_qT"] = (ap(nc.dram_tensor("tap_qT", [NHL * HD, S], BF16, kind="ExternalOutput")), qT_scr)
        taps["tap_kT"] = (ap(nc.dram_tensor("tap_kT", [NHL * HD, S], BF16, kind="ExternalOutput")), kT_scr)
        taps["tap_v"] = (ap(nc.dram_tensor("tap_v", [S, NHL * HD], BF16, kind="ExternalOutput")), v_scr)
        taps["tap_o"] = (ap(nc.dram_tensor("tap_o", [NHL * HD, S], BF16, kind="ExternalOutput")), o_loc)
        taps["tap_x1"] = (ap(nc.dram_tensor("tap_x1", [T, D], F32, kind="ExternalOutput")), x1_scr)
        taps["tap_qc"] = (ap(nc.dram_tensor("tap_qc", [3072, 2048], BF16, kind="ExternalOutput")), qc_out[0:3072, :])
        tap_ln = ap(nc.dram_tensor("tap_ln", [128, 12], F32, kind="ExternalOutput"))
        tap_xln = ap(nc.dram_tensor("tap_xln", [128, 2048], BF16, kind="ExternalOutput"))

    with ctx:
        b = Builder(nc, ctx, S)
        sb = lambda name, shape, dt: ctx.enter_context(nc.sbuf_tensor(name, shape, dt))

        identb = sb("identb", [128, 128], BF16)
        oidx = sb("oidx", [128, 32], I32)
        ones_r = sb("ones_r", [1, 512], BF16)
        ones_c = sb("ones_c", [128, 1], BF16)
        s1v = sb("s1v", [128, 2], F32)
        sqv = sb("sqv", [128, 2], F32)
        mv = sb("mv", [128, 2], F32)
        msqv = sb("msqv", [128, 2], F32)
        varv = sb("varv", [128, 2], F32)
        rstdv = sb("rstdv", [128, 2], F32)
        cst = sb("cst", [128, 2], F32)

        s_consts = b.sem("s_consts")
        s_cast = b.sem("s_cast")
        s_xt = [b.sem(f"s_xt{i}") for i in range(2)]

        block = ctx.enter_context(nc.Block())

        b.dma("sync", identb[:], identb_in, s_consts)
        b.dma("sync", oidx[:], oidx_in, s_consts)
        b.dma("gpsimd", wq_bf, wq_in, s_cast)
        b.dma("gpsimd", wk_bf, wk_in, s_cast)
        b.dma("gpsimd", wv_bf, wv_in, s_cast)
        b.dma("gpsimd", wo_bf, wo_in, s_cast)
        b.op("vector", lambda e: e.memset(ones_r[:], 1.0), inc=b.ve)
        b.op("vector", lambda e: e.memset(cst[:, 0:1], LN_EPS), inc=b.ve)
        b.op("vector", lambda e: e.memset(cst[:, 1:2], RMS_EPS), inc=b.ve)
        b.op("vector", lambda e: e.memset(ones_c[:], 1.0), inc=b.ve)


        TC = min(S, 2048)
        NHALF = S // TC
        with ExitStack() as p1:
            psb = lambda name, shape, dt: p1.enter_context(nc.psum_tensor(name, shape, dt))
            sb1 = lambda name, shape, dt: p1.enter_context(nc.sbuf_tensor(name, shape, dt))
            xln1T = sb1("xln1T", [128, 16 * TC], BF16)
            wT = [sb1(f"wT{i}", [128, 16 * 1024], BF16) for i in range(2)]
            qstage = [sb1(f"qstage{i}", [128, 512], BF16) for i in range(4)]
            xt = [sb1(f"p1xt{i}", [128, D], F32) for i in range(2)]
            xlnt = [sb1(f"p1xlnt{i}", [128, D], BF16) for i in range(2)]
            sqscr = sb1("p1sqscr", [128, D], F32)
            bq_sb = sb1("bq_sb", [1, NHL * HD], BF16)
            bk_sb = sb1("bk_sb", [1, NHL * HD], BF16)
            bv_sb = sb1("bv_sb", [1, NHL * HD], BF16)
            b.dma("gpsimd", bq_sb[:], bq_in, s_cast)
            b.dma("gpsimd", bk_sb[:], bk_in, s_cast)
            b.dma("gpsimd", bv_sb[:], bv_in, s_cast)
            psp = [psb(f"p1_psp{i}", [128, 512], F32) for i in range(4)]
            pst1 = [psb(f"p1_pst{i}", [128, 128], BF16) for i in range(2)]
            s_wT = [b.sem(f"s_wT{i}") for i in range(2)]
            s_qst = [b.sem(f"s_qst{i}") for i in range(4)]

            pst_read = [0, 0]
            psp_read = [0, 0, 0, 0]
            sqv_read = [0, 0]
            ln_ac = [0, 0]
            xt_sq = [0, 0]
            xt_ap = [0, 0]
            xlnt_read = [0, 0]

            xln_tile_ve = [0] * 16

            def emit_ln_tile(half, i):
                par = i % 2
                c = par
                t0 = half * TC + i * 128
                wl = [(b.ve, xt_ap[par]), (b.ac, xt_sq[par])] if (half + i) else []
                b.dma("sync", xt[par][:], x_full[t0:t0 + 128, :], s_xt[par], waits=wl)
                lw = [(s_xt[par], s_xt[par].n)]
                b.op("vector", lambda e, o=s1v[:, c:c + 1], a=xt[par][:]:
                     e.tensor_reduce(o, a, AX.X, ALU.add), waits=lw, inc=b.ve)
                b.op("vector", lambda e, o=mv[:, c:c + 1], a=s1v[:, c:c + 1]:
                     e.tensor_scalar(o, a, -1.0 / D, None, ALU.mult),
                     waits=[(b.ve, b.ve.n), (b.ac, ln_ac[c])], inc=b.ve)
                mv_ve = b.ve.n
                b.op("scalar", lambda e, a=xt[par][:], m2=mv[:, c:c + 1],
                     o=sqscr[:]:
                     e.activation(o, a, AF.Square, bias=m2),
                     waits=lw + [(b.ve, mv_ve), (b.ve, max(sqv_read))],
                     inc=b.ac)
                sq_ac = b.ac.n
                xt_sq[par] = sq_ac
                b.op("vector", lambda e, o=sqv[:, c:c + 1], a=sqscr[:]:
                     e.tensor_reduce(o, a, AX.X, ALU.add),
                     waits=[(b.ac, sq_ac)], inc=b.ve)
                sqv_read[par] = b.ve.n
                b.op("scalar", lambda e, o=rstdv[:, c:c + 1], a=sqv[:, c:c + 1]:
                     e.activation(o, a, AF.Ln, scale=1.0 / D, bias=cst[:, 0:1]),
                     waits=[(b.ve, sqv_read[par])], inc=b.ac)
                b.op("scalar", lambda e, o=rstdv[:, c:c + 1], a=rstdv[:, c:c + 1]:
                     e.activation(o, a, AF.Exp, scale=-0.5),
                     waits=[(b.ac, b.ac.n)], inc=b.ac)
                ln_ac[c] = b.ac.n
                b.op("vector", lambda e, o=xlnt[par][:], a=xt[par][:],
                     m2=mv[:, c:c + 1], r2=rstdv[:, c:c + 1]:
                     e.tensor_scalar(o, a, m2, r2, ALU.add, ALU.mult),
                     waits=[(b.ac, ln_ac[c]), (b.pe, xlnt_read[par])],
                     inc=b.ve)
                apply_ve = b.ve.n
                xt_ap[par] = apply_ve
                for k in range(16):
                    tp = k % 2
                    b.op("tensor", lambda e, o=pst1[tp][:],
                         a=xlnt[par][:, 128 * k:128 * (k + 1)]:
                         e.transpose(o, a, identb[:]),
                         waits=[(b.ve, apply_ve), (b.ve, pst_read[tp]),
                                (b.ac, qpst_read[tp])],
                         inc=b.pe)
                    b.op("vector", lambda e,
                         o=xln1T[:, TC * k + 128 * i:TC * k + 128 * (i + 1)],
                         a=pst1[tp][:]: e.tensor_copy(o, a),
                         waits=b.w(b.pe), inc=b.ve, drain=False)
                    pst_read[tp] = b.ve.n
                xln_tile_ve[i] = b.ve.n
                xlnt_read[par] = b.pe.n

            def load_w(wpar, src_bf, wl):
                base = s_wT[wpar].n
                for k in range(16):
                    b.dma("sync", wT[wpar][:, 1024 * k:1024 * (k + 1)],
                          src_bf[0:1024, 128 * k:128 * (k + 1)], s_wT[wpar],
                          waits=wl if k == 0 else (), transpose=True)
                return base

            pcnt = [0]

            def emit_qk_tg(wpar, bias_sb, dst, half, tg, base, first):
                for hd in range(8):
                    pp = pcnt[0] % 4
                    pcnt[0] += 1
                    for dc in range(16):
                        wl2 = ([(b.ve, xln_tile_ve[4 * tg + 3]),
                                (b.ac, psp_read[pp])] if dc == 0 else [])
                        if first and hd == 0:
                            wl2 = wl2 + [(s_wT[wpar], base + 16 * (dc + 1))]
                        b.op("tensor", lambda e, o=psp[pp][:],
                             l=wT[wpar][:, 1024 * dc + 128 * hd:1024 * dc + 128 * (hd + 1)],
                             r=xln1T[:, TC * dc + 512 * tg:TC * dc + 512 * (tg + 1)],
                             s=(dc == 0):
                             e.matmul(o, l, r, start=s, stop=False),
                             waits=wl2)
                    b.op("tensor", lambda e, o=psp[pp][:],
                         l=bias_sb[0:1, 128 * hd:128 * (hd + 1)]:
                         e.matmul(o, l, ones_r[0:1, :], start=False, stop=True),
                         inc=b.pe)
                    b.op("scalar", lambda e, o=qstage[pp][:], a=psp[pp][:]:
                         e.activation(o, a, AF.Copy),
                         waits=b.w(b.pe) + [(s_qst[pp], s_qst[pp].n)],
                         inc=b.ac)
                    psp_read[pp] = b.ac.n
                    b.dma("scalar", dst[128 * hd:128 * (hd + 1),
                                        half * TC + 512 * tg:half * TC + 512 * (tg + 1)],
                          qstage[pp][:], s_qst[pp],
                          waits=[(b.ac, b.ac.n)])

            wb = [sb1(f"q_wb{i}", [128, 2048], BF16) for i in range(2)]
            gs = sb1("q_gs", [128, 32], F32)
            meanb = sb1("q_meanb", [128, 32], BF16)
            qs32 = sb1("q_s32", [128, 32], F32)
            qth = sb1("q_th", [128, 32], F32)
            qnth = sb1("q_nth", [128, 32], F32)
            tpos = sb1("q_tpos", [128, 2048], BF16)
            tneg = sb1("q_tneg", [128, 2048], BF16)
            wqo = [sb1(f"q_wqo{i}", [128, 2048], BF16) for i in range(2)]
            asm_q = sb1("q_asm", [128, 16 * 256], BF16)
            s_wb = [b.sem(f"s_wb{i}") for i in range(2)]
            s_qc = b.sem("s_qc")
            wb_read = [0, 0]
            qpst_read = [0, 0]
            qtr_pe = [0, 0]
            wqo_ve_t = [0, 0]

            def quant_tile(i, src_ap, W, asm_cols):
                par = i % 2
                b.dma("gpsimd", wb[par][:, 0:W], src_ap, s_wb[par],
                      waits=[(b.ve, wb_read[par])])
                wbt = wb[par][:, 0:W]
                wb3 = wbt.rearrange("p (g k) -> p g k", k=GQ)
                ng = W // GQ
                wl = [(s_wb[par], s_wb[par].n)]
                b.op("vector", lambda e, o=gs[:, 0:ng], i3=wb3: e.tensor_reduce(
                    o, i3, AX.X, ALU.add, apply_absolute_value=True), waits=wl,
                    inc=b.ve)
                b.op("vector", lambda e, o=meanb[:, 0:ng], a=gs[:, 0:ng]:
                     e.tensor_scalar(o, a, 1.0 / GQ, None, ALU.mult),
                     waits=[(b.ve, b.ve.n)], inc=b.ve)
                b.op("vector", lambda e, o=meanb[:, 0:ng], a=meanb[:, 0:ng]:
                     e.tensor_scalar(o, a, QEPS, None, ALU.max),
                     waits=[(b.ve, b.ve.n)], inc=b.ve)
                b.op("vector", lambda e, o=qs32[:, 0:ng], a=meanb[:, 0:ng]:
                     e.tensor_copy(o, a), waits=[(b.ve, b.ve.n)], inc=b.ve)
                b.op("vector", lambda e, o=qth[:, 0:ng], a=qs32[:, 0:ng]:
                     e.tensor_scalar(o, a, TH_C, None, ALU.mult),
                     waits=[(b.ve, b.ve.n)], inc=b.ve)
                b.op("vector", lambda e, o=qnth[:, 0:ng], a=qth[:, 0:ng]:
                     e.tensor_scalar(o, a, -1.0, None, ALU.mult),
                     waits=[(b.ve, b.ve.n)], inc=b.ve)
                thb = qth[:, 0:ng].unsqueeze(2).broadcast_to([128, ng, GQ])
                nthb = qnth[:, 0:ng].unsqueeze(2).broadcast_to([128, ng, GQ])
                s32b = qs32[:, 0:ng].unsqueeze(2).broadcast_to([128, ng, GQ])
                tp3 = tpos[:, 0:W].rearrange("p (g k) -> p g k", k=GQ)
                tn3 = tneg[:, 0:W].rearrange("p (g k) -> p g k", k=GQ)

                b.op("vector", lambda e, o=tp3, a=wb3, c=thb:
                     e.tensor_tensor(o, a, c, ALU.is_gt),
                     waits=[(b.ve, b.ve.n)])
                b.op("vector", lambda e, o=tn3, a=wb3, c=nthb:
                     e.tensor_tensor(o, a, c, ALU.is_lt))
                b.op("vector", lambda e, o=tpos[:, 0:W], a=tpos[:, 0:W], c=tneg[:, 0:W]:
                     e.tensor_tensor(o, a, c, ALU.subtract))
                b.op("vector", lambda e, o=tp3, a=tp3, c=s32b:
                     e.tensor_tensor(o, a, c, ALU.mult))
                b.op("vector", lambda e, o=tneg[:, 0:W], a=tpos[:, 0:W], c=wbt:
                     e.tensor_tensor(o, a, c, ALU.subtract))
                b.op("vector", lambda e, o=wqo[par][:, 0:W], a=wbt, c=tneg[:, 0:W]:
                     e.tensor_tensor(o, a, c, ALU.add),
                     waits=[(b.pe, qtr_pe[par])], inc=b.ve)
                wb_read[par] = b.ve.n
                wqo_ve_t[par] = b.ve.n

            def quant_trans(i, W, asm_cols):
                par = i % 2
                wv = wqo_ve_t[par]
                nk = W // 128
                ib = i % 2
                for k in range(nk):
                    tp = k % 2
                    b.op("tensor", lambda e, o=pst1[tp][:],
                         a=wqo[par][:, 128 * k:128 * (k + 1)]:
                         e.transpose(o, a, identb[:]),
                         waits=[(b.ve, wv), (b.ac, qpst_read[tp]),
                                (b.ve, pst_read[tp])],
                         inc=b.pe)
                    qtr_pe[par] = b.pe.n
                    b.op("scalar", lambda e,
                         o=asm_q[:, asm_cols * k + 128 * ib:asm_cols * k + 128 * (ib + 1)],
                         a=pst1[tp][:]: e.activation(o, a, AF.Copy),
                         waits=b.w(b.pe) + [(s_qc, s_qc.n)], inc=b.ac,
                         drain=(k == nk - 1 and ib == 1))
                    qpst_read[tp] = b.ac.n

            def unit_args(u):
                bb = u % 8
                if u < 8:
                    return bb, [(2 * bb + ii, wgu_in[128 * (2 * bb + ii):
                                                    128 * (2 * bb + ii + 1), :],
                                 2048) for ii in range(2)], True
                return bb, [(2 * bb + ii, wp_in[128 * (2 * bb + ii):
                                                128 * (2 * bb + ii + 1), :],
                             1024) for ii in range(2)], False

            def emit_quant_store(u):
                bb, tiles, is_gu = unit_args(u)
                for (i, src_ap, W) in tiles:
                    quant_trans(i, W, 256)
                if is_gu:
                    b.dma("scalar",
                          qc_in[0:2048, 256 * bb:256 * (bb + 1)].rearrange(
                              "(k p) h -> p k h", p=128),
                          asm_q[:].rearrange("p (k h) -> p k h", h=256), s_qc,
                          waits=[(b.ac, b.ac.n)])
                else:
                    b.dma("scalar",
                          qc_in[2048:3072, 256 * bb:256 * (bb + 1)].rearrange(
                              "(k p) h -> p k h", p=128),
                          asm_q[:, 0:8 * 256].rearrange("p (k h) -> p k h", h=256),
                          s_qc, waits=[(b.ac, b.ac.n)])

            qu_ptr = [0]

            def emit_quant_unit():
                if qu_ptr[0] >= 17:
                    return
                u = qu_ptr[0]
                qu_ptr[0] += 1
                if u > 0:
                    emit_quant_store(u - 1)
                if u < 16:
                    _, tiles, _ = unit_args(u)
                    for (i, src_ap, W) in tiles:
                        quant_tile(i, src_ap, W, 256)

            for half in range(NHALF):
                base_q = load_w(0, wq_bf,
                                b.w(s_cast) + (b.w(b.pe) if half else []))
                base_k = load_w(1, wk_bf, [])
                for tg in range(TC // 512):
                    if half == 0:
                        for i in range(4 * tg, 4 * tg + 4):
                            emit_ln_tile(0, i)
                    emit_qk_tg(0, bq_sb, qT_scr, half, tg, base_q, tg == 0)
                    emit_quant_unit()
                    emit_qk_tg(1, bk_sb, kT_scr, half, tg, base_k, tg == 0)
                    emit_quant_unit()
                base_v = load_w(0, wv_bf, b.w(b.pe))
                for tt in range(TC // 128):
                    for hg in range(2):
                        pp = pcnt[0] % 4
                        pcnt[0] += 1
                        for dc in range(16):
                            wl2 = ([(b.ve, xln_tile_ve[tt]),
                                    (b.ac, psp_read[pp])] if dc == 0 else [])
                            if tt == 0 and hg == 0:
                                wl2 = wl2 + [(s_wT[0], base_v + 16 * (dc + 1))]
                            b.op("tensor", lambda e, o=psp[pp][:],
                                 l=xln1T[:, TC * dc + 128 * tt:TC * dc + 128 * (tt + 1)],
                                 r=wT[0][:, 1024 * dc + 512 * hg:1024 * dc + 512 * (hg + 1)],
                                 s=(dc == 0):
                                 e.matmul(o, l, r, start=s, stop=False),
                                 waits=wl2)
                        b.op("tensor", lambda e, o=psp[pp][:],
                             r=bv_sb[0:1, 512 * hg:512 * (hg + 1)]:
                             e.matmul(o, ones_r[0:1, 0:128], r, start=False, stop=True),
                             inc=b.pe)
                        b.op("scalar", lambda e, o=qstage[pp][:], a=psp[pp][:]:
                             e.activation(o, a, AF.Copy),
                             waits=b.w(b.pe) + [(s_qst[pp], s_qst[pp].n)],
                             inc=b.ac)
                        psp_read[pp] = b.ac.n
                        b.dma("scalar", v_scr[half * TC + 128 * tt:half * TC + 128 * (tt + 1),
                                              512 * hg:512 * (hg + 1)],
                              qstage[pp][:], s_qst[pp],
                              waits=[(b.ac, b.ac.n)])
                    if half + 1 < NHALF:
                        emit_ln_tile(half + 1, tt)
                    elif NHALF == 1:
                        emit_quant_unit()

            while qu_ptr[0] < 17:
                emit_quant_unit()
            b.op("gpsimd", lambda e: e.collective_compute(
                "AllGather", ALU.bypass, replica_groups=[list(range(8))],
                ins=[qc_in.opt()], outs=[qc_out.opt()]),
                waits=b.w(s_qc), inc=b.cc, raw_inc=True)
            cc_q = b.cc.n
            b.barrier()
            b.flush(block)

        with ExitStack() as p2:
            psb = lambda name, shape, dt: p2.enter_context(nc.psum_tensor(name, shape, dt))
            sb2 = lambda name, shape, dt: p2.enter_context(nc.sbuf_tensor(name, shape, dt))
            qh = sb2("qh", [128, 2 * S], BF16)
            kh = sb2("kh", [128, 2 * S], BF16)
            vh = sb2("vh", [128, 2 * S], BF16)
            pT = [sb2(f"pT{i}", [128, 1024], BF16) for i in range(3)]
            r_full = [sb2(f"r_full{i}", [1, S], F32) for i in range(2)]
            esA = [sb2(f"esA{i}", [128, 512], F32) for i in range(2)]
            esB = [sb2(f"esB{i}", [128, 512], F32) for i in range(2)]
            esum_b = [sb2(f"esum_b{i}", [128, 1024], BF16) for i in range(2)]
            r_2 = sb2("r_2", [1, S], F32)
            r_hi = sb2("r_hi", [1, S], BF16)
            r_lo = sb2("r_lo", [1, S], BF16)
            rb_sb = [sb2(f"rb_sb{i}", [128, 512], F32) for i in range(2)]
            ostage = [sb2(f"ostage{i}", [128, S], BF16) for i in range(2)]
            masks = sb2("masks", [128, 4 * 512], BF16)
            b.dma("sync", masks[:], masks_in, s_consts)
            pss = [psb(f"pss{i}", [128, 1024], F32) for i in range(2)]
            pso = [psb(f"pso{i}", [128, 512], F32) for i in range(2)]
            psr = psb("psr", [1, 512], F32)
            psbc = psb("psbc", [128, 512], F32)
            s_head = [b.sem(f"s_head{i}") for i in range(2)]
            s_ost = b.sem("s_ost")

            NG = S // 512

            def load_head(h, slot, prefetched):
                if prefetched:
                    wl = b.w(b.pe, b.ve)
                else:
                    wl = b.w(*s_qst)
                b.dma("sync", qh[:, slot * S:(slot + 1) * S],
                      qT_scr[128 * h:128 * (h + 1), :], s_head[slot], waits=wl)
                b.dma("sync", kh[:, slot * S:(slot + 1) * S],
                      kT_scr[128 * h:128 * (h + 1), :], s_head[slot])
                b.dma("sync", vh[:, slot * S:(slot + 1) * S].rearrange("p (j h) -> p j h", h=128),
                      v_scr[0:S, 128 * h:128 * (h + 1)].rearrange("(j p) h -> p j h", p=128),
                      s_head[slot])

            horder = [4, 5, 6, 7, 0, 1, 2, 3] if NHL == 8 else list(range(NHL))
            load_head(horder[0], 0, False)
            if NHL > 1:
                load_head(horder[1], 1, False)
            lastexp = [0, 0]
            pT_free = [[], [], []]
            oevac = [0, 0]
            psr_free = 0
            last_bmm = 0
            rf_read = [0, 0]
            rf_done = [0, 0]
            psbc_evac = [0]
            cc_b = 0

            bc_ve_s = [0]

            def emit_tail_chain(h, hp):
                nonlocal last_bmm
                b.op("scalar", lambda e: e.activation(r_2[0:1, 0:S],
                                                      r_full[hp][0:1, 0:S], AF.Ln),
                     waits=[(b.ve, rf_done[hp])], inc=b.ac)
                rf_read[hp] = b.ac.n
                b.op("scalar", lambda e: e.activation(r_2[0:1, 0:S], r_2[0:1, 0:S],
                                                      AF.Exp, scale=-1.0),
                     waits=[(b.ac, b.ac.n)], inc=b.ac)
                b.op("vector", lambda e: e.tensor_copy(r_hi[0:1, 0:S], r_2[0:1, 0:S]),
                     waits=[(b.pe, last_bmm)] + b.w(b.ac), inc=b.ve, drain=False)
                b.op("vector", lambda e: e.tensor_tensor(
                    r_lo[0:1, 0:S], r_2[0:1, 0:S], r_hi[0:1, 0:S], ALU.subtract),
                    waits=[(b.ve, b.ve.n)], inc=b.ve)
                bc_ve_s[0] = b.ve.n

            def emit_tail(h, hp):
                nonlocal last_bmm, cc_b
                bc_ve = bc_ve_s[0]
                for g in range(NG):
                    bp = g % 2
                    wl = [(b.ve, bc_ve if g == 0 else psbc_evac[0])]
                    b.op("tensor", lambda e, r=r_hi[0:1, 512 * g:512 * (g + 1)]:
                         e.matmul(psbc[:], ones_r[0:1, 0:128], r, start=True, stop=False),
                         waits=wl)
                    b.op("tensor", lambda e, r=r_lo[0:1, 512 * g:512 * (g + 1)]:
                         e.matmul(psbc[:], ones_r[0:1, 0:128], r, start=False, stop=True),
                         inc=b.pe)
                    last_bmm = b.pe.n
                    b.op("vector", lambda e, o=rb_sb[bp][:]:
                         e.tensor_copy(o, psbc[:]),
                         waits=[(b.pe, last_bmm)], inc=b.ve, drain=False)
                    psbc_evac[0] = b.ve.n
                    b.op("vector", lambda e, o=ostage[hp][:, 512 * g:512 * (g + 1)],
                         b2=bp: e.tensor_tensor(o, o, rb_sb[b2][:], ALU.mult),
                         waits=[(b.ve, b.ve.n)], inc=b.ve)
                b.dma("scalar", o_loc[128 * h:128 * (h + 1), :], ostage[hp][:], s_ost,
                      waits=b.w(b.ve))
                if h == (7 if NHL == 8 else 3):
                    b.op("gpsimd", lambda e: e.collective_compute(
                        "AllGather", ALU.bypass, replica_groups=[list(range(8))],
                        ins=[o_loc[512:1024, :].opt()], outs=[o_all_b.opt()]),
                        waits=b.w(s_ost), inc=b.cc, raw_inc=True)
                    cc_b = b.cc.n

            summm = [0, 0]
            esum_done = [0, 0]
            for hi in range(NHL):
                h = horder[hi]
                slot = hi % 2
                hp = hi % 2
                ost_wait = s_ost.n
                hl = [(s_head[slot], s_head[slot].n)]
                steps = [(g, jp) for g in range(NG) for jp in range(2 * g + 2)]

                def emit_av(g, jp, pb, gate):
                    nonlocal psr_free
                    gp = g % 2
                    njp = 2 * g + 2
                    for half2 in range(2):
                        j = 2 * jp + half2
                        wl = []
                        if half2 == 0:
                            wl = [(b.ac, gate)]
                            if jp == 0:
                                wl.append((b.ve, oevac[gp]))
                        b.op("tensor", lambda e, o=pso[gp][:],
                             l=vh[:, slot * S + 128 * j:slot * S + 128 * (j + 1)],
                             r=pT[pb][:, 512 * half2:512 * (half2 + 1)],
                             s=(jp == 0 and half2 == 0),
                             t=(jp == njp - 1 and half2 == 1):
                             e.matmul(o, l, r, start=s, stop=t),
                             waits=wl, inc=(b.pe if half2 == 1 else None))
                    pT_free[pb] = [(b.pe, b.pe.n)]
                    if jp == njp - 1:
                        av_pe = b.pe.n
                        b.op("tensor", lambda e, m=esum_b[gp][:, 0:512]:
                             e.matmul(psr[:], ones_c[:], m, start=True, stop=False),
                             waits=[(b.ve, esum_done[gp]), (b.ve, psr_free)])
                        b.op("tensor", lambda e, m=esum_b[gp][:, 512:1024]:
                             e.matmul(psr[:], ones_c[:], m, start=False, stop=True),
                             inc=b.pe)
                        summm[gp] = b.pe.n
                        b.op("vector", lambda e, o=ostage[hp][:, 512 * g:512 * (g + 1)],
                             g2=gp: e.tensor_copy(o, pso[g2][:]),
                             waits=[(b.pe, av_pe)] +
                                   ([(s_ost, ost_wait)] if g == 0 else []),
                             inc=b.ve, drain=False)
                        oevac[gp] = b.ve.n
                        b.op("vector", lambda e, o=r_full[hp][0:1, 512 * g:512 * (g + 1)],
                             g2=gp: e.tensor_copy(o, psr[:]),
                             waits=[(b.pe, summm[gp]), (b.ac, rf_read[hp])],
                             inc=b.ve, drain=(g == NG - 1))
                        psr_free = b.ve.n
                        if g == NG - 1:
                            rf_done[hp] = b.ve.n

                pend = None
                esum_read = [0, 0, 0]
                for i, (g, jp) in enumerate(steps):
                    pp = i % 2
                    pb = i % 3
                    gp = g % 2
                    wl = [(b.ac, lastexp[pp])] + (hl if i == 0 else [])
                    sc_ops = []
                    for half2 in range(2):
                        j = 2 * jp + half2
                        diag = j >= 4 * g
                        sc_ops.append((
                            lambda e, o=pss[pp][:, 512 * half2:512 * (half2 + 1)],
                            l=kh[:, slot * S + 128 * j:slot * S + 128 * (j + 1)],
                            r=qh[:, slot * S + 512 * g:slot * S + 512 * (g + 1)],
                            t=(not diag):
                            e.matmul(o, l, r, start=True, stop=t),
                            wl if half2 == 0 else []))
                        if diag:
                            mi = j - 4 * g
                            sc_ops.append((
                                lambda e, o=pss[pp][:, 512 * half2:512 * (half2 + 1)],
                                m=masks[:, 512 * mi:512 * (mi + 1)]:
                                e.matmul(o, identb[:], m, start=False, stop=True),
                                b.w(s_consts) if (h == 0 and g == 0) else []))
                    for si, (fn, swl) in enumerate(sc_ops):
                        b.op("tensor", fn, waits=swl,
                             inc=(b.pe if si == len(sc_ops) - 1 else None))
                    sc_pe = b.pe.n
                    b.op("scalar", lambda e, o=pT[pb][:], a=pss[pp][:]:
                         e.activation(o, a, AF.Exp, scale=SM_SCALE),
                         waits=[(b.pe, sc_pe), (b.ve, esum_read[pb])] +
                               pT_free[pb],
                         inc=b.ac, drain=False)
                    lastexp[pp] = b.ac.n
                    gate = b.ac.n
                    if pend is not None:
                        emit_av(*pend)
                    last = (jp == 2 * g + 1)
                    if jp == 0:
                        b.op("vector", lambda e, o=esA[gp][:], a=pT[pb][:, 0:512]:
                             e.tensor_copy(o, a),
                             waits=[(b.ac, gate), (b.pe, summm[gp])],
                             inc=b.ve, drain=False)
                        b.op("vector", lambda e, o=esB[gp][:], a=pT[pb][:, 512:1024]:
                             e.tensor_copy(o, a), inc=b.ve, drain=False)
                    else:
                        oA = esum_b[gp][:, 0:512] if last else esA[gp][:]
                        oB = esum_b[gp][:, 512:1024] if last else esB[gp][:]
                        b.op("vector", lambda e, o=oA, s1=esA[gp][:],
                             a=pT[pb][:, 0:512]:
                             e.tensor_tensor(o, s1, a, ALU.add),
                             waits=[(b.ac, gate)], inc=b.ve, drain=False)
                        b.op("vector", lambda e, o=oB, s1=esB[gp][:],
                             a=pT[pb][:, 512:1024]:
                             e.tensor_tensor(o, s1, a, ALU.add),
                             inc=b.ve, drain=last)
                        if last:
                            esum_done[gp] = b.ve.n
                    esum_read[pb] = b.ve.n
                    pend = (g, jp, pb, gate)
                    if hi > 0 and i == min(2, len(steps) - 3):
                        emit_tail_chain(horder[hi - 1], (hi - 1) % 2)
                    if hi > 0 and i == min(40, len(steps) - 2):
                        emit_tail(horder[hi - 1], (hi - 1) % 2)
                emit_av(*pend)
                if hi + 2 < NHL:
                    load_head(horder[hi + 2], hi % 2, True)
            emit_tail_chain(horder[NHL - 1], (NHL - 1) % 2)
            emit_tail(horder[NHL - 1], (NHL - 1) % 2)
            b.op("gpsimd", lambda e: e.collective_compute(
                "AllGather", ALU.bypass, replica_groups=[list(range(8))],
                ins=[o_loc[0:512, :].opt()], outs=[o_all_a.opt()]),
                waits=b.w(s_ost), inc=b.cc, raw_inc=True)
            cc_a = b.cc.n
            b.barrier()
            b.flush(block)

        with ExitStack() as p3:
            psb = lambda name, shape, dt: p3.enter_context(nc.psum_tensor(name, shape, dt))
            sb3 = lambda name, shape, dt: p3.enter_context(nc.sbuf_tensor(name, shape, dt))
            og = sb3("og", [128, 16 * OH], BF16)
            woT = sb3("woT", [128, 16 * 2048], BF16)
            x1t = [sb3(f"x1t{i}", [128, D], F32) for i in range(2)]
            x2st = [sb3(f"x2st{i}", [128, D], BF16) for i in range(2)]
            xt = [sb3(f"p3xt{i}", [128, D], F32) for i in range(2)]
            xlnt = [sb3(f"p3xlnt{i}", [128, D], BF16) for i in range(2)]
            sqscr = sb3("p3sqscr", [128, D], F32)
            bo_sb = sb3("bo_sb", [1, D], BF16)
            b.dma("gpsimd", bo_sb[:], bo_in, s_cast)
            psop = [psb(f"psop{i}", [128, 512], F32) for i in range(2)]
            pst3 = [psb(f"p3_pst{i}", [128, 128], BF16) for i in range(2)]
            s_og = b.sem("s_og")
            s_woT = b.sem("s_woT")
            s_x1 = [b.sem(f"s_x1{i}") for i in range(2)]
            s_x2T = [b.sem(f"s_x2T{i}") for i in range(2)]

            o_view_a = o_all_a.rearrange("r (q t) -> (r q) t", t=OH)
            o_view_b = o_all_b.rearrange("r (q t) -> (r q) t", t=OH)
            forder = [4, 5, 6, 7, 12, 13, 14, 15, 0, 1, 2, 3, 8, 9, 10, 11] \
                if NHL == 8 else [0, 1, 2, 3, 8, 9, 10, 11, 4, 5, 6, 7, 12, 13, 14, 15]
            for ki, k in enumerate(forder):
                b.dma("sync", woT[:, 2048 * k:2048 * (k + 1)],
                      wo_bf[0:2048, 128 * k:128 * (k + 1)], s_woT,
                      waits=b.w(s_cast) if ki == 0 else (), transpose=True)

            psop_read = [0, 0]
            pst_read = [0, 0]
            lnsq_read = [0, 0]
            lnsq_ac = [0, 0]
            ln_ac3 = [0, 0]
            xt_ap3 = [0, 0]
            xlnt_read3 = [0, 0]
            for hf in range(2):
                gw = (b.w(b.pe) if hf == 1 else [])
                for fi, f in enumerate(forder):
                    col = f + 16 * hf
                    vw = o_view_a if (f % 8) < 4 else o_view_b
                    wl0 = []
                    if fi == 0:
                        wl0 = gw + b.w(s_consts) + [(b.cc, cc_b if NHL == 8 else cc_a)]
                    elif fi == 8:
                        wl0 = [(b.cc, cc_a if NHL == 8 else cc_b)]
                    b.op("gpsimd", lambda e, o=og[:, OH * f:OH * (f + 1)],
                         ix=oidx[:, col:col + 1], v2=vw:
                         e.indirect_dma_start(o, None, v2,
                                              bass.IndirectOffsetOnAxis(ix, 0)),
                         waits=wl0, inc=s_og, n=16)
                    if fi == 7:
                        og_a = s_og.n
                ogl = [(s_og, og_a)]
                og_b = s_og.n
                for sub in range(OH // 128):
                    par = sub % 2
                    c = par
                    r0 = hf * OH + sub * 128
                    b.dma("scalar", xt[par][:], x_res[r0:r0 + 128, :], s_xt[par],
                          waits=[(b.ve, xt_ap3[par])] if (hf + sub) else [])
                    for dg in range(4):
                        pp = dg % 2
                        for fi2 in range(16):
                            fc = forder[fi2]
                            wl = (ogl + [(s_woT, 16),
                                         (b.ve, psop_read[pp])]) if fi2 == 0 else \
                                ([(s_og, og_b)] if (fi2 == 8 and sub == 0 and dg == 0
                                                    ) else [])
                            if hf == 0 and sub == 0 and dg == 0 and fi2 > 0:
                                wl = wl + [(s_woT, 16 * (fi2 + 1))]
                            b.op("tensor", lambda e, o=psop[pp][:],
                                 l=og[:, OH * fc + 128 * sub:OH * fc + 128 * (sub + 1)],
                                 r=woT[:, 2048 * fc + 512 * dg:2048 * fc + 512 * (dg + 1)],
                                 s=(fi2 == 0):
                                 e.matmul(o, l, r, start=s, stop=False), waits=wl)
                        b.op("tensor", lambda e, o=psop[pp][:],
                             r=bo_sb[0:1, 512 * dg:512 * (dg + 1)]:
                             e.matmul(o, ones_r[0:1, 0:128], r, start=False, stop=True),
                             inc=b.pe)
                        wl = [(b.pe, b.pe.n), (s_xt[par], s_xt[par].n)]
                        if dg == 0:
                            wl += [(s_x1[par], s_x1[par].n), (b.ac, lnsq_ac[par])]
                        b.op("vector", lambda e, o=x1t[par][:, 512 * dg:512 * (dg + 1)],
                             a=psop[pp][:], x2=xt[par][:, 512 * dg:512 * (dg + 1)]:
                             e.tensor_tensor(o, a, x2, ALU.add), waits=wl, inc=b.ve)
                        psop_read[pp] = b.ve.n
                    x1_ve = b.ve.n
                    xt_ap3[par] = x1_ve
                    b.dma("scalar", x1_scr[r0:r0 + 128, :], x1t[par][:], s_x1[par],
                          waits=[(b.ve, x1_ve)])
                    b.op("vector", lambda e, o=s1v[:, c:c + 1], a=x1t[par][:]:
                         e.tensor_reduce(o, a, AX.X, ALU.add),
                         waits=[(b.ve, x1_ve)], inc=b.ve)
                    b.op("vector", lambda e, o=mv[:, c:c + 1], a=s1v[:, c:c + 1]:
                         e.tensor_scalar(o, a, -1.0 / D, None, ALU.mult),
                         waits=[(b.ve, b.ve.n), (b.ac, ln_ac3[c])], inc=b.ve)
                    mv_ve3 = b.ve.n
                    b.op("scalar", lambda e, a=x1t[par][:], m2=mv[:, c:c + 1]:
                         e.activation(sqscr[:], a, AF.Square, bias=m2),
                         waits=[(b.ve, mv_ve3), (b.ve, max(lnsq_read))], inc=b.ac)
                    lnsq_ac[par] = b.ac.n
                    b.op("vector", lambda e, o=sqv[:, c:c + 1]:
                         e.tensor_reduce(o, sqscr[:], AX.X, ALU.add),
                         waits=[(b.ac, b.ac.n)], inc=b.ve)
                    lnsq_read[0] = lnsq_read[1] = b.ve.n
                    b.op("scalar", lambda e, o=rstdv[:, c:c + 1], a=sqv[:, c:c + 1]:
                         e.activation(o, a, AF.Ln, scale=1.0 / D, bias=cst[:, 0:1]),
                         waits=[(b.ve, max(lnsq_read))], inc=b.ac)
                    b.op("scalar", lambda e, o=rstdv[:, c:c + 1], a=rstdv[:, c:c + 1]:
                         e.activation(o, a, AF.Exp, scale=-0.5),
                         waits=[(b.ac, b.ac.n)], inc=b.ac)
                    ln_ac3[c] = b.ac.n
                    b.op("vector", lambda e, o=xlnt[par][:], a=x1t[par][:],
                         m2=mv[:, c:c + 1], r2=rstdv[:, c:c + 1]:
                         e.tensor_scalar(o, a, m2, r2, ALU.add, ALU.mult),
                         waits=[(b.ac, ln_ac3[c]), (b.pe, xlnt_read3[par]),
                                (s_x2T[par], s_x2T[par].n)], inc=b.ve)
                    ap_ve = b.ve.n
                    for k in range(16):
                        tp = k % 2
                        b.op("tensor", lambda e, o=pst3[tp][:],
                             a=xlnt[par][:, 128 * k:128 * (k + 1)]:
                             e.transpose(o, a, identb[:]),
                             waits=[(b.ve, ap_ve), (b.ve, pst_read[tp])],
                             inc=b.pe)
                        b.op("vector", lambda e, o=x2st[par][:, 128 * k:128 * (k + 1)],
                             a=pst3[tp][:]: e.tensor_copy(o, a),
                             waits=b.w(b.pe), inc=b.ve, drain=(k == 15))
                        if k == 15:
                            xlnt_read3[par] = b.pe.n
                        pst_read[tp] = b.ve.n
                    b.dma("scalar",
                          x2T_scr[:, r0:r0 + 128].rearrange("(k p) q -> p k q", p=128),
                          x2st[par][:].rearrange("p (k q) -> p k q", q=128),
                          s_x2T[par], waits=b.w(b.ve))
            b.barrier()
            b.flush(block)

        with ExitStack() as p4:
            psb = lambda name, shape, dt: p4.enter_context(nc.psum_tensor(name, shape, dt))
            sb4 = lambda name, shape, dt: p4.enter_context(nc.sbuf_tensor(name, shape, dt))
            hT = sb4("hT", [128, 64 * 512], BF16)
            xl2 = sb4("xl2", [128, 16 * 512], BF16)
            wgu = [sb4(f"wgu{i}", [128, 16 * 1024], BF16) for i in range(3)]
            x1p = [sb4(f"x1p{i}", [128, 512], F32) for i in range(2)]
            sg = sb4("sg", [128, 4 * 512], BF16)
            hsq = [sb4(f"hsq{i}", [128, 512], BF16) for i in range(2)]
            ssA = sb4("ssA", [128, 512], F32)
            ssB = sb4("ssB", [128, 512], F32)
            ssb16 = sb4("ssb16", [128, 512], BF16)
            invr = sb4("invr", [1, 512], F32)
            inv_hi = sb4("inv_hi", [1, 512], BF16)
            inv_lo = sb4("inv_lo", [1, 512], BF16)
            invc = sb4("invc", [128, 4], F32)
            x2p = [sb4(f"x2p{i}", [128, 512], F32) for i in range(2)]
            pw = [psb(f"pw{i}", [128, 512], F32) for i in range(8)]
            s_xl2 = b.sem("s_xl2")
            s_x1p = [b.sem(f"s_x1p{i}") for i in range(2)]
            s_wgu = [b.sem(f"s_wgu{i}") for i in range(3)]
            s_out = [b.sem(f"s_out{i}") for i in range(2)]

            NGRP = T // 512
            silu_cnt = [0, 0, 0, 0]
            bank_read = [0, 0, 0, 0]
            sg_read = [0, 0, 0, 0]
            hsq_ve = [0, 0]
            wbuf_pe = [0, 0, 0]
            evac_cnt = [0] * 8
            x1p_read = [0, 0]
            ss_edge = (None, 0)
            inv55_read = 0

            def gu_window(habs):
                rb = habs // 2048
                off = habs % 2048
                return qc_out[3072 * rb:3072 * rb + 2048, off:off + 1024].rearrange(
                    "(k p) h -> p k h", p=128)

            def load_gu(k, first=False):
                buf = (k + 2) % 3
                jH = k // 2
                habs = (0 if k % 2 == 0 else FF) + 1024 * jH
                b.dma("sync", wgu[buf][:].rearrange("p (k h) -> p k h", h=1024),
                      gu_window(habs), s_wgu[buf],
                      waits=[(b.pe, wbuf_pe[buf])] +
                            ([(b.cc, cc_q)] if first else []))

            def load_xl2(grp, wl):
                t0 = grp * 512
                b.dma("gpsimd", xl2[:].rearrange("p (k t) -> p k t", t=512),
                      x2T_scr[:, t0:t0 + 512].rearrange("(k p) t -> p k t", p=128),
                      s_xl2, waits=wl)
                xl2_cnt[0] = s_xl2.n

            xl2_cnt = [0]
            load_xl2(0, b.w(s_x2T[0], s_x2T[1]))
            for grp in range(NGRP):
                t0 = grp * 512
                xl2w = [(s_xl2, xl2_cnt[0])]
                for k in ((0, 1, 2) if grp == 0 else (1, 2)):
                    load_gu(k, first=(grp == 0 and k == 0))

                for jH in range(8):
                    bg, bu = (2 * jH + 2) % 3, (2 * jH + 3) % 3
                    gl = [(s_wgu[bg], s_wgu[bg].n)]
                    ul = [(s_wgu[bu], s_wgu[bu].n)]
                    for rnd in range(2):
                        for hx4 in range(4):
                            hx = rnd * 4 + hx4
                            for dc in range(16):
                                wl2 = (gl + [(b.ve, bank_read[hx4]),
                                             (b.ve, evac_cnt[hx4])] + xl2w) \
                                    if dc == 0 else []
                                b.op("tensor", lambda e, o=pw[hx4][:], bg2=bg,
                                     l=wgu[bg][:, 1024 * dc + 128 * hx:1024 * dc + 128 * (hx + 1)],
                                     r=xl2[:, 512 * dc:512 * (dc + 1)], s=(dc == 0),
                                     t=(dc == 15):
                                     e.matmul(o, l, r, start=s, stop=t),
                                     waits=wl2, inc=(b.pe if dc == 15 else None))
                            gate_pe = b.pe.n
                            b.op("scalar", lambda e, o=sg[:, 512 * hx4:512 * (hx4 + 1)],
                                 a=pw[hx4][:]:
                                 e.activation(o, a, AF.Silu),
                                 waits=[(b.pe, gate_pe), (b.ve, sg_read[hx4])],
                                 inc=b.ac, drain=False)
                            silu_cnt[hx4] = b.ac.n
                        if rnd == 1:
                            wbuf_pe[bg] = b.pe.n
                        for hx4 in range(4):
                            hx = rnd * 4 + hx4
                            jc = 8 * jH + hx
                            for dc in range(16):
                                wl2 = (ul + [(b.ac, silu_cnt[hx4])]) if dc == 0 else []
                                b.op("tensor", lambda e, o=pw[hx4][:],
                                     l=wgu[bu][:, 1024 * dc + 128 * hx:1024 * dc + 128 * (hx + 1)],
                                     r=xl2[:, 512 * dc:512 * (dc + 1)], s=(dc == 0),
                                     t=(dc == 15):
                                     e.matmul(o, l, r, start=s, stop=t),
                                     waits=wl2, inc=(b.pe if dc == 15 else None))
                            up_pe = b.pe.n
                            b.op("vector", lambda e, o=hT[:, 512 * jc:512 * (jc + 1)],
                                 a=sg[:, 512 * hx4:512 * (hx4 + 1)], p=pw[hx4][:]:
                                 e.tensor_tensor(o, a, p, ALU.mult),
                                 waits=[(b.pe, up_pe)], inc=b.ve, drain=False)
                            bank_read[hx4] = b.ve.n
                            sg_read[hx4] = b.ve.n
                            hp = jc % 2
                            b.op("scalar", lambda e, o=hsq[hp][:],
                                 a=hT[:, 512 * jc:512 * (jc + 1)]:
                                 e.activation(o, a, AF.Square),
                                 waits=[(b.ve, b.ve.n), (b.ve, hsq_ve[hp])],
                                 inc=b.ac, drain=False)
                            sswl = [(b.ac, b.ac.n)]
                            acc = ssA if hp == 0 else ssB
                            if jc < 2:
                                b.op("vector", lambda e, o=acc[:], a=hsq[hp][:]:
                                     e.tensor_copy(o, a),
                                     waits=sswl, inc=b.ve, drain=False)
                            else:
                                b.op("vector", lambda e, o=acc[:], s1=acc[:],
                                     a=hsq[hp][:]:
                                     e.tensor_tensor(o, s1, a, ALU.add),
                                     waits=sswl, inc=b.ve, drain=False)
                            hsq_ve[hp] = b.ve.n
                            if jc == 63:
                                b.op("vector", lambda e:
                                     e.tensor_tensor(ssb16[:], ssA[:], ssB[:],
                                                     ALU.add),
                                     waits=[(b.ve, b.ve.n)], inc=b.ve)
                                ss16_ve = b.ve.n
                                mmwl = [(b.ve, ss16_ve), (b.ve, evac_cnt[4])]
                                if ss_edge[0] is not None:
                                    mmwl = mmwl + [ss_edge]
                                b.op("tensor", lambda e:
                                     e.matmul(pw[4][0:1, :], ones_c[:], ssb16[:],
                                              start=True, stop=True),
                                     waits=mmwl, inc=b.pe)
                        if rnd == 1:
                            wbuf_pe[bu] = b.pe.n
                    for k in (2 * jH + 3, 2 * jH + 4):
                        if k < 16:
                            load_gu(k)
                hT_done = b.ve.n
                gu_pe_last = b.pe.n
                if grp + 1 < NGRP:
                    load_xl2(grp + 1, [(b.pe, gu_pe_last)])

                b.op("scalar", lambda e: e.activation(invr[:], pw[4][0:1, :], AF.Ln,
                                                      scale=1.0 / FF, bias=cst[0:1, 1:2]),
                     waits=[(b.pe, gu_pe_last)], inc=b.ac)
                ss_edge = (b.ac, b.ac.n)
                b.op("scalar", lambda e: e.activation(invr[:], invr[:], AF.Exp,
                                                      scale=-0.5),
                     waits=[(b.ac, b.ac.n)], inc=b.ac)
                b.op("vector", lambda e: e.tensor_copy(inv_hi[:], invr[:]),
                     waits=b.w(b.ac) + [(b.pe, inv55_read)], inc=b.ve)
                b.op("vector", lambda e: e.tensor_tensor(inv_lo[:], invr[:], inv_hi[:],
                                                         ALU.subtract),
                     waits=[(b.ve, b.ve.n)], inc=b.ve)
                for sub in range(4):
                    b.op("tensor", lambda e, o=pw[5][:, sub:sub + 1],
                         l=inv_hi[0:1, 128 * sub:128 * (sub + 1)]:
                         e.matmul(o, l, ones_r[0:1, 0:1], start=True, stop=False),
                         waits=(b.w(b.ve) + [(b.ve, evac_cnt[5])]) if sub == 0 else ())
                    b.op("tensor", lambda e, o=pw[5][:, sub:sub + 1],
                         l=inv_lo[0:1, 128 * sub:128 * (sub + 1)]:
                         e.matmul(o, l, ones_r[0:1, 0:1], start=False, stop=True),
                         inc=b.pe)
                inv55_read = b.pe.n
                b.op("vector", lambda e: e.tensor_copy(invc[:], pw[5][:, 0:4]),
                     waits=b.w(b.pe), inc=b.ve)
                inv_ve = b.ve.n
                evac_cnt[5] = max(evac_cnt[5], inv_ve)

                wqp = [wgu[0][:, 0:8 * 1024], wgu[1][:, 0:8 * 1024]]
                for wave in range(2):
                    for rb in range(8):
                        wp_par = rb % 2
                        b.dma("sync", wqp[wp_par].rearrange("p (k d) -> p k d", d=1024),
                              qc_out[3072 * rb + 2048:3072 * rb + 3072,
                                     1024 * wave:1024 * (wave + 1)].rearrange(
                                         "(k p) d -> p k d", p=128),
                              s_wgu[wp_par], waits=[(b.pe, wbuf_pe[wp_par])])
                        for pair in range(8):
                            sub, dgw = pair // 2, pair % 2
                            for hx in range(8):
                                j = 8 * rb + hx
                                wl2 = []
                                if hx == 0:
                                    wl2 = [(s_wgu[wp_par], s_wgu[wp_par].n)]
                                    if rb == 0:
                                        wl2 += [(b.ve, hT_done), (b.ve, inv_ve),
                                                (b.ve, evac_cnt[pair])]
                                b.op("tensor", lambda e, o=pw[pair][:], w2=wp_par,
                                     l=hT[:, 512 * j + 128 * sub:512 * j + 128 * (sub + 1)],
                                     r=wqp[wp_par][:, 1024 * hx + 512 * dgw:1024 * hx + 512 * (dgw + 1)],
                                     s=(rb == 0 and hx == 0), t=(rb == 7 and hx == 7):
                                     e.matmul(o, l, r, start=s, stop=t),
                                     waits=wl2,
                                     inc=(b.pe if hx == 7 else None))
                        wbuf_pe[wp_par] = b.pe.n
                    if wave == 0 and grp + 1 < NGRP:
                        load_gu(0)
                    proj_pe = b.pe.n
                    for pair in range(8):
                        sub, dgw = pair // 2, pair % 2
                        dg = wave * 2 + dgw
                        xp = pair % 2
                        b.dma("sync", x1p[xp][:],
                              x1_scr[t0 + 128 * sub:t0 + 128 * (sub + 1),
                                     512 * dg:512 * (dg + 1)], s_x1p[xp],
                              waits=[(b.ve, x1p_read[xp])] +
                              (b.w(s_x1[0], s_x1[1]) if (grp == 0 and wave == 0
                                                         and pair < 2) else []))
                        wl = [(b.pe, proj_pe), (s_out[xp], s_out[xp].n),
                              (s_x1p[xp], s_x1p[xp].n)]
                        b.op("vector", lambda e, o=x2p[xp][:], p=pw[pair][:],
                             iv=invc[:, sub:sub + 1], x1=x1p[xp][:]:
                             e.scalar_tensor_tensor(o, p, iv, x1, ALU.mult, ALU.add),
                             waits=wl, inc=b.ve)
                        evac_cnt[pair] = b.ve.n
                        x1p_read[xp] = b.ve.n
                        b.dma("scalar",
                              out[t0 + 128 * sub:t0 + 128 * (sub + 1), 512 * dg:512 * (dg + 1)],
                              x2p[xp][:], s_out[xp], waits=b.w(b.ve))

            if debug:
                s_tap = b.sem("s_tap")
                tw = b.w(s_out[0], s_out[1], s_qst[0], s_qst[1], s_ost,
                         s_x1[0], s_x1[1], s_x2T[0], s_x2T[1])
                for name, (dst, src) in taps.items():
                    b.dma("gpsimd", dst, src, s_tap, waits=tw)
                b.op("gpsimd", lambda e: e.nop(), waits=b.w(s_tap))
            b.op("gpsimd", lambda e: e.nop(), waits=b.w(s_out[0], s_out[1]))
            b.op("sync", lambda e: e.nop(), waits=b.w(s_out[0], s_out[1]))
            b.flush(block)

    return nc



def _host_consts():
    identb = np.eye(128, dtype=BFD)
    tk = np.arange(128)[:, None]
    tq = np.arange(512)[None, :]
    masks = np.concatenate(
        [np.where((tk + 128 * mi) <= tq, 0.0, -30000.0).astype(BFD)
         for mi in range(4)], axis=1)
    return identb, masks


def _oidx_for_core(c, S):
    T = S // 2
    OH = T // 2
    b_, g = c // 2, c % 2
    qpr = S // OH
    idx = np.zeros((128, 32), dtype=np.int32)
    p = np.arange(128)
    for hf in range(2):
        for f in range(16):
            rank = 2 * b_ + f // 8
            hl = f % 8
            R = rank * 512 + (hl % 4) * 128 + p
            q = (T * g + OH * hf) // OH
            idx[:, f + 16 * hf] = R * qpr + q
    return idx


def kernel(L0_attn_h, L0_attn_Wq, L0_attn_bq, L0_attn_Wk, L0_attn_bk,
           L0_attn_Wv, L0_attn_bv, L0_attn_Wo, L0_attn_bo,
           L0_ffn_w_gate_up, L0_ffn_w_proj, _debug=False):
    h = np.asarray(L0_attn_h, dtype=np.float32)
    Wq = np.asarray(L0_attn_Wq, dtype=np.float32)
    bq = np.asarray(L0_attn_bq, dtype=np.float32)
    Wk = np.asarray(L0_attn_Wk, dtype=np.float32)
    bk = np.asarray(L0_attn_bk, dtype=np.float32)
    Wv = np.asarray(L0_attn_Wv, dtype=np.float32)
    bv = np.asarray(L0_attn_bv, dtype=np.float32)
    Wo = np.asarray(L0_attn_Wo, dtype=np.float32)
    bo = np.asarray(L0_attn_bo, dtype=np.float32)
    Wgu = np.asarray(L0_ffn_w_gate_up, dtype=np.float32)
    Wp = np.asarray(L0_ffn_w_proj, dtype=np.float32)

    B, S, _ = h.shape
    T = S // 2
    nc = build_nc(S=S, debug=_debug)
    identb, masks = _host_consts()

    in_maps = []
    for c in range(8):
        b_, g = c // 2, c % 2
        in_maps.append({
            "x_full": np.ascontiguousarray(h[b_]),
            "x_res": np.ascontiguousarray(h[b_, g * T:(g + 1) * T]),
            "wq_in": np.ascontiguousarray(Wq[1024 * g:1024 * (g + 1)]),
            "wk_in": np.ascontiguousarray(Wk[1024 * g:1024 * (g + 1)]),
            "wv_in": np.ascontiguousarray(Wv[1024 * g:1024 * (g + 1)]),
            "bq_in": np.ascontiguousarray(bq[1024 * g:1024 * (g + 1)].reshape(1, -1)),
            "bk_in": np.ascontiguousarray(bk[1024 * g:1024 * (g + 1)].reshape(1, -1)),
            "bv_in": np.ascontiguousarray(bv[1024 * g:1024 * (g + 1)].reshape(1, -1)),
            "wo_in": np.ascontiguousarray(Wo),
            "bo_in": np.ascontiguousarray(bo.reshape(1, -1)),
            "wgu_in": np.ascontiguousarray(Wgu[2048 * c:2048 * (c + 1)]),
            "wp_in": np.ascontiguousarray(Wp[:, 1024 * c:1024 * (c + 1)]),
            "identb_in": identb,
            "masks_in": masks,
            "oidx_in": _oidx_for_core(c, S),
        })

    res = run_bass_kernel_spmd(nc, in_maps, core_ids=list(range(8)))
    outp = np.empty((B, S, D), dtype=np.float32)
    for c in range(8):
        b_, g = c // 2, c % 2
        outp[b_, g * T:(g + 1) * T] = res.results[c]["out"]
    if _debug:
        return outp, res
    return outp


# version 18
# speedup vs baseline: 1.1422x; 1.0331x over previous
import numpy as np
import ml_dtypes
from contextlib import ExitStack

import concourse.bass as bass
import concourse.mybir as mybir
from concourse.bass_utils import run_bass_kernel_spmd

F32 = mybir.dt.float32
BF16 = mybir.dt.bfloat16
I32 = mybir.dt.int32
AF = mybir.ActivationFunctionType
ALU = mybir.AluOpType
AX = mybir.AxisListType
BFD = ml_dtypes.bfloat16

D = 2048
HD = 128
NHL = 8
FF = 8192
GQ = 64
SM_SCALE = float(np.float32(1.0) / np.sqrt(np.float32(HD)))
LN_EPS = 1e-5
RMS_EPS = float(np.finfo(np.float32).eps)
QEPS = float(np.float32(BFD(1e-8)))
TH_C = float(np.float32(513.0 / 1024.0))

ENGS = ("tensor", "vector", "scalar", "gpsimd", "sync")


class _Ev:
    __slots__ = ("fn", "waits", "incs")

    def __init__(self, fn, waits, incs):
        self.fn, self.waits, self.incs = fn, waits, incs


class _S:
    __slots__ = ("h", "n")


class Builder:
    def __init__(self, nc, ctx, S):
        self.nc = nc
        self.ctx = ctx
        self.S = S
        self.ev = {e: [] for e in ENGS}
        self.sems = {}
        self.pe = self.sem("pe")
        self.ve = self.sem("ve")
        self.ac = self.sem("ac")
        self.po = self.sem("po")
        self.bar = self.sem("bar")
        self.cc = self.sem("cc")

    def sem(self, name):
        if name not in self.sems:
            s = _S()
            s.h = self.ctx.enter_context(self.nc.semaphore(name))
            s.n = 0
            self.sems[name] = s
        return self.sems[name]

    DRAIN_ENGS = ("vector", "scalar", "gpsimd")

    def op(self, eng, fn, waits=(), inc=None, n=1, raw_inc=False, drain=None):
        if drain is False:
            raw_inc = True
        if (inc is not None and n == 1 and not raw_inc
                and eng in self.DRAIN_ENGS):
            self.ev[eng].append(_Ev(fn, list(waits), []))
            inc.n += 1
            self.ev[eng].append(_Ev(lambda e: e.drain(), [], [(inc, 1)]))
            return inc.n
        incs = []
        if inc is not None:
            inc.n += n
            incs.append((inc, n))
        self.ev[eng].append(_Ev(fn, list(waits), incs))
        return inc.n if inc is not None else 0

    def dma(self, q, out, in_, sem, waits=(), **kw):
        self.op(q, lambda e, o=out, i=in_, kw=kw: e.dma_start(out=o, in_=i, **kw),
                waits=waits, inc=sem, n=16)
        return sem.n

    def w(self, *sems):
        return [(s, s.n) for s in sems if s.n > 0]

    def barrier(self):
        t = self.bar.n + len(ENGS)
        for e in ENGS:
            if e in self.DRAIN_ENGS:
                self.op(e, lambda eng: eng.drain(), inc=self.bar, raw_inc=True)
            else:
                self.op(e, lambda eng: eng.nop(), inc=self.bar, raw_inc=True)
        for e in ENGS:
            self.ev[e].append(_Ev(None, [(self.bar, t)], []))

    def flush(self, block):
        for ename in ENGS:
            evs = self.ev[ename]
            if not evs:
                continue
            self.ev[ename] = []

            def run(eng, evs=evs):
                for ev in evs:
                    for (s, v) in ev.waits:
                        if v > 0:
                            eng.wait_ge(s.h, v)
                    if ev.fn is not None:
                        inst = ev.fn(eng)
                        for (s, k) in ev.incs:
                            inst.then_inc(s.h, k)
            getattr(block, ename)(run)


def build_nc(S=4096, debug=False):
    nc = bass.Bass("TRN2", target_bir_lowering=False, debug=False)
    T = S // 2
    OH = T // 2
    ctx = ExitStack()

    ap = lambda t: t.ap()
    x_full = ap(nc.dram_tensor("x_full", [S, D], F32, kind="ExternalInput"))
    x_res = ap(nc.dram_tensor("x_res", [T, D], F32, kind="ExternalInput"))
    wq_in = ap(nc.dram_tensor("wq_in", [NHL * HD, D], F32, kind="ExternalInput"))
    wk_in = ap(nc.dram_tensor("wk_in", [NHL * HD, D], F32, kind="ExternalInput"))
    wv_in = ap(nc.dram_tensor("wv_in", [NHL * HD, D], F32, kind="ExternalInput"))
    bq_in = ap(nc.dram_tensor("bq_in", [1, NHL * HD], F32, kind="ExternalInput"))
    bk_in = ap(nc.dram_tensor("bk_in", [1, NHL * HD], F32, kind="ExternalInput"))
    bv_in = ap(nc.dram_tensor("bv_in", [1, NHL * HD], F32, kind="ExternalInput"))
    wo_in = ap(nc.dram_tensor("wo_in", [D, D], F32, kind="ExternalInput"))
    bo_in = ap(nc.dram_tensor("bo_in", [1, D], F32, kind="ExternalInput"))
    wgu_in = ap(nc.dram_tensor("wgu_in", [2048, 2048], F32, kind="ExternalInput"))
    wp_in = ap(nc.dram_tensor("wp_in", [2048, 1024], F32, kind="ExternalInput"))
    identb_in = ap(nc.dram_tensor("identb_in", [128, 128], BF16, kind="ExternalInput"))
    masks_in = ap(nc.dram_tensor("masks_in", [128, 4 * 512], BF16, kind="ExternalInput"))
    oidx_in = ap(nc.dram_tensor("oidx_in", [128, 32], I32, kind="ExternalInput"))

    out = ap(nc.dram_tensor("out", [T, D], F32, kind="ExternalOutput"))

    wq_bf = ap(nc.dram_tensor("wq_bf", [NHL * HD, D], BF16, kind="Internal"))
    wk_bf = ap(nc.dram_tensor("wk_bf", [NHL * HD, D], BF16, kind="Internal"))
    wv_bf = ap(nc.dram_tensor("wv_bf", [NHL * HD, D], BF16, kind="Internal"))
    wo_bf = ap(nc.dram_tensor("wo_bf", [D, D], BF16, kind="Internal"))
    qc_in = ap(nc.dram_tensor("qc_in", [3072, 2048], BF16, kind="Internal"))
    qc_out = ap(nc.dram_tensor("qc_out", [8 * 3072, 2048], BF16, kind="Internal",
                               addr_space="Shared"))
    qT_scr = ap(nc.dram_tensor("qT_scr", [NHL * HD, S], BF16, kind="Internal"))
    kT_scr = ap(nc.dram_tensor("kT_scr", [NHL * HD, S], BF16, kind="Internal"))
    v_scr = ap(nc.dram_tensor("v_scr", [S, NHL * HD], BF16, kind="Internal"))
    o_loc = ap(nc.dram_tensor("o_loc", [NHL * HD, S], BF16, kind="Internal"))
    o_all_a = ap(nc.dram_tensor("o_all_a", [8 * 512, S], BF16, kind="Internal",
                                addr_space="Shared"))
    o_all_b = ap(nc.dram_tensor("o_all_b", [8 * 512, S], BF16, kind="Internal",
                                addr_space="Shared"))
    x1_scr = ap(nc.dram_tensor("x1_scr", [T, D], F32, kind="Internal"))
    x2T_scr = ap(nc.dram_tensor("x2T_scr", [D, T], BF16, kind="Internal"))

    taps = {}
    if debug:
        taps["tap_qT"] = (ap(nc.dram_tensor("tap_qT", [NHL * HD, S], BF16, kind="ExternalOutput")), qT_scr)
        taps["tap_kT"] = (ap(nc.dram_tensor("tap_kT", [NHL * HD, S], BF16, kind="ExternalOutput")), kT_scr)
        taps["tap_v"] = (ap(nc.dram_tensor("tap_v", [S, NHL * HD], BF16, kind="ExternalOutput")), v_scr)
        taps["tap_o"] = (ap(nc.dram_tensor("tap_o", [NHL * HD, S], BF16, kind="ExternalOutput")), o_loc)
        taps["tap_x1"] = (ap(nc.dram_tensor("tap_x1", [T, D], F32, kind="ExternalOutput")), x1_scr)
        taps["tap_qc"] = (ap(nc.dram_tensor("tap_qc", [3072, 2048], BF16, kind="ExternalOutput")), qc_out[0:3072, :])
        tap_ln = ap(nc.dram_tensor("tap_ln", [128, 12], F32, kind="ExternalOutput"))
        tap_xln = ap(nc.dram_tensor("tap_xln", [128, 2048], BF16, kind="ExternalOutput"))

    with ctx:
        b = Builder(nc, ctx, S)
        sb = lambda name, shape, dt: ctx.enter_context(nc.sbuf_tensor(name, shape, dt))

        identb = sb("identb", [128, 128], BF16)
        oidx = sb("oidx", [128, 32], I32)
        ones_r = sb("ones_r", [1, 512], BF16)
        ones_c = sb("ones_c", [128, 1], BF16)
        s1v = sb("s1v", [128, 2], F32)
        sqv = sb("sqv", [128, 2], F32)
        mv = sb("mv", [128, 2], F32)
        msqv = sb("msqv", [128, 2], F32)
        varv = sb("varv", [128, 2], F32)
        rstdv = sb("rstdv", [128, 2], F32)
        cst = sb("cst", [128, 2], F32)

        s_consts = b.sem("s_consts")
        s_cast = b.sem("s_cast")
        s_xt = [b.sem(f"s_xt{i}") for i in range(2)]

        block = ctx.enter_context(nc.Block())

        b.dma("sync", identb[:], identb_in, s_consts)
        b.dma("sync", oidx[:], oidx_in, s_consts)
        b.dma("gpsimd", wq_bf, wq_in, s_cast)
        b.dma("gpsimd", wk_bf, wk_in, s_cast)
        b.dma("gpsimd", wv_bf, wv_in, s_cast)
        b.dma("gpsimd", wo_bf, wo_in, s_cast)
        b.op("vector", lambda e: e.memset(ones_r[:], 1.0), inc=b.ve)
        b.op("vector", lambda e: e.memset(cst[:, 0:1], LN_EPS), inc=b.ve)
        b.op("vector", lambda e: e.memset(cst[:, 1:2], RMS_EPS), inc=b.ve)
        b.op("vector", lambda e: e.memset(ones_c[:], 1.0), inc=b.ve)


        TC = min(S, 2048)
        NHALF = S // TC
        with ExitStack() as p1:
            psb = lambda name, shape, dt: p1.enter_context(nc.psum_tensor(name, shape, dt))
            sb1 = lambda name, shape, dt: p1.enter_context(nc.sbuf_tensor(name, shape, dt))
            xln1T = sb1("xln1T", [128, 16 * TC], BF16)
            wT = [sb1(f"wT{i}", [128, 16 * 1024], BF16) for i in range(2)]
            qstage = [sb1(f"qstage{i}", [128, 512], BF16) for i in range(4)]
            xt = [sb1(f"p1xt{i}", [128, D], F32) for i in range(2)]
            xlnt = [sb1(f"p1xlnt{i}", [128, D], BF16) for i in range(2)]
            sqscr = sb1("p1sqscr", [128, D], F32)
            bq_sb = sb1("bq_sb", [1, NHL * HD], BF16)
            bk_sb = sb1("bk_sb", [1, NHL * HD], BF16)
            bv_sb = sb1("bv_sb", [1, NHL * HD], BF16)
            b.dma("gpsimd", bq_sb[:], bq_in, s_cast)
            b.dma("gpsimd", bk_sb[:], bk_in, s_cast)
            b.dma("gpsimd", bv_sb[:], bv_in, s_cast)
            psp = [psb(f"p1_psp{i}", [128, 512], F32) for i in range(4)]
            pst1 = [psb(f"p1_pst{i}", [128, 128], BF16) for i in range(2)]
            s_wT = [b.sem(f"s_wT{i}") for i in range(2)]
            s_qst = [b.sem(f"s_qst{i}") for i in range(4)]

            pst_read = [0, 0]
            psp_read = [0, 0, 0, 0]
            sqv_read = [0, 0]
            ln_ac = [0, 0]
            xt_sq = [0, 0]
            xt_ap = [0, 0]
            xlnt_read = [0, 0]

            xln_tile_ve = [0] * 16
            ln_apply = [0] * 16

            def emit_ln_tile(half, i):
                par = i % 2
                c = par
                t0 = half * TC + i * 128
                wl = [(b.ve, xt_ap[par]), (b.ac, xt_sq[par])] if (half + i) else []
                b.dma("sync", xt[par][:], x_full[t0:t0 + 128, :], s_xt[par], waits=wl)
                lw = [(s_xt[par], s_xt[par].n)]
                b.op("vector", lambda e, o=s1v[:, c:c + 1], a=xt[par][:]:
                     e.tensor_reduce(o, a, AX.X, ALU.add), waits=lw, inc=b.ve)
                b.op("vector", lambda e, o=mv[:, c:c + 1], a=s1v[:, c:c + 1]:
                     e.tensor_scalar(o, a, -1.0 / D, None, ALU.mult),
                     waits=[(b.ve, b.ve.n), (b.ac, ln_ac[c])], inc=b.ve)
                mv_ve = b.ve.n
                b.op("scalar", lambda e, a=xt[par][:], m2=mv[:, c:c + 1],
                     o=sqscr[:]:
                     e.activation(o, a, AF.Square, bias=m2),
                     waits=lw + [(b.ve, mv_ve), (b.ve, max(sqv_read))],
                     inc=b.ac)
                sq_ac = b.ac.n
                xt_sq[par] = sq_ac
                b.op("vector", lambda e, o=sqv[:, c:c + 1], a=sqscr[:]:
                     e.tensor_reduce(o, a, AX.X, ALU.add),
                     waits=[(b.ac, sq_ac)], inc=b.ve)
                sqv_read[par] = b.ve.n
                b.op("scalar", lambda e, o=rstdv[:, c:c + 1], a=sqv[:, c:c + 1]:
                     e.activation(o, a, AF.Ln, scale=1.0 / D, bias=cst[:, 0:1]),
                     waits=[(b.ve, sqv_read[par])], inc=b.ac)
                b.op("scalar", lambda e, o=rstdv[:, c:c + 1], a=rstdv[:, c:c + 1]:
                     e.activation(o, a, AF.Exp, scale=-0.5),
                     waits=[(b.ac, b.ac.n)], inc=b.ac)
                ln_ac[c] = b.ac.n
                b.op("vector", lambda e, o=xlnt[par][:], a=xt[par][:],
                     m2=mv[:, c:c + 1], r2=rstdv[:, c:c + 1]:
                     e.tensor_scalar(o, a, m2, r2, ALU.add, ALU.mult),
                     waits=[(b.ac, ln_ac[c]), (b.pe, xlnt_read[par])],
                     inc=b.ve)
                apply_ve = b.ve.n
                xt_ap[par] = apply_ve
                ln_apply[i] = apply_ve

            def emit_ln_trans(half, i):
                par = i % 2
                apply_ve = ln_apply[i]
                for k in range(16):
                    tp = k % 2
                    b.op("tensor", lambda e, o=pst1[tp][:],
                         a=xlnt[par][:, 128 * k:128 * (k + 1)]:
                         e.transpose(o, a, identb[:]),
                         waits=[(b.ve, apply_ve), (b.ve, pst_read[tp])],
                         inc=b.pe)
                    b.op("vector", lambda e,
                         o=xln1T[:, TC * k + 128 * i:TC * k + 128 * (i + 1)],
                         a=pst1[tp][:]: e.tensor_copy(o, a),
                         waits=b.w(b.pe), inc=b.ve, drain=False)
                    pst_read[tp] = b.ve.n
                xln_tile_ve[i] = b.ve.n
                xlnt_read[par] = b.pe.n

            def load_w(wpar, src_bf, wl):
                base = s_wT[wpar].n
                for k in range(16):
                    b.dma("sync", wT[wpar][:, 1024 * k:1024 * (k + 1)],
                          src_bf[0:1024, 128 * k:128 * (k + 1)], s_wT[wpar],
                          waits=wl if k == 0 else (), transpose=True)
                return base

            pcnt = [0]

            def emit_qk_tg(wpar, bias_sb, dst, half, tg, base, first):
                for hd in range(8):
                    pp = pcnt[0] % 4
                    pcnt[0] += 1
                    for dc in range(16):
                        wl2 = ([(b.ve, xln_tile_ve[4 * tg + 3]),
                                (b.ac, psp_read[pp])] if dc == 0 else [])
                        if first and hd == 0:
                            wl2 = wl2 + [(s_wT[wpar], base + 16 * (dc + 1))]
                        b.op("tensor", lambda e, o=psp[pp][:],
                             l=wT[wpar][:, 1024 * dc + 128 * hd:1024 * dc + 128 * (hd + 1)],
                             r=xln1T[:, TC * dc + 512 * tg:TC * dc + 512 * (tg + 1)],
                             s=(dc == 0):
                             e.matmul(o, l, r, start=s, stop=False),
                             waits=wl2)
                    b.op("tensor", lambda e, o=psp[pp][:],
                         l=bias_sb[0:1, 128 * hd:128 * (hd + 1)]:
                         e.matmul(o, l, ones_r[0:1, :], start=False, stop=True),
                         inc=b.pe)
                    b.op("scalar", lambda e, o=qstage[pp][:], a=psp[pp][:]:
                         e.activation(o, a, AF.Copy),
                         waits=b.w(b.pe) + [(s_qst[pp], s_qst[pp].n)],
                         inc=b.ac)
                    psp_read[pp] = b.ac.n
                    b.dma("scalar", dst[128 * hd:128 * (hd + 1),
                                        half * TC + 512 * tg:half * TC + 512 * (tg + 1)],
                          qstage[pp][:], s_qst[pp],
                          waits=[(b.ac, b.ac.n)])

            for half in range(NHALF):
                base_q = load_w(0, wq_bf,
                                b.w(s_cast) + (b.w(b.pe) if half else []))
                base_k = load_w(1, wk_bf, [])
                ntg = TC // 512
                for tg in range(ntg):
                    if half == 0:
                        if tg == 0:
                            emit_ln_tile(0, 0)
                            emit_ln_tile(0, 1)
                            emit_ln_trans(0, 0)
                            emit_ln_trans(0, 1)
                            emit_ln_tile(0, 2)
                            emit_ln_tile(0, 3)
                            emit_ln_trans(0, 2)
                            emit_ln_trans(0, 3)
                        if tg + 1 < ntg:
                            emit_ln_tile(0, 4 * tg + 4)
                            emit_ln_tile(0, 4 * tg + 5)
                    emit_qk_tg(0, bq_sb, qT_scr, half, tg, base_q, tg == 0)
                    if half == 0 and tg + 1 < ntg:
                        emit_ln_trans(0, 4 * tg + 4)
                        emit_ln_trans(0, 4 * tg + 5)
                        emit_ln_tile(0, 4 * tg + 6)
                        emit_ln_tile(0, 4 * tg + 7)
                    emit_qk_tg(1, bk_sb, kT_scr, half, tg, base_k, tg == 0)
                    if half == 0 and tg + 1 < ntg:
                        emit_ln_trans(0, 4 * tg + 6)
                        emit_ln_trans(0, 4 * tg + 7)
                base_v = load_w(0, wv_bf, b.w(b.pe))
                for tt in range(TC // 128):
                    if half + 1 < NHALF:
                        emit_ln_tile(half + 1, tt)
                    for hg in range(2):
                        pp = pcnt[0] % 4
                        pcnt[0] += 1
                        for dc in range(16):
                            wl2 = ([(b.ve, xln_tile_ve[tt]),
                                    (b.ac, psp_read[pp])] if dc == 0 else [])
                            if tt == 0 and hg == 0:
                                wl2 = wl2 + [(s_wT[0], base_v + 16 * (dc + 1))]
                            b.op("tensor", lambda e, o=psp[pp][:],
                                 l=xln1T[:, TC * dc + 128 * tt:TC * dc + 128 * (tt + 1)],
                                 r=wT[0][:, 1024 * dc + 512 * hg:1024 * dc + 512 * (hg + 1)],
                                 s=(dc == 0):
                                 e.matmul(o, l, r, start=s, stop=False),
                                 waits=wl2)
                        b.op("tensor", lambda e, o=psp[pp][:],
                             r=bv_sb[0:1, 512 * hg:512 * (hg + 1)]:
                             e.matmul(o, ones_r[0:1, 0:128], r, start=False, stop=True),
                             inc=b.pe)
                        b.op("scalar", lambda e, o=qstage[pp][:], a=psp[pp][:]:
                             e.activation(o, a, AF.Copy),
                             waits=b.w(b.pe) + [(s_qst[pp], s_qst[pp].n)],
                             inc=b.ac)
                        psp_read[pp] = b.ac.n
                        b.dma("scalar", v_scr[half * TC + 128 * tt:half * TC + 128 * (tt + 1),
                                              512 * hg:512 * (hg + 1)],
                              qstage[pp][:], s_qst[pp],
                              waits=[(b.ac, b.ac.n)])
                    if half + 1 < NHALF and tt > 0:
                        emit_ln_trans(half + 1, tt - 1)
                if half + 1 < NHALF:
                    emit_ln_trans(half + 1, TC // 128 - 1)

            wb = [sb1(f"q_wb{i}", [128, 2048], BF16) for i in range(2)]
            gs = sb1("q_gs", [128, 32], F32)
            meanb = sb1("q_meanb", [128, 32], BF16)
            qs32 = sb1("q_s32", [128, 32], F32)
            qth = sb1("q_th", [128, 32], F32)
            qnth = sb1("q_nth", [128, 32], F32)
            tpos = sb1("q_tpos", [128, 2048], BF16)
            tneg = sb1("q_tneg", [128, 2048], BF16)
            wqo = sb1("q_wqo", [128, 2048], BF16)
            asm_q = sb1("q_asm", [128, 16 * 256], BF16)
            s_wb = [b.sem(f"s_wb{i}") for i in range(2)]
            s_qc = b.sem("s_qc")
            wb_read = [0, 0]
            qpst_read = [0, 0]
            qtr_pe = [0]
            lnpst_ve = b.ve.n

            def quant_tile(i, src_ap, W, asm_cols):
                par = i % 2
                b.dma("gpsimd", wb[par][:, 0:W], src_ap, s_wb[par],
                      waits=[(b.ve, wb_read[par])])
                wbt = wb[par][:, 0:W]
                wb3 = wbt.rearrange("p (g k) -> p g k", k=GQ)
                ng = W // GQ
                wl = [(s_wb[par], s_wb[par].n)]
                b.op("vector", lambda e, o=gs[:, 0:ng], i3=wb3: e.tensor_reduce(
                    o, i3, AX.X, ALU.add, apply_absolute_value=True), waits=wl,
                    inc=b.ve)
                b.op("vector", lambda e, o=meanb[:, 0:ng], a=gs[:, 0:ng]:
                     e.tensor_scalar(o, a, 1.0 / GQ, None, ALU.mult),
                     waits=[(b.ve, b.ve.n)], inc=b.ve)
                b.op("vector", lambda e, o=meanb[:, 0:ng], a=meanb[:, 0:ng]:
                     e.tensor_scalar(o, a, QEPS, None, ALU.max),
                     waits=[(b.ve, b.ve.n)], inc=b.ve)
                b.op("vector", lambda e, o=qs32[:, 0:ng], a=meanb[:, 0:ng]:
                     e.tensor_copy(o, a), waits=[(b.ve, b.ve.n)], inc=b.ve)
                b.op("vector", lambda e, o=qth[:, 0:ng], a=qs32[:, 0:ng]:
                     e.tensor_scalar(o, a, TH_C, None, ALU.mult),
                     waits=[(b.ve, b.ve.n)], inc=b.ve)
                b.op("vector", lambda e, o=qnth[:, 0:ng], a=qth[:, 0:ng]:
                     e.tensor_scalar(o, a, -1.0, None, ALU.mult),
                     waits=[(b.ve, b.ve.n)], inc=b.ve)
                thb = qth[:, 0:ng].unsqueeze(2).broadcast_to([128, ng, GQ])
                nthb = qnth[:, 0:ng].unsqueeze(2).broadcast_to([128, ng, GQ])
                s32b = qs32[:, 0:ng].unsqueeze(2).broadcast_to([128, ng, GQ])
                tp3 = tpos[:, 0:W].rearrange("p (g k) -> p g k", k=GQ)
                tn3 = tneg[:, 0:W].rearrange("p (g k) -> p g k", k=GQ)

                b.op("vector", lambda e, o=tp3, a=wb3, c=thb:
                     e.tensor_tensor(o, a, c, ALU.is_gt),
                     waits=[(b.ve, b.ve.n)])
                b.op("vector", lambda e, o=tn3, a=wb3, c=nthb:
                     e.tensor_tensor(o, a, c, ALU.is_lt))
                b.op("vector", lambda e, o=tpos[:, 0:W], a=tpos[:, 0:W], c=tneg[:, 0:W]:
                     e.tensor_tensor(o, a, c, ALU.subtract))
                b.op("vector", lambda e, o=tp3, a=tp3, c=s32b:
                     e.tensor_tensor(o, a, c, ALU.mult))
                b.op("vector", lambda e, o=tneg[:, 0:W], a=tpos[:, 0:W], c=wbt:
                     e.tensor_tensor(o, a, c, ALU.subtract))
                b.op("vector", lambda e, o=wqo[:, 0:W], a=wbt, c=tneg[:, 0:W]:
                     e.tensor_tensor(o, a, c, ALU.add),
                     waits=[(b.pe, qtr_pe[0])], inc=b.ve)
                wb_read[par] = b.ve.n
                wqo_ve = b.ve.n
                nk = W // 128
                ib = i % 2
                for k in range(nk):
                    tp = k % 2
                    b.op("tensor", lambda e, o=pst1[tp][:],
                         a=wqo[:, 128 * k:128 * (k + 1)]:
                         e.transpose(o, a, identb[:]),
                         waits=[(b.ve, wqo_ve), (b.ac, qpst_read[tp]),
                                (b.ve, lnpst_ve)],
                         inc=b.pe)
                    qtr_pe[0] = b.pe.n
                    b.op("scalar", lambda e,
                         o=asm_q[:, asm_cols * k + 128 * ib:asm_cols * k + 128 * (ib + 1)],
                         a=pst1[tp][:]: e.activation(o, a, AF.Copy),
                         waits=b.w(b.pe) + [(s_qc, s_qc.n)], inc=b.ac,
                         drain=(k == nk - 1 and ib == 1))
                    qpst_read[tp] = b.ac.n

            for bb in range(8):
                for ii in range(2):
                    i = 2 * bb + ii
                    quant_tile(i, wgu_in[128 * i:128 * (i + 1), :], 2048, 256)
                b.dma("scalar",
                      qc_in[0:2048, 256 * bb:256 * (bb + 1)].rearrange(
                          "(k p) h -> p k h", p=128),
                      asm_q[:].rearrange("p (k h) -> p k h", h=256), s_qc,
                      waits=[(b.ac, b.ac.n)])
            for bb in range(8):
                for ii in range(2):
                    i = 2 * bb + ii
                    quant_tile(i, wp_in[128 * i:128 * (i + 1), :], 1024, 256)
                b.dma("scalar",
                      qc_in[2048:3072, 256 * bb:256 * (bb + 1)].rearrange(
                          "(k p) h -> p k h", p=128),
                      asm_q[:, 0:8 * 256].rearrange("p (k h) -> p k h", h=256), s_qc,
                      waits=[(b.ac, b.ac.n)])
            b.op("gpsimd", lambda e: e.collective_compute(
                "AllGather", ALU.bypass, replica_groups=[list(range(8))],
                ins=[qc_in.opt()], outs=[qc_out.opt()]),
                waits=b.w(s_qc), inc=b.cc, raw_inc=True)
            cc_q = b.cc.n
            b.barrier()
            b.flush(block)

        with ExitStack() as p2:
            psb = lambda name, shape, dt: p2.enter_context(nc.psum_tensor(name, shape, dt))
            sb2 = lambda name, shape, dt: p2.enter_context(nc.sbuf_tensor(name, shape, dt))
            qh = sb2("qh", [128, 2 * S], BF16)
            kh = sb2("kh", [128, 2 * S], BF16)
            vh = sb2("vh", [128, 2 * S], BF16)
            pT = [sb2(f"pT{i}", [128, 1024], BF16) for i in range(3)]
            acc_d = sb2("acc_d", [128, 1024], F32)
            acc_p = sb2("acc_p", [128, 1024], F32)
            acc_f = sb2("acc_f", [128, 512], F32)
            acc_bf = sb2("acc_bf", [128, 512], BF16)
            r_full = [sb2(f"r_full{i}", [1, S], F32) for i in range(2)]
            esA = [sb2(f"esA{i}", [128, 512], F32) for i in range(2)]
            esB = [sb2(f"esB{i}", [128, 512], F32) for i in range(2)]
            esum_b = [sb2(f"esum_b{i}", [128, 1024], BF16) for i in range(2)]
            r_2 = sb2("r_2", [1, S], F32)
            r_hi = sb2("r_hi", [1, S], BF16)
            r_lo = sb2("r_lo", [1, S], BF16)
            rb_sb = [sb2(f"rb_sb{i}", [128, 512], F32) for i in range(2)]
            ostage = [sb2(f"ostage{i}", [128, S], BF16) for i in range(2)]
            masks = sb2("masks", [128, 4 * 512], BF16)
            b.dma("sync", masks[:], masks_in, s_consts)
            pss = [psb(f"pss{i}", [128, 1024], F32) for i in range(2)]
            pso = [psb(f"pso{i}", [128, 512], F32) for i in range(2)]
            psr = psb("psr", [1, 512], F32)
            psbc = psb("psbc", [128, 512], F32)
            s_head = [b.sem(f"s_head{i}") for i in range(2)]
            s_ost = b.sem("s_ost")

            NG = S // 512

            def load_head(h, slot, prefetched):
                if prefetched:
                    wl = b.w(b.pe, b.ve)
                else:
                    wl = b.w(*s_qst)
                b.dma("sync", qh[:, slot * S:(slot + 1) * S],
                      qT_scr[128 * h:128 * (h + 1), :], s_head[slot], waits=wl)
                b.dma("sync", kh[:, slot * S:(slot + 1) * S],
                      kT_scr[128 * h:128 * (h + 1), :], s_head[slot])
                b.dma("sync", vh[:, slot * S:(slot + 1) * S].rearrange("p (j h) -> p j h", h=128),
                      v_scr[0:S, 128 * h:128 * (h + 1)].rearrange("(j p) h -> p j h", p=128),
                      s_head[slot])

            horder = [4, 5, 6, 7, 0, 1, 2, 3] if NHL == 8 else list(range(NHL))
            load_head(horder[0], 0, False)
            if NHL > 1:
                load_head(horder[1], 1, False)
            lastexp = [0, 0]
            pT_free = [[], [], []]
            oevac = [0, 0]
            psr_free = 0
            last_bmm = 0
            rf_read = [0, 0]
            rf_done = [0, 0]
            psbc_evac = [0]
            cc_b = 0

            bc_ve_s = [0]

            def emit_tail_chain(h, hp):
                nonlocal last_bmm
                b.op("scalar", lambda e: e.activation(r_2[0:1, 0:S],
                                                      r_full[hp][0:1, 0:S], AF.Ln),
                     waits=[(b.ve, rf_done[hp])], inc=b.ac)
                rf_read[hp] = b.ac.n
                b.op("scalar", lambda e: e.activation(r_2[0:1, 0:S], r_2[0:1, 0:S],
                                                      AF.Exp, scale=-1.0),
                     waits=[(b.ac, b.ac.n)], inc=b.ac)
                b.op("vector", lambda e: e.tensor_copy(r_hi[0:1, 0:S], r_2[0:1, 0:S]),
                     waits=[(b.pe, last_bmm)] + b.w(b.ac), inc=b.ve, drain=False)
                b.op("vector", lambda e: e.tensor_tensor(
                    r_lo[0:1, 0:S], r_2[0:1, 0:S], r_hi[0:1, 0:S], ALU.subtract),
                    waits=[(b.ve, b.ve.n)], inc=b.ve)
                bc_ve_s[0] = b.ve.n

            def emit_tail(h, hp):
                nonlocal last_bmm, cc_b
                bc_ve = bc_ve_s[0]
                for g in range(NG):
                    bp = g % 2
                    wl = [(b.ve, bc_ve if g == 0 else psbc_evac[0])]
                    b.op("tensor", lambda e, r=r_hi[0:1, 512 * g:512 * (g + 1)]:
                         e.matmul(psbc[:], ones_r[0:1, 0:128], r, start=True, stop=False),
                         waits=wl)
                    b.op("tensor", lambda e, r=r_lo[0:1, 512 * g:512 * (g + 1)]:
                         e.matmul(psbc[:], ones_r[0:1, 0:128], r, start=False, stop=True),
                         inc=b.pe)
                    last_bmm = b.pe.n
                    b.op("vector", lambda e, o=rb_sb[bp][:]:
                         e.tensor_copy(o, psbc[:]),
                         waits=[(b.pe, last_bmm)], inc=b.ve, drain=False)
                    psbc_evac[0] = b.ve.n
                    b.op("vector", lambda e, o=ostage[hp][:, 512 * g:512 * (g + 1)],
                         b2=bp: e.tensor_tensor(o, o, rb_sb[b2][:], ALU.mult),
                         waits=[(b.ve, b.ve.n)], inc=b.ve)
                b.dma("scalar", o_loc[128 * h:128 * (h + 1), :], ostage[hp][:], s_ost,
                      waits=b.w(b.ve))
                if h == (7 if NHL == 8 else 3):
                    b.op("gpsimd", lambda e: e.collective_compute(
                        "AllGather", ALU.bypass, replica_groups=[list(range(8))],
                        ins=[o_loc[512:1024, :].opt()], outs=[o_all_b.opt()]),
                        waits=b.w(s_ost), inc=b.cc, raw_inc=True)
                    cc_b = b.cc.n

            summm = [0, 0]
            esum_done = [0, 0]
            for hi in range(NHL):
                h = horder[hi]
                slot = hi % 2
                hp = hi % 2
                ost_wait = s_ost.n
                hl = [(s_head[slot], s_head[slot].n)]
                steps = [(g, jp) for g in range(NG) for jp in range(2 * g + 2)]

                def emit_av(g, jp, pb, gate):
                    nonlocal psr_free
                    gp = g % 2
                    njp = 2 * g + 2
                    for half2 in range(2):
                        j = 2 * jp + half2
                        wl = []
                        if half2 == 0:
                            wl = [(b.ac, gate)]
                            if jp == 0:
                                wl.append((b.ve, oevac[gp]))
                        b.op("tensor", lambda e, o=pso[gp][:],
                             l=vh[:, slot * S + 128 * j:slot * S + 128 * (j + 1)],
                             r=pT[pb][:, 512 * half2:512 * (half2 + 1)],
                             s=(jp == 0 and half2 == 0),
                             t=(jp == njp - 1 and half2 == 1):
                             e.matmul(o, l, r, start=s, stop=t),
                             waits=wl, inc=(b.pe if half2 == 1 else None))
                    pT_free[pb] = [(b.pe, b.pe.n)]
                    if jp == njp - 1:
                        av_pe = b.pe.n
                        b.op("tensor", lambda e, m=esum_b[gp][:, 0:512]:
                             e.matmul(psr[:], ones_c[:], m, start=True, stop=False),
                             waits=[(b.ve, esum_done[gp]), (b.ve, psr_free)])
                        b.op("tensor", lambda e, m=esum_b[gp][:, 512:1024]:
                             e.matmul(psr[:], ones_c[:], m, start=False, stop=True),
                             inc=b.pe)
                        summm[gp] = b.pe.n
                        b.op("vector", lambda e, o=ostage[hp][:, 512 * g:512 * (g + 1)],
                             g2=gp: e.tensor_copy(o, pso[g2][:]),
                             waits=[(b.pe, av_pe)] +
                                   ([(s_ost, ost_wait)] if g == 0 else []),
                             inc=b.ve, drain=False)
                        oevac[gp] = b.ve.n
                        b.op("vector", lambda e, o=r_full[hp][0:1, 512 * g:512 * (g + 1)],
                             g2=gp: e.tensor_copy(o, psr[:]),
                             waits=[(b.pe, summm[gp]), (b.ac, rf_read[hp])],
                             inc=b.ve, drain=(g == NG - 1))
                        psr_free = b.ve.n
                        if g == NG - 1:
                            rf_done[hp] = b.ve.n

                pend = None
                esum_read = [0, 0, 0]
                for i, (g, jp) in enumerate(steps):
                    pp = i % 2
                    pb = i % 3
                    gp = g % 2
                    wl = [(b.ac, lastexp[pp])] + (hl if i == 0 else [])
                    sc_ops = []
                    for half2 in range(2):
                        j = 2 * jp + half2
                        diag = j >= 4 * g
                        sc_ops.append((
                            lambda e, o=pss[pp][:, 512 * half2:512 * (half2 + 1)],
                            l=kh[:, slot * S + 128 * j:slot * S + 128 * (j + 1)],
                            r=qh[:, slot * S + 512 * g:slot * S + 512 * (g + 1)],
                            t=(not diag):
                            e.matmul(o, l, r, start=True, stop=t),
                            wl if half2 == 0 else []))
                        if diag:
                            mi = j - 4 * g
                            sc_ops.append((
                                lambda e, o=pss[pp][:, 512 * half2:512 * (half2 + 1)],
                                m=masks[:, 512 * mi:512 * (mi + 1)]:
                                e.matmul(o, identb[:], m, start=False, stop=True),
                                b.w(s_consts) if (h == 0 and g == 0) else []))
                    for si, (fn, swl) in enumerate(sc_ops):
                        b.op("tensor", fn, waits=swl,
                             inc=(b.pe if si == len(sc_ops) - 1 else None))
                    sc_pe = b.pe.n
                    b.op("scalar", lambda e, o=pT[pb][:], a=pss[pp][:]:
                         e.activation(o, a, AF.Exp, scale=SM_SCALE),
                         waits=[(b.pe, sc_pe), (b.ve, esum_read[pb])] +
                               pT_free[pb],
                         inc=b.ac, drain=False)
                    lastexp[pp] = b.ac.n
                    gate = b.ac.n
                    if pend is not None:
                        emit_av(*pend)
                    last = (jp == 2 * g + 1)
                    if jp == 0:
                        b.op("vector", lambda e, o=esA[gp][:], a=pT[pb][:, 0:512]:
                             e.tensor_copy(o, a),
                             waits=[(b.ac, gate), (b.pe, summm[gp])],
                             inc=b.ve, drain=False)
                        b.op("vector", lambda e, o=esB[gp][:], a=pT[pb][:, 512:1024]:
                             e.tensor_copy(o, a), inc=b.ve, drain=False)
                    else:
                        oA = esum_b[gp][:, 0:512] if last else esA[gp][:]
                        oB = esum_b[gp][:, 512:1024] if last else esB[gp][:]
                        b.op("vector", lambda e, o=oA, s1=esA[gp][:],
                             a=pT[pb][:, 0:512]:
                             e.tensor_tensor(o, s1, a, ALU.add),
                             waits=[(b.ac, gate)], inc=b.ve, drain=False)
                        b.op("vector", lambda e, o=oB, s1=esB[gp][:],
                             a=pT[pb][:, 512:1024]:
                             e.tensor_tensor(o, s1, a, ALU.add),
                             inc=b.ve, drain=last)
                        if last:
                            esum_done[gp] = b.ve.n
                    esum_read[pb] = b.ve.n
                    pend = (g, jp, pb, gate)
                    if hi > 0 and i == min(2, len(steps) - 3):
                        emit_tail_chain(horder[hi - 1], (hi - 1) % 2)
                    if hi > 0 and i == min(40, len(steps) - 2):
                        emit_tail(horder[hi - 1], (hi - 1) % 2)
                emit_av(*pend)
                if hi + 2 < NHL:
                    load_head(horder[hi + 2], hi % 2, True)
            emit_tail_chain(horder[NHL - 1], (NHL - 1) % 2)
            emit_tail(horder[NHL - 1], (NHL - 1) % 2)
            b.op("gpsimd", lambda e: e.collective_compute(
                "AllGather", ALU.bypass, replica_groups=[list(range(8))],
                ins=[o_loc[0:512, :].opt()], outs=[o_all_a.opt()]),
                waits=b.w(s_ost), inc=b.cc, raw_inc=True)
            cc_a = b.cc.n
            b.barrier()
            b.flush(block)

        with ExitStack() as p3:
            psb = lambda name, shape, dt: p3.enter_context(nc.psum_tensor(name, shape, dt))
            sb3 = lambda name, shape, dt: p3.enter_context(nc.sbuf_tensor(name, shape, dt))
            og = sb3("og", [128, 16 * OH], BF16)
            woT = sb3("woT", [128, 16 * 2048], BF16)
            x1t = [sb3(f"x1t{i}", [128, D], F32) for i in range(2)]
            x2st = [sb3(f"x2st{i}", [128, D], BF16) for i in range(2)]
            xt = [sb3(f"p3xt{i}", [128, D], F32) for i in range(2)]
            xlnt = [sb3(f"p3xlnt{i}", [128, D], BF16) for i in range(2)]
            sqscr = sb3("p3sqscr", [128, D], F32)
            bo_sb = sb3("bo_sb", [1, D], BF16)
            b.dma("gpsimd", bo_sb[:], bo_in, s_cast)
            psop = [psb(f"psop{i}", [128, 512], F32) for i in range(2)]
            pst3 = [psb(f"p3_pst{i}", [128, 128], BF16) for i in range(2)]
            s_og = b.sem("s_og")
            s_woT = b.sem("s_woT")
            s_x1 = [b.sem(f"s_x1{i}") for i in range(2)]
            s_x2T = [b.sem(f"s_x2T{i}") for i in range(2)]

            o_view_a = o_all_a.rearrange("r (q t) -> (r q) t", t=OH)
            o_view_b = o_all_b.rearrange("r (q t) -> (r q) t", t=OH)
            forder = [4, 5, 6, 7, 12, 13, 14, 15, 0, 1, 2, 3, 8, 9, 10, 11] \
                if NHL == 8 else [0, 1, 2, 3, 8, 9, 10, 11, 4, 5, 6, 7, 12, 13, 14, 15]
            for ki, k in enumerate(forder):
                b.dma("sync", woT[:, 2048 * k:2048 * (k + 1)],
                      wo_bf[0:2048, 128 * k:128 * (k + 1)], s_woT,
                      waits=b.w(s_cast) if ki == 0 else (), transpose=True)

            psop_read = [0, 0]
            pst_read = [0, 0]
            lnsq_read = [0, 0]
            lnsq_ac = [0, 0]
            ln_ac3 = [0, 0]
            xt_ap3 = [0, 0]
            xlnt_read3 = [0, 0]
            for hf in range(2):
                gw = (b.w(b.pe) if hf == 1 else [])
                for fi, f in enumerate(forder):
                    col = f + 16 * hf
                    vw = o_view_a if (f % 8) < 4 else o_view_b
                    wl0 = []
                    if fi == 0:
                        wl0 = gw + b.w(s_consts) + [(b.cc, cc_b if NHL == 8 else cc_a)]
                    elif fi == 8:
                        wl0 = [(b.cc, cc_a if NHL == 8 else cc_b)]
                    b.op("gpsimd", lambda e, o=og[:, OH * f:OH * (f + 1)],
                         ix=oidx[:, col:col + 1], v2=vw:
                         e.indirect_dma_start(o, None, v2,
                                              bass.IndirectOffsetOnAxis(ix, 0)),
                         waits=wl0, inc=s_og, n=16)
                    if fi == 7:
                        og_a = s_og.n
                ogl = [(s_og, og_a)]
                og_b = s_og.n
                for sub in range(OH // 128):
                    par = sub % 2
                    c = par
                    r0 = hf * OH + sub * 128
                    b.dma("scalar", xt[par][:], x_res[r0:r0 + 128, :], s_xt[par],
                          waits=[(b.ve, xt_ap3[par])] if (hf + sub) else [])
                    for dg in range(4):
                        pp = dg % 2
                        for fi2 in range(16):
                            fc = forder[fi2]
                            wl = (ogl + [(s_woT, 16),
                                         (b.ve, psop_read[pp])]) if fi2 == 0 else \
                                ([(s_og, og_b)] if (fi2 == 8 and sub == 0 and dg == 0
                                                    ) else [])
                            if hf == 0 and sub == 0 and dg == 0 and fi2 > 0:
                                wl = wl + [(s_woT, 16 * (fi2 + 1))]
                            b.op("tensor", lambda e, o=psop[pp][:],
                                 l=og[:, OH * fc + 128 * sub:OH * fc + 128 * (sub + 1)],
                                 r=woT[:, 2048 * fc + 512 * dg:2048 * fc + 512 * (dg + 1)],
                                 s=(fi2 == 0):
                                 e.matmul(o, l, r, start=s, stop=False), waits=wl)
                        b.op("tensor", lambda e, o=psop[pp][:],
                             r=bo_sb[0:1, 512 * dg:512 * (dg + 1)]:
                             e.matmul(o, ones_r[0:1, 0:128], r, start=False, stop=True),
                             inc=b.pe)
                        wl = [(b.pe, b.pe.n), (s_xt[par], s_xt[par].n)]
                        if dg == 0:
                            wl += [(s_x1[par], s_x1[par].n), (b.ac, lnsq_ac[par])]
                        b.op("vector", lambda e, o=x1t[par][:, 512 * dg:512 * (dg + 1)],
                             a=psop[pp][:], x2=xt[par][:, 512 * dg:512 * (dg + 1)]:
                             e.tensor_tensor(o, a, x2, ALU.add), waits=wl, inc=b.ve)
                        psop_read[pp] = b.ve.n
                    x1_ve = b.ve.n
                    xt_ap3[par] = x1_ve
                    b.dma("scalar", x1_scr[r0:r0 + 128, :], x1t[par][:], s_x1[par],
                          waits=[(b.ve, x1_ve)])
                    b.op("vector", lambda e, o=s1v[:, c:c + 1], a=x1t[par][:]:
                         e.tensor_reduce(o, a, AX.X, ALU.add),
                         waits=[(b.ve, x1_ve)], inc=b.ve)
                    b.op("vector", lambda e, o=mv[:, c:c + 1], a=s1v[:, c:c + 1]:
                         e.tensor_scalar(o, a, -1.0 / D, None, ALU.mult),
                         waits=[(b.ve, b.ve.n), (b.ac, ln_ac3[c])], inc=b.ve)
                    mv_ve3 = b.ve.n
                    b.op("scalar", lambda e, a=x1t[par][:], m2=mv[:, c:c + 1]:
                         e.activation(sqscr[:], a, AF.Square, bias=m2),
                         waits=[(b.ve, mv_ve3), (b.ve, max(lnsq_read))], inc=b.ac)
                    lnsq_ac[par] = b.ac.n
                    b.op("vector", lambda e, o=sqv[:, c:c + 1]:
                         e.tensor_reduce(o, sqscr[:], AX.X, ALU.add),
                         waits=[(b.ac, b.ac.n)], inc=b.ve)
                    lnsq_read[0] = lnsq_read[1] = b.ve.n
                    b.op("scalar", lambda e, o=rstdv[:, c:c + 1], a=sqv[:, c:c + 1]:
                         e.activation(o, a, AF.Ln, scale=1.0 / D, bias=cst[:, 0:1]),
                         waits=[(b.ve, max(lnsq_read))], inc=b.ac)
                    b.op("scalar", lambda e, o=rstdv[:, c:c + 1], a=rstdv[:, c:c + 1]:
                         e.activation(o, a, AF.Exp, scale=-0.5),
                         waits=[(b.ac, b.ac.n)], inc=b.ac)
                    ln_ac3[c] = b.ac.n
                    b.op("vector", lambda e, o=xlnt[par][:], a=x1t[par][:],
                         m2=mv[:, c:c + 1], r2=rstdv[:, c:c + 1]:
                         e.tensor_scalar(o, a, m2, r2, ALU.add, ALU.mult),
                         waits=[(b.ac, ln_ac3[c]), (b.pe, xlnt_read3[par]),
                                (s_x2T[par], s_x2T[par].n)], inc=b.ve)
                    ap_ve = b.ve.n
                    for k in range(16):
                        tp = k % 2
                        b.op("tensor", lambda e, o=pst3[tp][:],
                             a=xlnt[par][:, 128 * k:128 * (k + 1)]:
                             e.transpose(o, a, identb[:]),
                             waits=[(b.ve, ap_ve), (b.ve, pst_read[tp])],
                             inc=b.pe)
                        b.op("vector", lambda e, o=x2st[par][:, 128 * k:128 * (k + 1)],
                             a=pst3[tp][:]: e.tensor_copy(o, a),
                             waits=b.w(b.pe), inc=b.ve, drain=(k == 15))
                        if k == 15:
                            xlnt_read3[par] = b.pe.n
                        pst_read[tp] = b.ve.n
                    b.dma("scalar",
                          x2T_scr[:, r0:r0 + 128].rearrange("(k p) q -> p k q", p=128),
                          x2st[par][:].rearrange("p (k q) -> p k q", q=128),
                          s_x2T[par], waits=b.w(b.ve))
            b.barrier()
            b.flush(block)

        with ExitStack() as p4:
            psb = lambda name, shape, dt: p4.enter_context(nc.psum_tensor(name, shape, dt))
            sb4 = lambda name, shape, dt: p4.enter_context(nc.sbuf_tensor(name, shape, dt))
            hT = sb4("hT", [128, 64 * 512], BF16)
            xl2 = sb4("xl2", [128, 16 * 512], BF16)
            wgu = [sb4(f"wgu{i}", [128, 16 * 1024], BF16) for i in range(3)]
            x1p = [sb4(f"x1p{i}", [128, 512], F32) for i in range(2)]
            sg = sb4("sg", [128, 4 * 512], BF16)
            hsq = [sb4(f"hsq{i}", [128, 512], BF16) for i in range(2)]
            ssA = sb4("ssA", [128, 512], F32)
            ssB = sb4("ssB", [128, 512], F32)
            ssb16 = sb4("ssb16", [128, 512], BF16)
            invr = sb4("invr", [1, 512], F32)
            inv_hi = sb4("inv_hi", [1, 512], BF16)
            inv_lo = sb4("inv_lo", [1, 512], BF16)
            invc = sb4("invc", [128, 4], F32)
            x2p = [sb4(f"x2p{i}", [128, 512], F32) for i in range(2)]
            pw = [psb(f"pw{i}", [128, 512], F32) for i in range(8)]
            s_xl2 = b.sem("s_xl2")
            s_x1p = [b.sem(f"s_x1p{i}") for i in range(2)]
            s_wgu = [b.sem(f"s_wgu{i}") for i in range(3)]
            s_out = [b.sem(f"s_out{i}") for i in range(2)]

            NGRP = T // 512
            silu_cnt = [0, 0, 0, 0]
            bank_read = [0, 0, 0, 0]
            sg_read = [0, 0, 0, 0]
            hsq_ve = [0, 0]
            wbuf_pe = [0, 0, 0]
            evac_cnt = [0] * 8
            x1p_read = [0, 0]
            ss_edge = (None, 0)
            inv55_read = 0

            def gu_window(habs):
                rb = habs // 2048
                off = habs % 2048
                return qc_out[3072 * rb:3072 * rb + 2048, off:off + 1024].rearrange(
                    "(k p) h -> p k h", p=128)

            def load_gu(k, first=False):
                buf = (k + 2) % 3
                jH = k // 2
                habs = (0 if k % 2 == 0 else FF) + 1024 * jH
                b.dma("sync", wgu[buf][:].rearrange("p (k h) -> p k h", h=1024),
                      gu_window(habs), s_wgu[buf],
                      waits=[(b.pe, wbuf_pe[buf])] +
                            ([(b.cc, cc_q)] if first else []))

            def load_xl2(grp, wl):
                t0 = grp * 512
                b.dma("gpsimd", xl2[:].rearrange("p (k t) -> p k t", t=512),
                      x2T_scr[:, t0:t0 + 512].rearrange("(k p) t -> p k t", p=128),
                      s_xl2, waits=wl)
                xl2_cnt[0] = s_xl2.n

            xl2_cnt = [0]
            load_xl2(0, b.w(s_x2T[0], s_x2T[1]))
            for grp in range(NGRP):
                t0 = grp * 512
                xl2w = [(s_xl2, xl2_cnt[0])]
                for k in ((0, 1, 2) if grp == 0 else (1, 2)):
                    load_gu(k, first=(grp == 0 and k == 0))

                for jH in range(8):
                    bg, bu = (2 * jH + 2) % 3, (2 * jH + 3) % 3
                    gl = [(s_wgu[bg], s_wgu[bg].n)]
                    ul = [(s_wgu[bu], s_wgu[bu].n)]
                    for rnd in range(2):
                        for hx4 in range(4):
                            hx = rnd * 4 + hx4
                            for dc in range(16):
                                wl2 = (gl + [(b.ve, bank_read[hx4]),
                                             (b.ve, evac_cnt[hx4])] + xl2w) \
                                    if dc == 0 else []
                                b.op("tensor", lambda e, o=pw[hx4][:], bg2=bg,
                                     l=wgu[bg][:, 1024 * dc + 128 * hx:1024 * dc + 128 * (hx + 1)],
                                     r=xl2[:, 512 * dc:512 * (dc + 1)], s=(dc == 0),
                                     t=(dc == 15):
                                     e.matmul(o, l, r, start=s, stop=t),
                                     waits=wl2, inc=(b.pe if dc == 15 else None))
                            gate_pe = b.pe.n
                            b.op("scalar", lambda e, o=sg[:, 512 * hx4:512 * (hx4 + 1)],
                                 a=pw[hx4][:]:
                                 e.activation(o, a, AF.Silu),
                                 waits=[(b.pe, gate_pe), (b.ve, sg_read[hx4])],
                                 inc=b.ac, drain=False)
                            silu_cnt[hx4] = b.ac.n
                        if rnd == 1:
                            wbuf_pe[bg] = b.pe.n
                        for hx4 in range(4):
                            hx = rnd * 4 + hx4
                            jc = 8 * jH + hx
                            for dc in range(16):
                                wl2 = (ul + [(b.ac, silu_cnt[hx4])]) if dc == 0 else []
                                b.op("tensor", lambda e, o=pw[hx4][:],
                                     l=wgu[bu][:, 1024 * dc + 128 * hx:1024 * dc + 128 * (hx + 1)],
                                     r=xl2[:, 512 * dc:512 * (dc + 1)], s=(dc == 0),
                                     t=(dc == 15):
                                     e.matmul(o, l, r, start=s, stop=t),
                                     waits=wl2, inc=(b.pe if dc == 15 else None))
                            up_pe = b.pe.n
                            b.op("vector", lambda e, o=hT[:, 512 * jc:512 * (jc + 1)],
                                 a=sg[:, 512 * hx4:512 * (hx4 + 1)], p=pw[hx4][:]:
                                 e.tensor_tensor(o, a, p, ALU.mult),
                                 waits=[(b.pe, up_pe)], inc=b.ve, drain=False)
                            bank_read[hx4] = b.ve.n
                            sg_read[hx4] = b.ve.n
                            hp = jc % 2
                            b.op("scalar", lambda e, o=hsq[hp][:],
                                 a=hT[:, 512 * jc:512 * (jc + 1)]:
                                 e.activation(o, a, AF.Square),
                                 waits=[(b.ve, b.ve.n), (b.ve, hsq_ve[hp])],
                                 inc=b.ac, drain=False)
                            sswl = [(b.ac, b.ac.n)]
                            acc = ssA if hp == 0 else ssB
                            if jc < 2:
                                b.op("vector", lambda e, o=acc[:], a=hsq[hp][:]:
                                     e.tensor_copy(o, a),
                                     waits=sswl, inc=b.ve, drain=False)
                            else:
                                b.op("vector", lambda e, o=acc[:], s1=acc[:],
                                     a=hsq[hp][:]:
                                     e.tensor_tensor(o, s1, a, ALU.add),
                                     waits=sswl, inc=b.ve, drain=False)
                            hsq_ve[hp] = b.ve.n
                            if jc == 63:
                                b.op("vector", lambda e:
                                     e.tensor_tensor(ssb16[:], ssA[:], ssB[:],
                                                     ALU.add),
                                     waits=[(b.ve, b.ve.n)], inc=b.ve)
                                ss16_ve = b.ve.n
                                mmwl = [(b.ve, ss16_ve), (b.ve, evac_cnt[4])]
                                if ss_edge[0] is not None:
                                    mmwl = mmwl + [ss_edge]
                                b.op("tensor", lambda e:
                                     e.matmul(pw[4][0:1, :], ones_c[:], ssb16[:],
                                              start=True, stop=True),
                                     waits=mmwl, inc=b.pe)
                        if rnd == 1:
                            wbuf_pe[bu] = b.pe.n
                    for k in (2 * jH + 3, 2 * jH + 4):
                        if k < 16:
                            load_gu(k)
                hT_done = b.ve.n
                gu_pe_last = b.pe.n
                if grp + 1 < NGRP:
                    load_xl2(grp + 1, [(b.pe, gu_pe_last)])

                b.op("scalar", lambda e: e.activation(invr[:], pw[4][0:1, :], AF.Ln,
                                                      scale=1.0 / FF, bias=cst[0:1, 1:2]),
                     waits=[(b.pe, gu_pe_last)], inc=b.ac)
                ss_edge = (b.ac, b.ac.n)
                b.op("scalar", lambda e: e.activation(invr[:], invr[:], AF.Exp,
                                                      scale=-0.5),
                     waits=[(b.ac, b.ac.n)], inc=b.ac)
                b.op("vector", lambda e: e.tensor_copy(inv_hi[:], invr[:]),
                     waits=b.w(b.ac) + [(b.pe, inv55_read)], inc=b.ve)
                b.op("vector", lambda e: e.tensor_tensor(inv_lo[:], invr[:], inv_hi[:],
                                                         ALU.subtract),
                     waits=[(b.ve, b.ve.n)], inc=b.ve)
                for sub in range(4):
                    b.op("tensor", lambda e, o=pw[5][:, sub:sub + 1],
                         l=inv_hi[0:1, 128 * sub:128 * (sub + 1)]:
                         e.matmul(o, l, ones_r[0:1, 0:1], start=True, stop=False),
                         waits=(b.w(b.ve) + [(b.ve, evac_cnt[5])]) if sub == 0 else ())
                    b.op("tensor", lambda e, o=pw[5][:, sub:sub + 1],
                         l=inv_lo[0:1, 128 * sub:128 * (sub + 1)]:
                         e.matmul(o, l, ones_r[0:1, 0:1], start=False, stop=True),
                         inc=b.pe)
                inv55_read = b.pe.n
                b.op("vector", lambda e: e.tensor_copy(invc[:], pw[5][:, 0:4]),
                     waits=b.w(b.pe), inc=b.ve)
                inv_ve = b.ve.n
                evac_cnt[5] = max(evac_cnt[5], inv_ve)

                wqp = [wgu[0][:, 0:8 * 1024], wgu[1][:, 0:8 * 1024]]
                for wave in range(2):
                    for rb in range(8):
                        wp_par = rb % 2
                        b.dma("sync", wqp[wp_par].rearrange("p (k d) -> p k d", d=1024),
                              qc_out[3072 * rb + 2048:3072 * rb + 3072,
                                     1024 * wave:1024 * (wave + 1)].rearrange(
                                         "(k p) d -> p k d", p=128),
                              s_wgu[wp_par], waits=[(b.pe, wbuf_pe[wp_par])])
                        for pair in range(8):
                            sub, dgw = pair // 2, pair % 2
                            for hx in range(8):
                                j = 8 * rb + hx
                                wl2 = []
                                if hx == 0:
                                    wl2 = [(s_wgu[wp_par], s_wgu[wp_par].n)]
                                    if rb == 0:
                                        wl2 += [(b.ve, hT_done), (b.ve, inv_ve),
                                                (b.ve, evac_cnt[pair])]
                                b.op("tensor", lambda e, o=pw[pair][:], w2=wp_par,
                                     l=hT[:, 512 * j + 128 * sub:512 * j + 128 * (sub + 1)],
                                     r=wqp[wp_par][:, 1024 * hx + 512 * dgw:1024 * hx + 512 * (dgw + 1)],
                                     s=(rb == 0 and hx == 0), t=(rb == 7 and hx == 7):
                                     e.matmul(o, l, r, start=s, stop=t),
                                     waits=wl2,
                                     inc=(b.pe if hx == 7 else None))
                        wbuf_pe[wp_par] = b.pe.n
                    if wave == 0 and grp + 1 < NGRP:
                        load_gu(0)
                    proj_pe = b.pe.n
                    for pair in range(8):
                        sub, dgw = pair // 2, pair % 2
                        dg = wave * 2 + dgw
                        xp = pair % 2
                        b.dma("sync", x1p[xp][:],
                              x1_scr[t0 + 128 * sub:t0 + 128 * (sub + 1),
                                     512 * dg:512 * (dg + 1)], s_x1p[xp],
                              waits=[(b.ve, x1p_read[xp])] +
                              (b.w(s_x1[0], s_x1[1]) if (grp == 0 and wave == 0
                                                         and pair < 2) else []))
                        wl = [(b.pe, proj_pe), (s_out[xp], s_out[xp].n),
                              (s_x1p[xp], s_x1p[xp].n)]
                        b.op("vector", lambda e, o=x2p[xp][:], p=pw[pair][:],
                             iv=invc[:, sub:sub + 1], x1=x1p[xp][:]:
                             e.scalar_tensor_tensor(o, p, iv, x1, ALU.mult, ALU.add),
                             waits=wl, inc=b.ve)
                        evac_cnt[pair] = b.ve.n
                        x1p_read[xp] = b.ve.n
                        b.dma("scalar",
                              out[t0 + 128 * sub:t0 + 128 * (sub + 1), 512 * dg:512 * (dg + 1)],
                              x2p[xp][:], s_out[xp], waits=b.w(b.ve))

            if debug:
                s_tap = b.sem("s_tap")
                tw = b.w(s_out[0], s_out[1], s_qst[0], s_qst[1], s_ost,
                         s_x1[0], s_x1[1], s_x2T[0], s_x2T[1])
                for name, (dst, src) in taps.items():
                    b.dma("gpsimd", dst, src, s_tap, waits=tw)
                b.op("gpsimd", lambda e: e.nop(), waits=b.w(s_tap))
            b.op("gpsimd", lambda e: e.nop(), waits=b.w(s_out[0], s_out[1]))
            b.op("sync", lambda e: e.nop(), waits=b.w(s_out[0], s_out[1]))
            b.flush(block)

    return nc



def _host_consts():
    identb = np.eye(128, dtype=BFD)
    tk = np.arange(128)[:, None]
    tq = np.arange(512)[None, :]
    masks = np.concatenate(
        [np.where((tk + 128 * mi) <= tq, 0.0, -30000.0).astype(BFD)
         for mi in range(4)], axis=1)
    return identb, masks


def _oidx_for_core(c, S):
    T = S // 2
    OH = T // 2
    b_, g = c // 2, c % 2
    qpr = S // OH
    idx = np.zeros((128, 32), dtype=np.int32)
    p = np.arange(128)
    for hf in range(2):
        for f in range(16):
            rank = 2 * b_ + f // 8
            hl = f % 8
            R = rank * 512 + (hl % 4) * 128 + p
            q = (T * g + OH * hf) // OH
            idx[:, f + 16 * hf] = R * qpr + q
    return idx


def kernel(L0_attn_h, L0_attn_Wq, L0_attn_bq, L0_attn_Wk, L0_attn_bk,
           L0_attn_Wv, L0_attn_bv, L0_attn_Wo, L0_attn_bo,
           L0_ffn_w_gate_up, L0_ffn_w_proj, _debug=False):
    h = np.asarray(L0_attn_h, dtype=np.float32)
    Wq = np.asarray(L0_attn_Wq, dtype=np.float32)
    bq = np.asarray(L0_attn_bq, dtype=np.float32)
    Wk = np.asarray(L0_attn_Wk, dtype=np.float32)
    bk = np.asarray(L0_attn_bk, dtype=np.float32)
    Wv = np.asarray(L0_attn_Wv, dtype=np.float32)
    bv = np.asarray(L0_attn_bv, dtype=np.float32)
    Wo = np.asarray(L0_attn_Wo, dtype=np.float32)
    bo = np.asarray(L0_attn_bo, dtype=np.float32)
    Wgu = np.asarray(L0_ffn_w_gate_up, dtype=np.float32)
    Wp = np.asarray(L0_ffn_w_proj, dtype=np.float32)

    B, S, _ = h.shape
    T = S // 2
    nc = build_nc(S=S, debug=_debug)
    identb, masks = _host_consts()

    in_maps = []
    for c in range(8):
        b_, g = c // 2, c % 2
        in_maps.append({
            "x_full": np.ascontiguousarray(h[b_]),
            "x_res": np.ascontiguousarray(h[b_, g * T:(g + 1) * T]),
            "wq_in": np.ascontiguousarray(Wq[1024 * g:1024 * (g + 1)]),
            "wk_in": np.ascontiguousarray(Wk[1024 * g:1024 * (g + 1)]),
            "wv_in": np.ascontiguousarray(Wv[1024 * g:1024 * (g + 1)]),
            "bq_in": np.ascontiguousarray(bq[1024 * g:1024 * (g + 1)].reshape(1, -1)),
            "bk_in": np.ascontiguousarray(bk[1024 * g:1024 * (g + 1)].reshape(1, -1)),
            "bv_in": np.ascontiguousarray(bv[1024 * g:1024 * (g + 1)].reshape(1, -1)),
            "wo_in": np.ascontiguousarray(Wo),
            "bo_in": np.ascontiguousarray(bo.reshape(1, -1)),
            "wgu_in": np.ascontiguousarray(Wgu[2048 * c:2048 * (c + 1)]),
            "wp_in": np.ascontiguousarray(Wp[:, 1024 * c:1024 * (c + 1)]),
            "identb_in": identb,
            "masks_in": masks,
            "oidx_in": _oidx_for_core(c, S),
        })

    res = run_bass_kernel_spmd(nc, in_maps, core_ids=list(range(8)))
    outp = np.empty((B, S, D), dtype=np.float32)
    for c in range(8):
        b_, g = c // 2, c % 2
        outp[b_, g * T:(g + 1) * T] = res.results[c]["out"]
    if _debug:
        return outp, res
    return outp
